# version 02
# baseline (speedup 1.0000x reference)
import numpy as np
import ml_dtypes
from contextlib import ExitStack

import concourse.bass as bass
import concourse.mybir as mybir
from concourse.bass_utils import run_bass_kernel_spmd

F32 = mybir.dt.float32
BF16 = mybir.dt.bfloat16
I32 = mybir.dt.int32
AF = mybir.ActivationFunctionType
ALU = mybir.AluOpType
AX = mybir.AxisListType

S_LEN = 4096
D = 2048
HEADS = 16
DH = 128
NMLP = 16384
NT = S_LEN // 128
NC_CHUNKS = D // 128
SCALE = DH ** -0.5
EPS = 1e-5
TSUP = 512
NSUP = S_LEN // TSUP

DEBUG = False
PA_STAGE = 3
PA_NCH = None
PHASES = "1AWTB"


def _host_consts():
    ident = np.eye(128, dtype=ml_dtypes.bfloat16)
    inv = 1.0 / 10000.0 ** (np.arange(0, DH, 2, dtype=np.float64) / DH)
    tpos = np.arange(S_LEN, dtype=np.float64)[:, None] * inv[None, :]
    cos = np.cos(tpos)
    sin = np.sin(tpos)
    cosf = np.concatenate([cos, cos], 1)
    sinf = np.concatenate([sin, sin], 1)
    rope = np.concatenate(
        [cosf * SCALE, sinf * SCALE, cosf, sinf], axis=1).astype(np.float32)
    jj = np.arange(512)[None, :]
    ii = np.arange(128)[:, None]
    mask = np.zeros((4, 128, 512), dtype=ml_dtypes.bfloat16)
    for v in range(4):
        mask[v] = (jj <= (128 * v + ii)).astype(ml_dtypes.bfloat16)
    ones = np.ones((1, 128), dtype=ml_dtypes.bfloat16)
    return dict(c_ident=ident, c_rope=rope, c_mask=mask.reshape(512, 512),
                c_ones=ones)


def build_nc():
    nc = bass.Bass("TRN2", target_bir_lowering=False)

    x_d = nc.dram_tensor("x", [S_LEN, D], F32, kind="ExternalInput").ap()
    wq_d = nc.dram_tensor("wq", [D, D], F32, kind="ExternalInput").ap()
    wkv_d = nc.dram_tensor("wkv", [D, 256], F32, kind="ExternalInput").ap()
    wout_d = nc.dram_tensor("wout", [D, D], F32, kind="ExternalInput").ap()
    gwp_d = nc.dram_tensor("gwp", [NMLP, D // 2], I32, kind="ExternalInput").ap()
    gsc_d = nc.dram_tensor("gsc", [NMLP, 64], F32, kind="ExternalInput").ap()
    uwp_d = nc.dram_tensor("uwp", [NMLP, D // 2], I32, kind="ExternalInput").ap()
    usc_d = nc.dram_tensor("usc", [NMLP, 64], F32, kind="ExternalInput").ap()
    dwp_d = nc.dram_tensor("dwp", [D, NMLP // 2], I32, kind="ExternalInput").ap()
    dsc_d = nc.dram_tensor("dsc", [D, NMLP // 32], F32, kind="ExternalInput").ap()
    ident_d = nc.dram_tensor("c_ident", [128, 128], BF16, kind="ExternalInput").ap()
    rope_d = nc.dram_tensor("c_rope", [S_LEN, 512], F32, kind="ExternalInput").ap()
    mask_d = nc.dram_tensor("c_mask", [512, 512], BF16, kind="ExternalInput").ap()
    ones_d = nc.dram_tensor("c_ones", [1, 128], BF16, kind="ExternalInput").ap()

    out_d = nc.dram_tensor("out", [S_LEN, D], F32, kind="ExternalOutput").ap()

    skind = dict(kind="ExternalOutput") if DEBUG else {}
    xn_d = nc.dram_tensor("xn_s", [S_LEN, D], BF16, **skind).ap()
    qt_d = nc.dram_tensor("qt_s", [NT, 128, D], BF16, **skind).ap()
    kt_d = nc.dram_tensor("kt_s", [128, S_LEN], BF16, **skind).ap()
    v_d = nc.dram_tensor("v_s", [128, S_LEN], BF16, **skind).ap()
    int_d = nc.dram_tensor("int_s", [NSUP, 128, 128, TSUP], BF16, **skind).ap()
    wdt_d = nc.dram_tensor("wdt_s", [128, 4, 128, 512], BF16, **skind).ap()
    s_d = nc.dram_tensor("s_s", [NT, 128, D], F32, **skind).ap()
    dbg_d = nc.dram_tensor("dbg_s", [128, 2 * NT * HEADS + 1], F32, **skind).ap()
    dbg2_d = nc.dram_tensor("dbg2_s", [128, 512], F32, **skind).ap()
    dbg3_d = nc.dram_tensor("dbg3_s", [4, 128, D], F32, **skind).ap()

    with ExitStack() as es:
        def sb(name, shape, dt):
            return es.enter_context(nc.sbuf_tensor(name, shape, dt))

        ident = sb("ident", [128, 128], BF16)
        ones = sb("ones", [1, 128], BF16)
        kn2 = sb("kn2", [128, NT], F32)
        qn2 = sb("qn2", [128, NT * HEADS], F32)
        k2m_b = sb("k2m_b", [128, 1], F32)
        czero = sb("czero", [128, 1], F32)
        ceps = sb("ceps", [128, 1], F32)

        if "1" in PHASES:
            _stage1(nc, es, locals())
        if "A" in PHASES:
            _passA(nc, es, locals())
        if "T" in PHASES:
            _attn(nc, es, locals())
        if "W" in PHASES:
            _wd(nc, es, locals())
        if "B" in PHASES:
            _passB(nc, es, locals())

    return nc




def _stage1(nc, outer_es, env):
    x_d, wq_d, wkv_d = env["x_d"], env["wq_d"], env["wkv_d"]
    ident_d, rope_d, ones_d = env["ident_d"], env["rope_d"], env["ones_d"]
    xn_d, qt_d, kt_d, v_d = env["xn_d"], env["qt_d"], env["kt_d"], env["v_d"]
    ident, ones, kn2, qn2, k2m_b = (env["ident"], env["ones"], env["kn2"],
                                    env["qn2"], env["k2m_b"])
    czero, ceps = env["czero"], env["ceps"]

    with ExitStack() as es:
        def sb(name, shape, dt):
            return es.enter_context(nc.sbuf_tensor(name, shape, dt))

        def sem(name):
            return es.enter_context(nc.semaphore(name))

        wq_f = [sb(f"wq_f{p}", [128, D], F32) for p in range(2)]
        wq_bf = sb("wq_bf", [128, NC_CHUNKS * D], BF16)
        wkv_f = sb("wkv_f", [128, NC_CHUNKS * 256], F32)
        wkv_bf = sb("wkv_bf", [128, NC_CHUNKS * 256], BF16)
        x_t = [sb(f"x_t{p}", [128, D], F32) for p in range(2)]
        rope_t = [sb(f"rope_t{p}", [128, 512], F32) for p in range(2)]
        xn_t = [sb(f"xn_t{p}", [128, D], BF16) for p in range(2)]
        xnT_t = [sb(f"xnT_t{p}", [128, D], BF16) for p in range(2)]
        q_f = sb("q_f", [128, D], F32)
        q_ro = sb("q_ro", [128, D], BF16)
        qT_t = [sb(f"qT_t{p}", [128, D], BF16) for p in range(2)]
        sq_s = sb("sq_s", [128, D], F32)
        k_f = sb("k_f", [128, DH], F32)
        k_ro = sb("k_ro", [128, DH], BF16)
        kT_sb = sb("kT_sb", [128, S_LEN], BF16)
        v_sb = sb("v_sb", [128, S_LEN], BF16)
        stats = [sb(f"stats{p}", [128, 8], F32) for p in range(2)]
        ksc = sb("ksc", [128, 2], F32)
        k2bf = sb("k2bf", [128, 1], BF16)

        ps_q = es.enter_context(nc.psum_tensor("ps_q", [128, D], F32))
        ps_kv = es.enter_context(nc.psum_tensor("ps_kv", [128, 256], F32))
        ps_tr = [es.enter_context(nc.psum_tensor(f"ps_tr{p}", [128, 512], F32))
                 for p in range(2)]
        ps_b = es.enter_context(nc.psum_tensor("ps_b", [128, 128], F32))

        sLd, sLdW, sWcv = sem("s1_ld"), sem("s1_ldw"), sem("s1_wcv")
        xL = [sem(f"s1_xl{p}") for p in range(2)]
        rL = [sem(f"s1_rl{p}") for p in range(2)]
        wqL = [sem(f"s1_wql{p}") for p in range(2)]
        sStX = [sem(f"s1_stx{p}") for p in range(2)]
        sStQ = [sem(f"s1_stq{p}") for p in range(2)]
        sSq, sLN, sXn = sem("s1_sq"), sem("s1_ln"), sem("s1_xn")
        sTr, sTrE, sKT = sem("s1_tr"), sem("s1_tre"), sem("s1_kt")
        sMM, sQE, sRo = sem("s1_mm"), sem("s1_qe"), sem("s1_ro")
        sKV, sSt, sFin = sem("s1_kv"), sem("s1_st"), sem("s1_fin")

        block = es.enter_context(nc.Block())

        @block.sync
        def _(sync):
            sync.dma_start(ident[:, :], ident_d).then_inc(sLd, 16)
            sync.dma_start(ones[:, :], ones_d).then_inc(sLd, 16)
            sync.dma_start(
                wkv_f[:, :].rearrange("p (c n) -> p c n", n=256),
                wkv_d.rearrange("(c p) n -> p c n", p=128)).then_inc(sLdW, 16)
            for c in range(NC_CHUNKS):
                if c >= 2:
                    sync.wait_ge(sWcv, c - 1)
                sync.dma_start(wq_f[c % 2][:, :], wq_d[c * 128:(c + 1) * 128, :]) \
                    .then_inc(wqL[c % 2], 16)
            for t in range(NT):
                if t >= 2:
                    sync.wait_ge(sXn, t - 1)
                    sync.wait_ge(sSq, 2 * (t - 2) + 1)
                    sync.wait_ge(sKV, 4 * (t - 2) + 2)
                sync.dma_start(x_t[t % 2][:, :], x_d[t * 128:(t + 1) * 128, :]) \
                    .then_inc(xL[t % 2], 16)
                sync.dma_start(rope_t[t % 2][:, :], rope_d[t * 128:(t + 1) * 128, :]) \
                    .then_inc(rL[t % 2], 16)

        @block.vector
        def _(vector):
            vector.memset(czero[:, :], 0.0)
            vector.memset(ceps[:, :], EPS)
            vector.wait_ge(sLdW, 16)
            vector.tensor_copy(wkv_bf[:, :], wkv_f[:, :])
            for c in range(NC_CHUNKS):
                vector.wait_ge(wqL[c % 2], 16 * (c // 2 + 1))
                vector.tensor_copy(wq_bf[:, c * D:(c + 1) * D], wq_f[c % 2][:, :]) \
                    .then_inc(sWcv, 1)
            for t in range(NT):
                p = t % 2
                st = stats[p]
                vector.wait_ge(xL[t % 2], 16 * (t // 2 + 1))
                vector.tensor_reduce(st[:, 0:1], x_t[p][:, :], AX.X, ALU.add,
                                     negate=True).then_inc(sLN, 1)
                vector.wait_ge(sLN, 6 * t + 1)
                vector.tensor_scalar(st[:, 2:3], st[:, 0:1], 1.0 / D, None,
                                     ALU.mult).then_inc(sLN, 1)
                vector.wait_ge(sLN, 6 * t + 2)
                vector.tensor_mul(st[:, 3:4], st[:, 2:3], st[:, 2:3]) \
                    .then_inc(sLN, 1)
                vector.wait_ge(sSq, 2 * t + 1)
                vector.wait_ge(sLN, 6 * t + 3)
                vector.scalar_tensor_tensor(st[:, 4:5], st[:, 1:2], 1.0 / D,
                                            st[:, 3:4], ALU.mult, ALU.subtract) \
                    .then_inc(sLN, 1)
                vector.wait_ge(sLN, 6 * t + 5)
                vector.reciprocal(st[:, 6:7], st[:, 5:6]).then_inc(sLN, 1)
                vector.wait_ge(sLN, 6 * t + 6)
                if t >= 2:
                    vector.wait_ge(sStX[t % 2], 16 * (t // 2))
                vector.tensor_scalar(xn_t[p][:, :], x_t[p][:, :], st[:, 2:3],
                                     st[:, 6:7], ALU.add, ALU.mult) \
                    .then_inc(sXn, 1)
                if t >= 2:
                    vector.wait_ge(sMM, 80 * (t - 1))
                for g in range(4):
                    H = 8 * t + g
                    vector.wait_ge(sTr, H + 1)
                    vector.tensor_copy(
                        xnT_t[p][:, :].rearrange("q (c n) -> q c n", n=128)
                        [:, 4 * g:4 * g + 4, :],
                        ps_tr[g % 2][:, :].rearrange("q (c n) -> q c n", n=128)) \
                        .then_inc(sTrE, 1)
                vector.wait_ge(sQE, 8 * t + 4)
                vector.wait_ge(rL[t % 2], 16 * (t // 2 + 1))
                qv = q_f[:, :].rearrange("q (h c) -> q h c", c=DH)
                ov = q_ro[:, :].rearrange("q (h c) -> q h c", c=DH)
                sv = sq_s[:, :].rearrange("q (h c) -> q h c", c=DH)
                rp = rope_t[p]
                clo = rp[:, 0:64].unsqueeze(1).broadcast_to([128, HEADS, 64])
                chi = rp[:, 64:128].unsqueeze(1).broadcast_to([128, HEADS, 64])
                slo = rp[:, 128:192].unsqueeze(1).broadcast_to([128, HEADS, 64])
                shi = rp[:, 192:256].unsqueeze(1).broadcast_to([128, HEADS, 64])
                vector.tensor_tensor(sv[:, :, 0:64], qv[:, :, 0:64], clo, ALU.mult)
                vector.tensor_tensor(sv[:, :, 64:128], qv[:, :, 64:128], slo, ALU.mult)
                vector.tensor_tensor(ov[:, :, 0:64], sv[:, :, 0:64], sv[:, :, 64:128],
                                     ALU.subtract)
                vector.tensor_tensor(sv[:, :, 0:64], qv[:, :, 64:128], chi, ALU.mult)
                vector.tensor_tensor(sv[:, :, 64:128], qv[:, :, 0:64], shi, ALU.mult)
                vector.tensor_add(ov[:, :, 64:128], sv[:, :, 0:64], sv[:, :, 64:128]) \
                    .then_inc(sRo, 4)
                vector.wait_ge(sQE, 8 * t + 8)
                vector.tensor_reduce(
                    qn2[:, t * HEADS:(t + 1) * HEADS].unsqueeze(2),
                    sq_s[:, :].rearrange("q (h c) -> q h c", c=DH),
                    AX.X, ALU.add).then_inc(sRo, 1)
                if t >= 2:
                    vector.wait_ge(sStQ[t % 2], 16 * (t // 2))
                for g in range(4):
                    H = 8 * t + 4 + g
                    vector.wait_ge(sTr, H + 1)
                    vector.tensor_copy(
                        qT_t[p][:, :].rearrange("q (c n) -> q c n", n=128)
                        [:, 4 * g:4 * g + 4, :],
                        ps_tr[g % 2][:, :].rearrange("q (c n) -> q c n", n=128)) \
                        .then_inc(sTrE, 1)
                vector.wait_ge(sKV, 4 * t + 1)
                t1k, t2k = sq_s[:, 0:64], sq_s[:, 64:128]
                vector.tensor_tensor(t1k, k_f[:, 0:64], rp[:, 256:320], ALU.mult) \
                    .then_inc(sRo, 1)
                vector.tensor_tensor(t2k, k_f[:, 64:128], rp[:, 384:448], ALU.mult) \
                    .then_inc(sRo, 1)
                vector.wait_ge(sRo, 10 * t + 7)
                vector.tensor_tensor(k_ro[:, 0:64], t1k, t2k, ALU.subtract) \
                    .then_inc(sRo, 1)
                vector.tensor_tensor(t1k, k_f[:, 64:128], rp[:, 320:384], ALU.mult) \
                    .then_inc(sRo, 1)
                vector.tensor_tensor(t2k, k_f[:, 0:64], rp[:, 448:512], ALU.mult) \
                    .then_inc(sRo, 1)
                vector.wait_ge(sRo, 10 * t + 10)
                vector.tensor_add(k_ro[:, 64:128], t1k, t2k).then_inc(sKV, 1)
                vector.wait_ge(sMM, 80 * (t + 1))
                vector.tensor_copy(v_sb[:, t * 128:(t + 1) * 128], ps_kv[:, 128:256]) \
                    .then_inc(sKV, 1)
                vector.wait_ge(sKT, t + 1)
                vector.tensor_copy(kT_sb[:, t * 128:(t + 1) * 128], ps_b[:, :]) \
                    .then_inc(sKV, 1)
            vector.reduce_max(ksc[:, 0:1], kn2[:, :], AX.X).then_inc(sFin, 1)
            vector.wait_ge(sFin, 1)
            vector.tensor_copy(k2bf[:, :], ksc[:, 0:1]).then_inc(sFin, 1)
            vector.wait_ge(sFin, 3)
            vector.reduce_max(ksc[0:1, 1:2], ps_b[0:1, 0:128], AX.X).then_inc(sFin, 1)
            vector.wait_ge(sFin, 4)
            vector.tensor_copy(k2bf[0:1, 0:1], ksc[0:1, 1:2]).then_inc(sFin, 1)
            vector.wait_ge(sFin, 6)
            vector.tensor_copy(k2m_b[:, :], ps_b[:, 0:1]).then_inc(sFin, 1)

        @block.scalar
        def _(scalar):
            for t in range(NT):
                p = t % 2
                st = stats[p]
                scalar.wait_ge(xL[t % 2], 16 * (t // 2 + 1))
                if t >= 1:
                    scalar.wait_ge(sKV, 4 * (t - 1) + 2)
                scalar.activation(sq_s[:, :], x_t[p][:, :], AF.Square,
                                  accum_out=st[:, 1:2]).then_inc(sSq, 1)
                scalar.wait_ge(sLN, 6 * t + 4)
                scalar.activation(st[:, 5:6], st[:, 4:5], AF.Sqrt,
                                  bias=ceps[:, :]).then_inc(sLN, 1)
                scalar.wait_ge(sMM, 80 * (t + 1))
                for qq in range(4):
                    scalar.activation(q_f[:, qq * 512:(qq + 1) * 512],
                                      ps_q[:, qq * 512:(qq + 1) * 512],
                                      AF.Copy).then_inc(sQE, 1)
                scalar.wait_ge(sRo, 10 * t + 4)
                scalar.activation(sq_s[:, :], q_ro[:, :], AF.Square) \
                    .then_inc(sQE, 4)
                scalar.activation(k_f[:, :], ps_kv[:, 0:128], AF.Copy) \
                    .then_inc(sKV, 1)
                scalar.wait_ge(sKV, 4 * t + 2)
                scalar.activation(sq_s[:, 1920:2048], k_ro[:, :], AF.Square,
                                  accum_out=kn2[:, t:t + 1]).then_inc(sSq, 1)
                scalar.wait_ge(sXn, t + 1)
                scalar.dma_start(xn_d[t * 128:(t + 1) * 128, :], xn_t[p][:, :]) \
                    .then_inc(sStX[t % 2], 16)
                scalar.wait_ge(sTrE, 8 * t + 8)
                scalar.dma_start(qt_d[t, :, :], qT_t[p][:, :]).then_inc(sStQ[t % 2], 16)
            scalar.wait_ge(sKV, 4 * NT)
            scalar.dma_start(kt_d, kT_sb[:, :]).then_inc(sSt, 16)
            scalar.dma_start(v_d, v_sb[:, :]).then_inc(sSt, 16)
            scalar.wait_ge(sSt, 32)
            for p in range(2):
                scalar.wait_ge(sStX[p], 16 * (NT // 2))
                scalar.wait_ge(sStQ[p], 16 * (NT // 2))
            scalar.wait_ge(sFin, 7)

        @block.tensor
        def _(tensor):
            tensor.wait_ge(sLd, 32)
            for t in range(NT):
                p = t % 2
                tensor.wait_ge(sXn, t + 1)
                for g in range(4):
                    H = 8 * t + g
                    if H >= 2:
                        tensor.wait_ge(sTrE, H - 1)
                    mm = None
                    for k in range(4):
                        c = 4 * g + k
                        mm = tensor.matmul(ps_tr[g % 2][:, k * 128:(k + 1) * 128],
                                           xn_t[p][:, c * 128:(c + 1) * 128],
                                           ident[:, :], start=True, stop=True)
                    mm.then_inc(sTr, 1)
                if t == 0:
                    tensor.wait_ge(sWcv, NC_CHUNKS)
                tensor.wait_ge(sTrE, 8 * t + 4)
                if t > 0:
                    tensor.wait_ge(sQE, 8 * (t - 1) + 4)
                    tensor.wait_ge(sKV, 4 * (t - 1) + 3)
                for c in range(NC_CHUNKS):
                    lhs = xnT_t[p][:, c * 128:(c + 1) * 128]
                    for qq in range(4):
                        tensor.matmul(ps_q[:, qq * 512:(qq + 1) * 512], lhs,
                                      wq_bf[:, c * D + qq * 512:c * D + qq * 512 + 512],
                                      start=(c == 0), stop=(c == NC_CHUNKS - 1)) \
                            .then_inc(sMM, 1)
                    tensor.matmul(ps_kv[:, :], lhs, wkv_bf[:, c * 256:(c + 1) * 256],
                                  start=(c == 0), stop=(c == NC_CHUNKS - 1)) \
                        .then_inc(sMM, 1)
                tensor.wait_ge(sRo, 10 * t + 4)
                for g in range(4):
                    H = 8 * t + 4 + g
                    tensor.wait_ge(sTrE, H - 1)
                    mm = None
                    for k in range(4):
                        c = 4 * g + k
                        mm = tensor.matmul(ps_tr[g % 2][:, k * 128:(k + 1) * 128],
                                           q_ro[:, c * 128:(c + 1) * 128],
                                           ident[:, :], start=True, stop=True)
                    mm.then_inc(sTr, 1)
                tensor.wait_ge(sKV, 4 * t + 2)
                tensor.matmul(ps_b[:, :], k_ro[:, :], ident[:, :],
                              start=True, stop=True).then_inc(sKT, 1)
            tensor.wait_ge(sFin, 2)
            tensor.matmul(ps_b[0:1, 0:128], k2bf[:, :], ident[:, :],
                          start=True, stop=True).then_inc(sFin, 1)
            tensor.wait_ge(sFin, 5)
            tensor.matmul(ps_b[:, 0:1], ones[0:1, :], k2bf[0:1, 0:1],
                          start=True, stop=True).then_inc(sFin, 1)




def _passA(nc, outer_es, env):
    xn_d, int_d = env["xn_d"], env["int_d"]
    gwp_d, gsc_d, uwp_d, usc_d = env["gwp_d"], env["gsc_d"], env["uwp_d"], env["usc_d"]
    ident = env["ident"]
    NCH = PA_NCH if PA_NCH is not None else NMLP // 128

    with ExitStack() as es:
        def sb(name, shape, dt):
            return es.enter_context(nc.sbuf_tensor(name, shape, dt))

        def sem(name):
            return es.enter_context(nc.semaphore(name))

        xnT = sb("pa_xnT", [128, NC_CHUNKS * S_LEN], BF16)
        pg = [sb(f"pa_pg{p}", [128, 1024], I32) for p in range(2)]
        pu = [sb(f"pa_pu{p}", [128, 1024], I32) for p in range(2)]
        sg = [sb(f"pa_sg{p}", [128, 64], F32) for p in range(2)]
        su = [sb(f"pa_su{p}", [128, 64], F32) for p in range(2)]
        nib_i = sb("pa_nibi", [128, 1024], I32)
        halfn = [sb(f"pa_hn{p}", [128, 1024], BF16) for p in range(2)]
        m4 = sb("pa_m4", [128, 1024], BF16)
        u_ = [sb(f"pa_u{p}", [128, 1024], BF16) for p in range(2)]
        sgn = [sb(f"pa_sgn{p}", [128, 1024], BF16) for p in range(2)]
        b1 = [sb(f"pa_b1{p}", [128, 1024], BF16) for p in range(2)]
        b2 = [sb(f"pa_b2{p}", [128, 1024], BF16) for p in range(2)]
        wdq = [sb(f"pa_wdq{p}", [128, D], BF16) for p in range(2)]
        wT = {"g": [sb(f"pa_wTg{p}", [128, D], BF16) for p in range(2)],
              "u": [sb(f"pa_wTu{p}", [128, D], BF16) for p in range(2)]}
        xn_l = [wT["g"][0], wT["g"][1]]
        gel = [sb(f"pa_gel{p}", [128, 512], F32) for p in range(2)]
        int_sb = [sb(f"pa_int{p}", [128, 512], BF16) for p in range(2)]
        cm2 = sb("pa_cm2", [128, 1], F32)
        cm3 = sb("pa_cm3", [128, 1], F32)

        ps_tr = [es.enter_context(nc.psum_tensor(f"pa_pstr{p}", [128, 512], F32))
                 for p in range(2)]
        ps_g = [es.enter_context(nc.psum_tensor(f"pa_psg{p}", [128, 512], F32))
                for p in range(2)]
        ps_u = [es.enter_context(nc.psum_tensor(f"pa_psu{p}", [128, 512], F32))
                for p in range(2)]

        sLdA, sLdW2 = sem("pa_lda"), sem("pa_ldw")
        xnL = [sem(f"pa_xnl{p}") for p in range(2)]
        pkL = [sem(f"pa_pkl{p}") for p in range(2)]
        sStI = [sem(f"pa_sti{p}") for p in range(2)]
        sTr, sTrE = sem("pa_tr"), sem("pa_tre")
        sDqV1, sDqP, sDqS, sDqV2 = (sem("pa_dqv1"), sem("pa_dqp"),
                                    sem("pa_dqs"), sem("pa_dqv2"))
        sMMA, sGel, sInt, sStA = sem("pa_mma"), sem("pa_gel"), sem("pa_int"), sem("pa_sta")

        block = es.enter_context(nc.Block())

        def v1_emit(vector, c, wi, hp):
            u = 4 * c + 2 * wi + hp
            wp = (pg if wi == 0 else pu)[c % 2]
            if u >= 2:
                vector.wait_ge(sDqP, u - 1)
            if hp == 0:
                vector.wait_ge(pkL[c % 2], 64 * (c // 2 + 1))
            lo = nib_i[:, :].rearrange("q (k two) -> q k two", two=2)[:, :, 0:1]
            hi = nib_i[:, :].rearrange("q (k two) -> q k two", two=2)[:, :, 1:2]
            srcp = wp[:, hp * 512:(hp + 1) * 512].unsqueeze(2)
            vector.tensor_scalar(lo, srcp, 15, None, ALU.bitwise_and)
            vector.tensor_scalar(hi, srcp, 4, 15, ALU.logical_shift_right,
                                 ALU.bitwise_and)
            vector.tensor_scalar(halfn[u % 2][:, :], nib_i[:, :], 0.5, None,
                                 ALU.mult).then_inc(sDqV1, 1)

        def v2_emit(vector, c, wi, hp):
            u = 4 * c + 2 * wi + hp
            cw = 2 * c + wi
            sc = (sg if wi == 0 else su)[c % 2]
            p = u % 2
            vector.wait_ge(sDqS, u + 1)
            if cw >= 2 and PA_STAGE >= 2:
                vector.wait_ge(sTr, 128 + 4 * (cw - 2) + 4)
            vector.scalar_tensor_tensor(b2[p][:, :], b2[p][:, :], 2.0, b1[p][:, :],
                                        ALU.mult, ALU.add)
            vector.tensor_add(u_[p][:, :], b2[p][:, :], u_[p][:, :])
            vector.tensor_mul(sgn[p][:, :], u_[p][:, :], sgn[p][:, :])
            scb = sc[:, hp * 32:(hp + 1) * 32].unsqueeze(2).broadcast_to(
                [128, 32, 32])
            vector.tensor_tensor(
                wdq[cw % 2][:, hp * 1024:(hp + 1) * 1024].rearrange(
                    "q (k c2) -> q k c2", c2=32),
                sgn[p][:, :].rearrange("q (k c2) -> q k c2", c2=32),
                scb, ALU.mult).then_inc(sDqV2, 1)

        @block.sync
        def _(sync):
            for t in range(NT):
                if t >= 2:
                    sync.wait_ge(sTr, 4 * (t - 1))
                sync.dma_start(xn_l[t % 2][:, :], xn_d[t * 128:(t + 1) * 128, :]) \
                    .then_inc(xnL[t % 2], 16)
            for c in range(NCH if PA_STAGE >= 1 else 0):
                if c >= 2:
                    sync.wait_ge(sDqV1, 4 * (c - 1))
                    sync.wait_ge(sDqV2, 4 * (c - 1))
                sync.dma_start(pg[c % 2][:, :], gwp_d[c * 128:(c + 1) * 128, :]) \
                    .then_inc(pkL[c % 2], 16)
                sync.dma_start(sg[c % 2][:, :], gsc_d[c * 128:(c + 1) * 128, :]) \
                    .then_inc(pkL[c % 2], 16)
                sync.dma_start(pu[c % 2][:, :], uwp_d[c * 128:(c + 1) * 128, :]) \
                    .then_inc(pkL[c % 2], 16)
                sync.dma_start(su[c % 2][:, :], usc_d[c * 128:(c + 1) * 128, :]) \
                    .then_inc(pkL[c % 2], 16)

        @block.vector
        def _(vector):
            vector.memset(cm2[:, :], -2.0)
            vector.memset(cm3[:, :], -3.0)
            for t in range(NT):
                for g in range(4):
                    GG = 4 * t + g
                    vector.wait_ge(sTr, GG + 1)
                    vector.tensor_copy(
                        xnT[:, :].rearrange("q (ch n) -> q ch n", n=S_LEN)
                        [:, 4 * g:4 * g + 4, t * 128:(t + 1) * 128],
                        ps_tr[GG % 2][:, :].rearrange("q (k n) -> q k n", n=128)) \
                        .then_inc(sTrE, 1)
            units = [(c, wi, hp) for c in range(NCH) for wi in range(2)
                     for hp in range(2)]
            def units_blk(vector, cc):
                for k in range(4):
                    uu = 4 * cc + k
                    v1_emit(vector, *units[uu])
                    if k >= 2:
                        v2_emit(vector, *units[uu - 2])
                v2_emit(vector, *units[4 * cc + 2])
                v2_emit(vector, *units[4 * cc + 3])

            def evacs_blk(vector, cc):
                for wi in range(2):
                    cw = 2 * cc + wi
                    for g in range(4):
                        GG = 128 + 4 * cw + g
                        vector.wait_ge(sTr, GG + 1)
                        if cc >= 2 and PA_STAGE in (3, 5, 6):
                            vector.wait_ge(sMMA, 16 * (cc - 1))
                        vector.tensor_copy(
                            wT[("g", "u")[wi]][cc % 2][:, :].rearrange(
                                "q (k n) -> q k n", n=128)[:, 4 * g:4 * g + 4, :],
                            ps_tr[GG % 2][:, :].rearrange(
                                "q (k n) -> q k n", n=128)).then_inc(sTrE, 1)

            def inter_blk(vector, cm, tcs):
                for tc in tcs:
                    G2 = 8 * cm + tc
                    vector.wait_ge(sGel, G2 + 1)
                    vector.wait_ge(sMMA, 16 * cm + 2 * tc + 2)
                    if G2 >= 2:
                        vector.wait_ge(sStI[tc % 2], 16 * (G2 // 2))
                    vector.tensor_tensor(int_sb[tc % 2][:, :], gel[tc % 2][:, :],
                                         ps_u[tc % 2][:, :], ALU.mult) \
                        .then_inc(sInt, 1)

            if PA_STAGE >= 1:
                units_blk(vector, 0)
                if PA_STAGE >= 2:
                    evacs_blk(vector, 0)
                for c in range(NCH):
                    if c + 1 < NCH:
                        units_blk(vector, c + 1)
                    if PA_STAGE == 3:
                        inter_blk(vector, c, range(0, 4))
                    if PA_STAGE >= 2 and c + 1 < NCH:
                        evacs_blk(vector, c + 1)
                    if PA_STAGE == 3:
                        inter_blk(vector, c, range(4, 8))

        @block.gpsimd
        def _(gpsimd):
            for u in range(4 * NCH if PA_STAGE >= 1 else 0):
                p = u % 2
                gpsimd.wait_ge(sDqV1, u + 1)
                if u >= 2:
                    gpsimd.wait_ge(sDqV2, u - 1)
                gpsimd.tensor_scalar(m4[:, :], halfn[p][:, :], 4.0, -4.0,
                                     ALU.is_ge, ALU.mult)
                gpsimd.tensor_add(u_[p][:, :], halfn[p][:, :], m4[:, :])
                gpsimd.tensor_scalar(sgn[p][:, :], m4[:, :], 0.5, 1.0,
                                     ALU.mult, ALU.add).then_inc(sDqP, 1)

        @block.scalar
        def _(scalar):
            def relus_blk(scalar, cc):
                for k in range(4):
                    u = 4 * cc + k
                    p = u % 2
                    scalar.wait_ge(sDqP, u + 1)
                    if u >= 2:
                        scalar.wait_ge(sDqV2, u - 1)
                    scalar.activation(b1[p][:, :], u_[p][:, :], AF.Relu,
                                      bias=cm2[:, :])
                    scalar.activation(b2[p][:, :], u_[p][:, :], AF.Relu,
                                      bias=cm3[:, :]).then_inc(sDqS, 1)

            if PA_STAGE >= 1:
                relus_blk(scalar, 0)
            for c in range(NCH if PA_STAGE >= 1 else 0):
                if c + 1 < NCH:
                    relus_blk(scalar, c + 1)
                for tc in range(8 if PA_STAGE in (3, 6) else 0):
                    G2 = 8 * c + tc
                    scalar.wait_ge(sMMA, 16 * c + 2 * tc + 1)
                    if G2 >= 2:
                        scalar.wait_ge(sInt, G2 - 1)
                    scalar.activation(gel[tc % 2][:, :], ps_g[tc % 2][:, :],
                                      AF.Gelu_apprx_tanh).then_inc(sGel, 1)
                    if tc >= 1 and PA_STAGE == 3:
                        scalar.wait_ge(sInt, G2)
                        scalar.dma_start(int_d[tc - 1, c, :, :],
                                         int_sb[(tc - 1) % 2][:, :]) \
                            .then_inc(sStI[(tc - 1) % 2], 16)
                if PA_STAGE == 3:
                    scalar.wait_ge(sInt, 8 * c + 8)
                    scalar.dma_start(int_d[7, c, :, :], int_sb[1][:, :]) \
                        .then_inc(sStI[1], 16)
            if PA_STAGE == 3:
                for pp in range(2):
                    scalar.wait_ge(sStI[pp], 16 * 4 * NCH)

        @block.tensor
        def _(tensor):
            for t in range(NT):
                tensor.wait_ge(xnL[t % 2], 16 * (t // 2 + 1))
                for g in range(4):
                    GG = 4 * t + g
                    if GG >= 2:
                        tensor.wait_ge(sTrE, GG - 1)
                    mm = None
                    for k in range(4):
                        ch = 4 * g + k
                        mm = tensor.matmul(
                            ps_tr[GG % 2][:, k * 128:(k + 1) * 128],
                            xn_l[t % 2][:, ch * 128:(ch + 1) * 128],
                            ident[:, :], start=True, stop=True)
                    mm.then_inc(sTr, 1)
            def trans_blk(tensor, cc):
                for wi in range(2):
                    cw = 2 * cc + wi
                    tensor.wait_ge(sDqV2, 4 * cc + 2 * wi + 2)
                    for g in range(4):
                        GG = 128 + 4 * cw + g
                        tensor.wait_ge(sTrE, GG - 1)
                        mm = None
                        for k in range(4):
                            ch = 4 * g + k
                            mm = tensor.matmul(
                                ps_tr[GG % 2][:, k * 128:(k + 1) * 128],
                                wdq[cw % 2][:, ch * 128:(ch + 1) * 128],
                                ident[:, :], start=True, stop=True)
                        mm.then_inc(sTr, 1)

            def mms_blk(tensor, c, tcs):
                for tc in tcs:
                    G2 = 8 * c + tc
                    if G2 >= 2 and PA_STAGE != 5:
                        tensor.wait_ge(sGel, G2 - 1)
                        if PA_STAGE != 6:
                            tensor.wait_ge(sInt, G2 - 1)
                    mm = None
                    for ch in range(NC_CHUNKS):
                        mm = tensor.matmul(
                            ps_g[tc % 2][:, :],
                            wT["g"][c % 2][:, ch * 128:(ch + 1) * 128],
                            xnT[:, ch * S_LEN + tc * 512:
                                ch * S_LEN + tc * 512 + 512],
                            start=(ch == 0), stop=(ch == NC_CHUNKS - 1))
                    mm.then_inc(sMMA, 1)
                    mm = None
                    for ch in range(NC_CHUNKS):
                        mm = tensor.matmul(
                            ps_u[tc % 2][:, :],
                            wT["u"][c % 2][:, ch * 128:(ch + 1) * 128],
                            xnT[:, ch * S_LEN + tc * 512:
                                ch * S_LEN + tc * 512 + 512],
                            start=(ch == 0), stop=(ch == NC_CHUNKS - 1))
                    mm.then_inc(sMMA, 1)

            if PA_STAGE >= 2:
                trans_blk(tensor, 0)
                for c in range(NCH):
                    tensor.wait_ge(sTrE, 128 + 8 * c + 8)
                    if PA_STAGE in (3, 5, 6):
                        mms_blk(tensor, c, range(0, 4))
                    if c + 1 < NCH:
                        trans_blk(tensor, c + 1)
                    if PA_STAGE in (3, 5, 6):
                        mms_blk(tensor, c, range(4, 8))


def _attn(nc, outer_es, env):
    qt_d, kt_d, v_d, s_d = env["qt_d"], env["kt_d"], env["v_d"], env["s_d"]
    wout_d, mask_d, dbg_d = env["wout_d"], env["mask_d"], env["dbg_d"]
    dbg2_d = env["dbg2_d"]
    ident, qn2, k2m_b = env["ident"], env["qn2"], env["k2m_b"]

    NG = NT // 4
    NU = NG * HEADS
    qk_per = [4 * (u // 16 + 1) for u in range(NU)]
    qk_base = [0]
    for u in range(NU):
        qk_base.append(qk_base[-1] + qk_per[u])
    pt_per = qk_per
    pt_base = qk_base

    with ExitStack() as es:
        def sb(name, shape, dt):
            return es.enter_context(nc.sbuf_tensor(name, shape, dt))

        def sem(name):
            return es.enter_context(nc.semaphore(name))

        kT = sb("at_kT", [128, S_LEN], BF16)
        v_sb = sb("at_v", [128, S_LEN], BF16)
        wout = sb("at_wout", [128, NC_CHUNKS * D], BF16)
        qT = [sb(f"at_qT{i}", [128, D], BF16) for i in range(4)]
        p_row = [sb(f"at_p{i}", [128, S_LEN], BF16) for i in range(4)]
        pT_sb = [sb(f"at_pT{p}", [128, 512], BF16) for p in range(2)]
        oT_sb = [sb(f"at_oT{p}", [128, HEADS * 512], BF16) for p in range(2)]
        diag = [[sb(f"at_dg{p}_{i}", [128, 128], BF16) for i in range(4)]
                for p in range(2)]
        masks = sb("at_mask", [128, 4 * 512], BF16)
        msneg = sb("at_msneg", [128, NT * HEADS], F32)
        denp = sb("at_denp", [128, 32], F32)
        den4 = sb("at_den4", [128, 4], F32)
        r4 = sb("at_r4", [128, 4], F32)
        s_t = [sb(f"at_s{p}", [128, D], F32) for p in range(2)]
        dbg2 = sb("at_dbg2", [128, 512], F32) if DEBUG else None
        wout_f = [s_t[0], s_t[1]]

        ps_qk = [es.enter_context(nc.psum_tensor(f"at_psqk{p}", [128, 512], F32))
                 for p in range(2)]
        ps_pt = [es.enter_context(nc.psum_tensor(f"at_pspt{p}", [128, 512], F32))
                 for p in range(2)]
        ps_oT = [es.enter_context(nc.psum_tensor(f"at_psot{p}", [128, 512], F32))
                 for p in range(2)]
        ps_s = es.enter_context(nc.psum_tensor("at_pss", [128, 512], F32))

        sLd3, sWcv3 = sem("at_ld"), sem("at_wcv")
        sQT2 = sem("at_qt")
        sQK, sEXP = sem("at_qk"), sem("at_exp")
        sMSK, sDIAG = sem("at_msk"), sem("at_diag")
        sPT, sPTE = sem("at_pt"), sem("at_pte")
        sPV, sOTE = sem("at_pv"), sem("at_ote")
        sSP, sSST = sem("at_sp"), sem("at_sst")
        wtL = [sem(f"at_wtl{p}") for p in range(2)]
        sStS = [sem(f"at_sts{p}") for p in range(2)]
        sPRE = sem("at_pre")

        block = es.enter_context(nc.Block())

        @block.sync
        def _(sync):
            sync.dma_start(kT[:, :], kt_d).then_inc(sLd3, 16)
            sync.dma_start(v_sb[:, :], v_d).then_inc(sLd3, 16)
            sync.dma_start(masks[:, :].rearrange("q (v j) -> q v j", j=512),
                           mask_d.rearrange("(v q) j -> q v j", q=128)) \
                .then_inc(sLd3, 16)
            for c in range(NC_CHUNKS):
                if c >= 2:
                    sync.wait_ge(sWcv3, c - 1)
                sync.dma_start(wout_f[c % 2][:, :], wout_d[c * 128:(c + 1) * 128, :]) \
                    .then_inc(wtL[c % 2], 16)
            for g in range(NG):
                if g >= 1:
                    sync.wait_ge(sQK, qk_base[16 * g])
                for i in range(4):
                    sync.dma_start(qT[i][:, :], qt_d[4 * g + i, :, :]) \
                        .then_inc(sQT2, 16)

        @block.vector
        def _(vector):
            for c in range(NC_CHUNKS):
                vector.wait_ge(wtL[c % 2], 16 * (c // 2 + 1))
                vector.tensor_copy(wout[:, c * D:(c + 1) * D], wout_f[c % 2][:, :]) \
                    .then_inc(sWcv3, 1)
            vector.tensor_tensor(msneg[:, :], qn2[:, :],
                                 k2m_b[:, :].broadcast_to([128, NT * HEADS]),
                                 ALU.mult).then_inc(sPRE, 1)
            vector.wait_ge(sPRE, 2)
            vector.tensor_scalar(msneg[:, :], msneg[:, :], -1.0, -1.0,
                                 ALU.mult, ALU.add).then_inc(sPRE, 1)
            for u in range(NU):
                g, h = u // 16, u % 16
                J = g + 1
                for i in range(4):
                    qk_idx = qk_base[u] + i * J + (J - 1)
                    vector.wait_ge(sEXP, qk_idx + 1)
                    vector.scalar_tensor_tensor(
                        p_row[i][:, g * 512:(g + 1) * 512],
                        p_row[i][:, g * 512:(g + 1) * 512], 1.0,
                        masks[:, i * 512:(i + 1) * 512],
                        ALU.mult, ALU.mult,
                        accum_out=denp[:, i * 8 + (J - 1):i * 8 + J]) \
                        .then_inc(sMSK, 1)
                vector.wait_ge(sEXP, qk_base[u + 1])
                vector.wait_ge(sMSK, 4 * (u + 1))
                vector.tensor_reduce(
                    den4[:, :].unsqueeze(2),
                    denp[:, :].rearrange("q (i j) -> q i j", j=8)[:, :, 0:J],
                    AX.X, ALU.add).then_inc(sDIAG, 1)
                vector.wait_ge(sDIAG, 3 * u + 1)
                if DEBUG:
                    vector.tensor_copy(dbg2[:, 4 * u:4 * u + 4], den4[:, :])
                vector.reciprocal(r4[:, :], den4[:, :]).then_inc(sDIAG, 1)
                vector.wait_ge(sDIAG, 3 * u + 2)
                for i in range(4):
                    ins = vector.tensor_scalar(diag[u % 2][i][:, :], ident[:, :],
                                               r4[:, i:i + 1], None, ALU.mult)
                ins.then_inc(sDIAG, 1)
                for jj in range(4 * J):
                    pt_idx = pt_base[u] + jj
                    vector.wait_ge(sPT, pt_idx + 1)
                    vector.tensor_copy(pT_sb[jj % 2][:, :], ps_pt[jj % 2][:, :]) \
                        .then_inc(sPTE, 1)
                vector.wait_ge(sPV, u + 1)
                if g >= 2:
                    vector.wait_ge(sSP, 16 * (g - 1))
                vector.tensor_copy(oT_sb[g % 2][:, h * 512:(h + 1) * 512],
                                   ps_oT[u % 2][:, :]).then_inc(sOTE, 1)

        @block.scalar
        def _(scalar):
            scalar.wait_ge(sPRE, 1)
            scalar.activation(msneg[:, :], msneg[:, :], AF.Sqrt).then_inc(sPRE, 1)
            if DEBUG:
                scalar.wait_ge(sPRE, 3)
                scalar.dma_start(dbg_d[:, 0:NT * HEADS], msneg[:, :]).then_inc(sPRE, 16)
                scalar.dma_start(dbg_d[:, NT * HEADS:2 * NT * HEADS], qn2[:, :]) \
                    .then_inc(sPRE, 16)
                scalar.dma_start(dbg_d[:, 2 * NT * HEADS:], k2m_b[:, :],
                                 allow_slow_non_contiguous=True).then_inc(sPRE, 16)
                scalar.wait_ge(sPRE, 51)
            for u in range(NU):
                g, h = u // 16, u % 16
                J = g + 1
                if u >= 1:
                    scalar.wait_ge(sPT, pt_base[u])
                    scalar.wait_ge(sDIAG, 3 * u)
                for i in range(4):
                    for jt in range(J):
                        qk_idx = qk_base[u] + i * J + jt
                        scalar.wait_ge(sQK, qk_idx + 1)
                        col = (4 * g + i) * HEADS + h
                        if jt < J - 1:
                            scalar.activation(
                                p_row[i][:, jt * 512:(jt + 1) * 512],
                                ps_qk[qk_idx % 2][:, :], AF.Exp,
                                bias=msneg[:, col:col + 1],
                                accum_out=denp[:, i * 8 + jt:i * 8 + jt + 1]) \
                                .then_inc(sEXP, 1)
                        else:
                            scalar.activation(
                                p_row[i][:, jt * 512:(jt + 1) * 512],
                                ps_qk[qk_idx % 2][:, :], AF.Exp,
                                bias=msneg[:, col:col + 1]).then_inc(sEXP, 1)
                if h == 15:
                    for i in range(4):
                        it = 4 * g + i
                        if it >= 2:
                            scalar.wait_ge(sStS[it % 2], 16 * (it // 2))
                        for dq in range(4):
                            sp_idx = 16 * g + 4 * i + dq
                            scalar.wait_ge(sSP, sp_idx + 1)
                            scalar.activation(s_t[i % 2][:, dq * 512:(dq + 1) * 512],
                                              ps_s[:, :], AF.Copy).then_inc(sSST, 1)
                        scalar.dma_start(s_d[it, :, :], s_t[i % 2][:, :]) \
                            .then_inc(sStS[it % 2], 16)
                    if g == NG - 1:
                        for pp in range(2):
                            scalar.wait_ge(sStS[pp], 16 * (NT // 2))
            if DEBUG and u == NU - 1:
                scalar.wait_ge(sDIAG, 3 * NU)
                scalar.dma_start(dbg2_d, dbg2[:, :]).then_inc(sPRE, 16)
                scalar.wait_ge(sPRE, 67)


        @block.tensor
        def _(tensor):
            tensor.wait_ge(sLd3, 48)
            for u in range(NU):
                g, h = u // 16, u % 16
                J = g + 1
                if u == 0:
                    tensor.wait_ge(sQT2, 64)
                elif h == 0:
                    tensor.wait_ge(sQT2, 64 * (g + 1))
                for i in range(4):
                    lhs = qT[i][:, h * 128:(h + 1) * 128]
                    for jt in range(J):
                        qk_idx = qk_base[u] + i * J + jt
                        if qk_idx >= 2:
                            tensor.wait_ge(sEXP, qk_idx - 1)
                        tensor.matmul(ps_qk[qk_idx % 2][:, :], lhs,
                                      kT[:, jt * 512:(jt + 1) * 512],
                                      start=True, stop=True).then_inc(sQK, 1)
                tensor.wait_ge(sDIAG, 3 * u + 3)
                for jj in range(4 * J):
                    pt_idx = pt_base[u] + jj
                    if pt_idx >= 2:
                        tensor.wait_ge(sPTE, pt_idx - 1)
                    mm = None
                    for i in range(4):
                        mm = tensor.matmul(ps_pt[jj % 2][:, i * 128:(i + 1) * 128],
                                           p_row[i][:, jj * 128:(jj + 1) * 128],
                                           diag[u % 2][i][:, :],
                                           start=True, stop=True)
                    mm.then_inc(sPT, 1)
                    tensor.wait_ge(sPTE, pt_idx + 1)
                    if jj == 0 and u >= 2:
                        tensor.wait_ge(sOTE, u - 1)
                    pv = tensor.matmul(ps_oT[u % 2][:, :],
                                       v_sb[:, jj * 128:(jj + 1) * 128],
                                       pT_sb[jj % 2][:, :],
                                       start=(jj == 0), stop=(jj == 4 * J - 1))
                    if jj == 4 * J - 1:
                        pv.then_inc(sPV, 1)
                if h == 15:
                    tensor.wait_ge(sOTE, 16 * (g + 1))
                    for i in range(4):
                        for dq in range(4):
                            sp_idx = 16 * g + 4 * i + dq
                            if sp_idx >= 1:
                                tensor.wait_ge(sSST, sp_idx)
                            mm = None
                            for e in range(NC_CHUNKS):
                                mm = tensor.matmul(
                                    ps_s[:, :],
                                    oT_sb[g % 2][:, e * 512 + i * 128:
                                                  e * 512 + i * 128 + 128],
                                    wout[:, e * D + dq * 512:e * D + dq * 512 + 512],
                                    start=(e == 0), stop=(e == NC_CHUNKS - 1))
                            mm.then_inc(sSP, 1)


def _sst_before(sp_idx):
    k, q = divmod(sp_idx, 4)
    return 20 * k + q


def _wd(nc, outer_es, env):
    dwp_d, dsc_d, wdt_d = env["dwp_d"], env["dsc_d"], env["wdt_d"]
    ident = env["ident"]

    with ExitStack() as es:
        def sb(name, shape, dt):
            return es.enter_context(nc.sbuf_tensor(name, shape, dt))

        def sem(name):
            return es.enter_context(nc.semaphore(name))

        pk = [sb(f"wd_pk{p}", [128, 4 * 512], I32) for p in range(2)]
        sc = [sb(f"wd_sc{p}", [128, 4 * 32], F32) for p in range(2)]
        nib_i = sb("wd_nibi", [128, 1024], I32)
        halfn = [sb(f"wd_hn{p}", [128, 1024], BF16) for p in range(2)]
        m4 = sb("wd_m4", [128, 1024], BF16)
        u_ = [sb(f"wd_u{p}", [128, 1024], BF16) for p in range(2)]
        sgn = [sb(f"wd_sgn{p}", [128, 1024], BF16) for p in range(2)]
        b1 = [sb(f"wd_b1{p}", [128, 1024], BF16) for p in range(2)]
        b2 = [sb(f"wd_b2{p}", [128, 1024], BF16) for p in range(2)]
        wdq = [sb(f"wd_wdq{s}", [128, 1024], BF16) for s in range(4)]
        stg = [sb(f"wd_stg{p}", [128, 512], BF16) for p in range(8)]
        cm2 = sb("wd_cm2", [128, 1], F32)
        cm3 = sb("wd_cm3", [128, 1], F32)
        ps_tr = [es.enter_context(nc.psum_tensor(f"wd_pstr{p}", [128, 512], F32))
                 for p in range(2)]

        sLd = sem("wd_ld")
        pkwL = [sem(f"wd_pkl{p}") for p in range(2)]
        sStW = [sem(f"wd_stw{k}") for k in range(8)]
        sV1, sP, sS, sV2 = sem("wd_v1"), sem("wd_p"), sem("wd_s"), sem("wd_v2")
        sTr, sTrE, sSt = sem("wd_tr"), sem("wd_tre"), sem("wd_st")

        block = es.enter_context(nc.Block())
        NIT = 64

        @block.sync
        def _(sync):
            for it in range(NIT):
                o, r = it // 4, it % 4
                if it >= 2:
                    sync.wait_ge(sV1, 4 * (it - 1))
                    sync.wait_ge(sV2, 4 * (it - 1))
                sync.dma_start(
                    pk[it % 2][:, :].rearrange("p (s c) -> p s c", c=512),
                    dwp_d[512 * r:512 * (r + 1), 512 * o:512 * (o + 1)]
                    .rearrange("(s p) c -> p s c", p=128)).then_inc(pkwL[it % 2], 16)
                sync.dma_start(
                    sc[it % 2][:, :].rearrange("p (s c) -> p s c", c=32),
                    dsc_d[512 * r:512 * (r + 1), 32 * o:32 * (o + 1)]
                    .rearrange("(s p) c -> p s c", p=128)).then_inc(pkwL[it % 2], 16)

        @block.vector
        def _(vector):
            vector.memset(cm2[:, :], -2.0)
            vector.memset(cm3[:, :], -3.0)
            for it in range(NIT):
                vector.wait_ge(pkwL[it % 2], 32 * (it // 2 + 1))
                for s in range(4):
                    u = 4 * it + s
                    p = u % 2
                    if u >= 2:
                        vector.wait_ge(sP, u - 1)
                    lo = nib_i[:, :].rearrange("q (k two) -> q k two", two=2)[:, :, 0:1]
                    hi = nib_i[:, :].rearrange("q (k two) -> q k two", two=2)[:, :, 1:2]
                    srcp = pk[it % 2][:, s * 512:(s + 1) * 512].unsqueeze(2)
                    vector.tensor_scalar(lo, srcp, 15, None, ALU.bitwise_and)
                    vector.tensor_scalar(hi, srcp, 4, 15, ALU.logical_shift_right,
                                         ALU.bitwise_and)
                    vector.tensor_scalar(halfn[p][:, :], nib_i[:, :], 0.5, None,
                                         ALU.mult).then_inc(sV1, 1)
                    if s >= 1:
                        _wd_v2(vector, u - 1, sV2, sc, wdq,
                               u_, sgn, b1, b2, sS, sTr, ALU)
                _wd_v2(vector, 4 * it + 3, sV2, sc, wdq, u_, sgn, b1, b2, sS,
                       sTr, ALU)
                for c8 in range(8):
                    GG = 8 * it + c8
                    vector.wait_ge(sTr, GG + 1)
                    if it >= 1:
                        vector.wait_ge(sStW[c8], 16 * it)
                    vector.tensor_copy(stg[c8][:, :], ps_tr[c8 % 2][:, :]) \
                        .then_inc(sTrE, 1)

        @block.gpsimd
        def _(gpsimd):
            for u in range(4 * NIT):
                p = u % 2
                gpsimd.wait_ge(sV1, u + 1)
                if u >= 2:
                    gpsimd.wait_ge(sV2, u - 1)
                gpsimd.tensor_scalar(m4[:, :], halfn[p][:, :], 4.0, -4.0,
                                     ALU.is_ge, ALU.mult)
                gpsimd.tensor_add(u_[p][:, :], halfn[p][:, :], m4[:, :])
                gpsimd.tensor_scalar(sgn[p][:, :], m4[:, :], 0.5, 1.0,
                                     ALU.mult, ALU.add).then_inc(sP, 1)

        @block.scalar
        def _(scalar):
            def relus(scalar, it):
                for s in range(4):
                    u = 4 * it + s
                    p = u % 2
                    scalar.wait_ge(sP, u + 1)
                    if u >= 2:
                        scalar.wait_ge(sV2, u - 1)
                    scalar.activation(b1[p][:, :], u_[p][:, :], AF.Relu,
                                      bias=cm2[:, :])
                    scalar.activation(b2[p][:, :], u_[p][:, :], AF.Relu,
                                      bias=cm3[:, :]).then_inc(sS, 1)

            relus(scalar, 0)
            for it in range(NIT):
                if it + 1 < NIT:
                    relus(scalar, it + 1)
                o, r = it // 4, it % 4
                for c8 in range(8):
                    GG = 8 * it + c8
                    scalar.wait_ge(sTrE, GG + 1)
                    scalar.dma_start(wdt_d[8 * o + c8, r, :, :], stg[c8][:, :]) \
                        .then_inc(sStW[c8], 16)
            for k in range(8):
                scalar.wait_ge(sStW[k], 16 * NIT)

        @block.tensor
        def _(tensor):
            for it in range(NIT):
                tensor.wait_ge(sV2, 4 * (it + 1))
                for c8 in range(8):
                    GG = 8 * it + c8
                    if GG >= 2:
                        tensor.wait_ge(sTrE, GG - 1)
                    mm = None
                    for s in range(4):
                        mm = tensor.matmul(
                            ps_tr[c8 % 2][:, s * 128:(s + 1) * 128],
                            wdq[s][:, c8 * 128:(c8 + 1) * 128],
                            ident[:, :], start=True, stop=True)
                    mm.then_inc(sTr, 1)


def _wd_v2(vector, u, sV2, sc, wdq, u_, sgn, b1, b2, sS, sTr, _ALU):
    s = u % 4
    p = u % 2
    vector.wait_ge(sS, u + 1)
    if u >= 4:
        vector.wait_ge(sTr, 8 * (u // 4 - 1) + 8)
    vector.scalar_tensor_tensor(b2[p][:, :], b2[p][:, :], 2.0, b1[p][:, :],
                                _ALU.mult, _ALU.add)
    vector.tensor_add(u_[p][:, :], b2[p][:, :], u_[p][:, :])
    vector.tensor_mul(sgn[p][:, :], u_[p][:, :], sgn[p][:, :])
    scb = sc[(u // 4) % 2][:, s * 32:(s + 1) * 32].unsqueeze(2).broadcast_to(
        [128, 32, 32])
    vector.tensor_tensor(wdq[s][:, :].rearrange("q (k c2) -> q k c2", c2=32),
                         sgn[p][:, :].rearrange("q (k c2) -> q k c2", c2=32),
                         scb, _ALU.mult).then_inc(sV2, 1)



def _passB(nc, outer_es, env):
    x_d, s_d, out_d = env["x_d"], env["s_d"], env["out_d"]
    int_d, wdt_d = env["int_d"], env["wdt_d"]
    dbg3_d = env["dbg3_d"]

    with ExitStack() as es:
        def sb(name, shape, dt):
            return es.enter_context(nc.sbuf_tensor(name, shape, dt))

        def sem(name):
            return es.enter_context(nc.semaphore(name))

        intT = sb("pb_int", [128, 128 * 512], BF16)
        wdt = [sb(f"pb_wdt{p}", [128, 512], BF16) for p in range(4)]
        g_sb = [sb(f"pb_g{s}", [128, D], F32) for s in range(4)]
        x_t = [sb(f"pb_x{p}", [128, D], F32) for p in range(2)]
        s_t = [sb(f"pb_s{p}", [128, D], F32) for p in range(2)]
        ps_d = [es.enter_context(nc.psum_tensor(f"pb_ps{k}", [128, 512], F32))
                for k in range(8)]

        sLdI, sXS = sem("pb_ldi"), sem("pb_xs")
        wdtL = [sem(f"pb_wdtl{k}") for k in range(4)]
        sMMc = sem("pb_mmc")
        sSIL, sCMB, sOUT = sem("pb_sil"), sem("pb_cmb"), sem("pb_out")
        xsL = [sem(f"pb_xsl{p}") for p in range(2)]
        sOutS = [sem(f"pb_outs{p}") for p in range(2)]
        sDBG = sem("pb_dbg")

        block = es.enter_context(nc.Block())

        @block.sync
        def _(sync):
            for c in range(128):
                sync.dma_start(intT[:, c * 512:(c + 1) * 512], int_d[0, c, :, :]) \
                    .then_inc(sLdI, 16)
            for ts in range(NSUP):
                for hq in range(4):
                    Q = 4 * ts + hq
                    for c in range(128):
                        if Q * 128 + c >= 4:
                            sync.wait_ge(sMMc, Q * 128 + c - 3)
                        sync.dma_start(wdt[c % 4][:, :], wdt_d[c, hq, :, :]) \
                            .then_inc(wdtL[c % 4], 16)
                        if hq == 3 and ts + 1 < NSUP:
                            sync.wait_ge(sMMc, 128 * Q + c + 1)
                            sync.dma_start(intT[:, c * 512:(c + 1) * 512],
                                           int_d[ts + 1, c, :, :]).then_inc(sLdI, 16)

        @block.gpsimd
        def _(gpsimd):
            for ts in range(NSUP):
                for sub in range(4):
                    SB = 4 * ts + sub
                    if SB >= 2:
                        gpsimd.wait_ge(sCMB, SB - 1)
                        gpsimd.wait_ge(sOutS[SB % 2], 16 * (SB // 2))
                    t0 = (4 * ts + sub) * 128
                    gpsimd.dma_start(x_t[sub % 2][:, :], x_d[t0:t0 + 128, :]) \
                        .then_inc(xsL[sub % 2], 16)
                    gpsimd.dma_start(s_t[sub % 2][:, :], s_d[4 * ts + sub, :, :]) \
                        .then_inc(xsL[sub % 2], 16)

        @block.tensor
        def _(tensor):
            for ts in range(NSUP):
                for hq in range(4):
                    Q = 4 * ts + hq
                    for c in range(128):
                        n_chunk = 128 * Q + c
                        tensor.wait_ge(wdtL[c % 4], 16 * (n_chunk // 4 + 1))
                        tensor.wait_ge(sLdI, 16 * 128 * (ts + 1))
                        mm = None
                        for sub in range(4):
                            if c == 0 and Q >= 2:
                                tensor.wait_ge(sSIL, 4 * (Q - 1))
                            mm = tensor.matmul(
                                ps_d[4 * (hq % 2) + sub][:, :],
                                intT[:, c * 512 + sub * 128:c * 512 + sub * 128 + 128],
                                wdt[c % 4][:, :],
                                start=(c == 0), stop=(c == 127))
                        mm.then_inc(sMMc, 1)
            tensor.matmul(ps_d[0][0:1, 0:1], intT[:, 0:1], wdt[0][:, 0:1],
                          start=True, stop=True).then_inc(sMMc, 1)

        @block.scalar
        def _(scalar):
            for ts in range(NSUP):
                for hq in range(4):
                    Q = 4 * ts + hq
                    for sub in range(4):
                        scalar.wait_ge(sMMc, min(128 * (Q + 1) + 64,
                                                 128 * 4 * NSUP + 1))
                        if ts >= 1 and hq == 0:
                            scalar.wait_ge(sCMB, 4 * (ts - 1) + sub + 1)
                        scalar.activation(g_sb[sub][:, hq * 512:(hq + 1) * 512],
                                          ps_d[4 * (hq % 2) + sub][:, :],
                                          AF.Silu).then_inc(sSIL, 1)
                if DEBUG and ts == 0:
                    for sub in range(4):
                        scalar.wait_ge(sSIL, 12 + sub + 1)
                        scalar.dma_start(dbg3_d[sub, :, :], g_sb[sub][:, :]) \
                            .then_inc(sDBG, 16)
                    scalar.wait_ge(sDBG, 64)
                for sub in range(4):
                    SB = 4 * ts + sub
                    scalar.wait_ge(sCMB, SB + 1)
                    t0 = (4 * ts + sub) * 128
                    scalar.dma_start(out_d[t0:t0 + 128, :], s_t[sub % 2][:, :]) \
                        .then_inc(sOutS[SB % 2], 16)
            for pp in range(2):
                scalar.wait_ge(sOutS[pp], 16 * 2 * NSUP)

        @block.vector
        def _(vector):
            for ts in range(NSUP):
                for sub in range(4):
                    SB = 4 * ts + sub
                    vector.wait_ge(sSIL, 4 * (4 * ts + 3) + sub + 1)
                    vector.wait_ge(xsL[sub % 2], 32 * (SB // 2 + 1))
                    if SB >= 2:
                        vector.wait_ge(sOutS[SB % 2], 16 * (SB // 2))
                    vector.tensor_mul(s_t[sub % 2][:, :], s_t[sub % 2][:, :],
                                      g_sb[sub][:, :])
                    vector.tensor_add(s_t[sub % 2][:, :], s_t[sub % 2][:, :],
                                      x_t[sub % 2][:, :]).then_inc(sCMB, 1)



_CONSTS = None


def kernel(L0_attn_x, L0_attn_Wq, L0_attn_Wkv, L0_attn_Wout,
           L0_ffn_gate_W_packed, L0_ffn_gate_scales,
           L0_ffn_up_W_packed, L0_ffn_up_scales,
           L0_ffn_down_W_packed, L0_ffn_down_scales):
    global _CONSTS
    if _CONSTS is None:
        _CONSTS = _host_consts()
    consts = _CONSTS

    x = np.ascontiguousarray(np.asarray(L0_attn_x, dtype=np.float32))
    shared = dict(
        wq=np.ascontiguousarray(np.asarray(L0_attn_Wq, dtype=np.float32)),
        wkv=np.ascontiguousarray(np.asarray(L0_attn_Wkv, dtype=np.float32)),
        wout=np.ascontiguousarray(np.asarray(L0_attn_Wout, dtype=np.float32)),
        gwp=np.ascontiguousarray(np.asarray(L0_ffn_gate_W_packed, dtype=np.int32)),
        gsc=np.ascontiguousarray(np.asarray(L0_ffn_gate_scales, dtype=np.float32)),
        uwp=np.ascontiguousarray(np.asarray(L0_ffn_up_W_packed, dtype=np.int32)),
        usc=np.ascontiguousarray(np.asarray(L0_ffn_up_scales, dtype=np.float32)),
        dwp=np.ascontiguousarray(np.asarray(L0_ffn_down_W_packed, dtype=np.int32)),
        dsc=np.ascontiguousarray(np.asarray(L0_ffn_down_scales, dtype=np.float32)),
        **consts)

    nc = build_nc()
    in_maps = [dict(x=np.ascontiguousarray(x[b]), **shared) for b in range(8)]
    res = run_bass_kernel_spmd(nc, in_maps, core_ids=list(range(8)))
    out = np.stack([res.results[b]["out"] for b in range(8)], axis=0)
    return out


# version 03
# speedup vs baseline: 1.0916x; 1.0916x over previous
import numpy as np
import ml_dtypes
from contextlib import ExitStack

import concourse.bass as bass
import concourse.mybir as mybir
from concourse.bass_utils import run_bass_kernel_spmd

F32 = mybir.dt.float32
BF16 = mybir.dt.bfloat16
I32 = mybir.dt.int32
AF = mybir.ActivationFunctionType
ALU = mybir.AluOpType
AX = mybir.AxisListType

S_LEN = 4096
D = 2048
HEADS = 16
DH = 128
NMLP = 16384
NT = S_LEN // 128
NC_CHUNKS = D // 128
SCALE = DH ** -0.5
EPS = 1e-5
TSUP = 512
NSUP = S_LEN // TSUP

DEBUG = False
PA_STAGE = 3
PA_NCH = None
PHASES = "1AWTB"


def _host_consts():
    ident = np.eye(128, dtype=ml_dtypes.bfloat16)
    inv = 1.0 / 10000.0 ** (np.arange(0, DH, 2, dtype=np.float64) / DH)
    tpos = np.arange(S_LEN, dtype=np.float64)[:, None] * inv[None, :]
    cos = np.cos(tpos)
    sin = np.sin(tpos)
    cosf = np.concatenate([cos, cos], 1)
    sinf = np.concatenate([sin, sin], 1)
    rope = np.concatenate(
        [cosf * SCALE, sinf * SCALE, cosf, sinf], axis=1).astype(np.float32)
    jj = np.arange(512)[None, :]
    ii = np.arange(128)[:, None]
    mask = np.zeros((4, 128, 512), dtype=ml_dtypes.bfloat16)
    for v in range(4):
        mask[v] = (jj <= (128 * v + ii)).astype(ml_dtypes.bfloat16)
    ones = np.ones((1, 128), dtype=ml_dtypes.bfloat16)
    return dict(c_ident=ident, c_rope=rope, c_mask=mask.reshape(512, 512),
                c_ones=ones)


def build_nc():
    nc = bass.Bass("TRN2", target_bir_lowering=False)

    x_d = nc.dram_tensor("x", [S_LEN, D], F32, kind="ExternalInput").ap()
    wq_d = nc.dram_tensor("wq", [D, D], F32, kind="ExternalInput").ap()
    wkv_d = nc.dram_tensor("wkv", [D, 256], F32, kind="ExternalInput").ap()
    wout_d = nc.dram_tensor("wout", [D, D], F32, kind="ExternalInput").ap()
    gwp_d = nc.dram_tensor("gwp", [NMLP, D // 2], I32, kind="ExternalInput").ap()
    gsc_d = nc.dram_tensor("gsc", [NMLP, 64], F32, kind="ExternalInput").ap()
    uwp_d = nc.dram_tensor("uwp", [NMLP, D // 2], I32, kind="ExternalInput").ap()
    usc_d = nc.dram_tensor("usc", [NMLP, 64], F32, kind="ExternalInput").ap()
    dwp_d = nc.dram_tensor("dwp", [D, NMLP // 2], I32, kind="ExternalInput").ap()
    dsc_d = nc.dram_tensor("dsc", [D, NMLP // 32], F32, kind="ExternalInput").ap()
    ident_d = nc.dram_tensor("c_ident", [128, 128], BF16, kind="ExternalInput").ap()
    rope_d = nc.dram_tensor("c_rope", [S_LEN, 512], F32, kind="ExternalInput").ap()
    mask_d = nc.dram_tensor("c_mask", [512, 512], BF16, kind="ExternalInput").ap()
    ones_d = nc.dram_tensor("c_ones", [1, 128], BF16, kind="ExternalInput").ap()

    out_d = nc.dram_tensor("out", [S_LEN, D], F32, kind="ExternalOutput").ap()

    skind = dict(kind="ExternalOutput") if DEBUG else {}
    xn_d = nc.dram_tensor("xn_s", [S_LEN, D], BF16, **skind).ap()
    qt_d = nc.dram_tensor("qt_s", [NT, 128, D], BF16, **skind).ap()
    kt_d = nc.dram_tensor("kt_s", [128, S_LEN], BF16, **skind).ap()
    v_d = nc.dram_tensor("v_s", [128, S_LEN], BF16, **skind).ap()
    int_d = nc.dram_tensor("int_s", [NSUP, 128, 128, TSUP], BF16, **skind).ap()
    wdt_d = nc.dram_tensor("wdt_s", [128, 4, 128, 512], BF16, **skind).ap()
    s_d = nc.dram_tensor("s_s", [NT, 128, D], F32, **skind).ap()
    dbg_d = nc.dram_tensor("dbg_s", [128, 2 * NT * HEADS + 1], F32, **skind).ap()
    dbg2_d = nc.dram_tensor("dbg2_s", [128, 512], F32, **skind).ap()
    dbg3_d = nc.dram_tensor("dbg3_s", [4, 128, D], F32, **skind).ap()

    with ExitStack() as es:
        def sb(name, shape, dt):
            return es.enter_context(nc.sbuf_tensor(name, shape, dt))

        ident = sb("ident", [128, 128], BF16)
        ones = sb("ones", [1, 128], BF16)
        kn2 = sb("kn2", [128, NT], F32)
        qn2 = sb("qn2", [128, NT * HEADS], F32)
        k2m_b = sb("k2m_b", [128, 1], F32)
        czero = sb("czero", [128, 1], F32)
        ceps = sb("ceps", [128, 1], F32)

        if "1" in PHASES:
            _stage1(nc, es, locals())
        if "A" in PHASES:
            _passA(nc, es, locals())
        if "T" in PHASES:
            _attn(nc, es, locals())
        if "W" in PHASES:
            _wd(nc, es, locals())
        if "B" in PHASES:
            _passB(nc, es, locals())

    return nc




def _stage1(nc, outer_es, env):
    x_d, wq_d, wkv_d = env["x_d"], env["wq_d"], env["wkv_d"]
    ident_d, rope_d, ones_d = env["ident_d"], env["rope_d"], env["ones_d"]
    xn_d, qt_d, kt_d, v_d = env["xn_d"], env["qt_d"], env["kt_d"], env["v_d"]
    ident, ones, kn2, qn2, k2m_b = (env["ident"], env["ones"], env["kn2"],
                                    env["qn2"], env["k2m_b"])
    czero, ceps = env["czero"], env["ceps"]

    with ExitStack() as es:
        def sb(name, shape, dt):
            return es.enter_context(nc.sbuf_tensor(name, shape, dt))

        def sem(name):
            return es.enter_context(nc.semaphore(name))

        wq_f = [sb(f"wq_f{p}", [128, D], F32) for p in range(2)]
        wq_bf = sb("wq_bf", [128, NC_CHUNKS * D], BF16)
        wkv_f = sb("wkv_f", [128, NC_CHUNKS * 256], F32)
        wkv_bf = sb("wkv_bf", [128, NC_CHUNKS * 256], BF16)
        x_t = [sb(f"x_t{p}", [128, D], F32) for p in range(2)]
        rope_t = [sb(f"rope_t{p}", [128, 512], F32) for p in range(2)]
        xn_t = [sb(f"xn_t{p}", [128, D], BF16) for p in range(2)]
        xnT_t = [sb(f"xnT_t{p}", [128, D], BF16) for p in range(2)]
        q_f = sb("q_f", [128, D], F32)
        q_ro = sb("q_ro", [128, D], BF16)
        qT_t = [sb(f"qT_t{p}", [128, D], BF16) for p in range(2)]
        sq_s = sb("sq_s", [128, D], F32)
        k_f = sb("k_f", [128, DH], F32)
        k_ro = sb("k_ro", [128, DH], BF16)
        kT_sb = sb("kT_sb", [128, S_LEN], BF16)
        v_sb = sb("v_sb", [128, S_LEN], BF16)
        stats = [sb(f"stats{p}", [128, 8], F32) for p in range(2)]
        ksc = sb("ksc", [128, 2], F32)
        k2bf = sb("k2bf", [128, 1], BF16)

        ps_q = es.enter_context(nc.psum_tensor("ps_q", [128, D], F32))
        ps_kv = es.enter_context(nc.psum_tensor("ps_kv", [128, 256], F32))
        ps_tr = [es.enter_context(nc.psum_tensor(f"ps_tr{p}", [128, 512], F32))
                 for p in range(2)]
        ps_b = es.enter_context(nc.psum_tensor("ps_b", [128, 128], F32))

        sLd, sLdW, sWcv = sem("s1_ld"), sem("s1_ldw"), sem("s1_wcv")
        xL = [sem(f"s1_xl{p}") for p in range(2)]
        rL = [sem(f"s1_rl{p}") for p in range(2)]
        wqL = [sem(f"s1_wql{p}") for p in range(2)]
        sStX = [sem(f"s1_stx{p}") for p in range(2)]
        sStQ = [sem(f"s1_stq{p}") for p in range(2)]
        sSq, sLN, sXn = sem("s1_sq"), sem("s1_ln"), sem("s1_xn")
        sTr, sTrE, sKT = sem("s1_tr"), sem("s1_tre"), sem("s1_kt")
        sMM, sQE, sRo = sem("s1_mm"), sem("s1_qe"), sem("s1_ro")
        sKV, sSt, sFin = sem("s1_kv"), sem("s1_st"), sem("s1_fin")

        block = es.enter_context(nc.Block())

        @block.sync
        def _(sync):
            sync.dma_start(ident[:, :], ident_d).then_inc(sLd, 16)
            sync.dma_start(ones[:, :], ones_d).then_inc(sLd, 16)
            sync.dma_start(
                wkv_f[:, :].rearrange("p (c n) -> p c n", n=256),
                wkv_d.rearrange("(c p) n -> p c n", p=128)).then_inc(sLdW, 16)
            for c in range(NC_CHUNKS):
                if c >= 2:
                    sync.wait_ge(sWcv, c - 1)
                sync.dma_start(wq_f[c % 2][:, :], wq_d[c * 128:(c + 1) * 128, :]) \
                    .then_inc(wqL[c % 2], 16)
            for t in range(NT):
                if t >= 2:
                    sync.wait_ge(sXn, t - 1)
                    sync.wait_ge(sSq, 2 * (t - 2) + 1)
                    sync.wait_ge(sKV, 4 * (t - 2) + 2)
                sync.dma_start(x_t[t % 2][:, :], x_d[t * 128:(t + 1) * 128, :]) \
                    .then_inc(xL[t % 2], 16)
                sync.dma_start(rope_t[t % 2][:, :], rope_d[t * 128:(t + 1) * 128, :]) \
                    .then_inc(rL[t % 2], 16)

        @block.vector
        def _(vector):
            vector.memset(czero[:, :], 0.0)
            vector.memset(ceps[:, :], EPS)
            vector.wait_ge(sLdW, 16)
            vector.tensor_copy(wkv_bf[:, :], wkv_f[:, :])
            for c in range(NC_CHUNKS):
                vector.wait_ge(wqL[c % 2], 16 * (c // 2 + 1))
                vector.tensor_copy(wq_bf[:, c * D:(c + 1) * D], wq_f[c % 2][:, :]) \
                    .then_inc(sWcv, 1)
            for t in range(NT):
                p = t % 2
                st = stats[p]
                vector.wait_ge(xL[t % 2], 16 * (t // 2 + 1))
                vector.tensor_reduce(st[:, 0:1], x_t[p][:, :], AX.X, ALU.add,
                                     negate=True).then_inc(sLN, 1)
                vector.wait_ge(sLN, 6 * t + 1)
                vector.tensor_scalar(st[:, 2:3], st[:, 0:1], 1.0 / D, None,
                                     ALU.mult).then_inc(sLN, 1)
                vector.wait_ge(sLN, 6 * t + 2)
                vector.tensor_mul(st[:, 3:4], st[:, 2:3], st[:, 2:3]) \
                    .then_inc(sLN, 1)
                vector.wait_ge(sSq, 2 * t + 1)
                vector.wait_ge(sLN, 6 * t + 3)
                vector.scalar_tensor_tensor(st[:, 4:5], st[:, 1:2], 1.0 / D,
                                            st[:, 3:4], ALU.mult, ALU.subtract) \
                    .then_inc(sLN, 1)
                vector.wait_ge(sLN, 6 * t + 5)
                vector.reciprocal(st[:, 6:7], st[:, 5:6]).then_inc(sLN, 1)
                vector.wait_ge(sLN, 6 * t + 6)
                if t >= 2:
                    vector.wait_ge(sStX[t % 2], 16 * (t // 2))
                vector.tensor_scalar(xn_t[p][:, :], x_t[p][:, :], st[:, 2:3],
                                     st[:, 6:7], ALU.add, ALU.mult) \
                    .then_inc(sXn, 1)
                if t >= 2:
                    vector.wait_ge(sMM, 80 * (t - 1))
                for g in range(4):
                    H = 8 * t + g
                    vector.wait_ge(sTr, H + 1)
                    vector.tensor_copy(
                        xnT_t[p][:, :].rearrange("q (c n) -> q c n", n=128)
                        [:, 4 * g:4 * g + 4, :],
                        ps_tr[g % 2][:, :].rearrange("q (c n) -> q c n", n=128)) \
                        .then_inc(sTrE, 1)
                vector.wait_ge(sQE, 8 * t + 4)
                vector.wait_ge(rL[t % 2], 16 * (t // 2 + 1))
                qv = q_f[:, :].rearrange("q (h c) -> q h c", c=DH)
                ov = q_ro[:, :].rearrange("q (h c) -> q h c", c=DH)
                sv = sq_s[:, :].rearrange("q (h c) -> q h c", c=DH)
                rp = rope_t[p]
                clo = rp[:, 0:64].unsqueeze(1).broadcast_to([128, HEADS, 64])
                chi = rp[:, 64:128].unsqueeze(1).broadcast_to([128, HEADS, 64])
                slo = rp[:, 128:192].unsqueeze(1).broadcast_to([128, HEADS, 64])
                shi = rp[:, 192:256].unsqueeze(1).broadcast_to([128, HEADS, 64])
                vector.tensor_tensor(sv[:, :, 0:64], qv[:, :, 0:64], clo, ALU.mult)
                vector.tensor_tensor(sv[:, :, 64:128], qv[:, :, 64:128], slo, ALU.mult)
                vector.tensor_tensor(ov[:, :, 0:64], sv[:, :, 0:64], sv[:, :, 64:128],
                                     ALU.subtract)
                vector.tensor_tensor(sv[:, :, 0:64], qv[:, :, 64:128], chi, ALU.mult)
                vector.tensor_tensor(sv[:, :, 64:128], qv[:, :, 0:64], shi, ALU.mult)
                vector.tensor_add(ov[:, :, 64:128], sv[:, :, 0:64], sv[:, :, 64:128]) \
                    .then_inc(sRo, 4)
                vector.wait_ge(sQE, 8 * t + 8)
                vector.tensor_reduce(
                    qn2[:, t * HEADS:(t + 1) * HEADS].unsqueeze(2),
                    sq_s[:, :].rearrange("q (h c) -> q h c", c=DH),
                    AX.X, ALU.add).then_inc(sRo, 1)
                if t >= 2:
                    vector.wait_ge(sStQ[t % 2], 16 * (t // 2))
                for g in range(4):
                    H = 8 * t + 4 + g
                    vector.wait_ge(sTr, H + 1)
                    vector.tensor_copy(
                        qT_t[p][:, :].rearrange("q (c n) -> q c n", n=128)
                        [:, 4 * g:4 * g + 4, :],
                        ps_tr[g % 2][:, :].rearrange("q (c n) -> q c n", n=128)) \
                        .then_inc(sTrE, 1)
                vector.wait_ge(sKV, 4 * t + 1)
                t1k, t2k = sq_s[:, 0:64], sq_s[:, 64:128]
                vector.tensor_tensor(t1k, k_f[:, 0:64], rp[:, 256:320], ALU.mult) \
                    .then_inc(sRo, 1)
                vector.tensor_tensor(t2k, k_f[:, 64:128], rp[:, 384:448], ALU.mult) \
                    .then_inc(sRo, 1)
                vector.wait_ge(sRo, 10 * t + 7)
                vector.tensor_tensor(k_ro[:, 0:64], t1k, t2k, ALU.subtract) \
                    .then_inc(sRo, 1)
                vector.tensor_tensor(t1k, k_f[:, 64:128], rp[:, 320:384], ALU.mult) \
                    .then_inc(sRo, 1)
                vector.tensor_tensor(t2k, k_f[:, 0:64], rp[:, 448:512], ALU.mult) \
                    .then_inc(sRo, 1)
                vector.wait_ge(sRo, 10 * t + 10)
                vector.tensor_add(k_ro[:, 64:128], t1k, t2k).then_inc(sKV, 1)
                vector.wait_ge(sMM, 80 * (t + 1))
                vector.tensor_copy(v_sb[:, t * 128:(t + 1) * 128], ps_kv[:, 128:256]) \
                    .then_inc(sKV, 1)
                vector.wait_ge(sKT, t + 1)
                vector.tensor_copy(kT_sb[:, t * 128:(t + 1) * 128], ps_b[:, :]) \
                    .then_inc(sKV, 1)
            vector.reduce_max(ksc[:, 0:1], kn2[:, :], AX.X).then_inc(sFin, 1)
            vector.wait_ge(sFin, 1)
            vector.tensor_copy(k2bf[:, :], ksc[:, 0:1]).then_inc(sFin, 1)
            vector.wait_ge(sFin, 3)
            vector.reduce_max(ksc[0:1, 1:2], ps_b[0:1, 0:128], AX.X).then_inc(sFin, 1)
            vector.wait_ge(sFin, 4)
            vector.tensor_copy(k2bf[0:1, 0:1], ksc[0:1, 1:2]).then_inc(sFin, 1)
            vector.wait_ge(sFin, 6)
            vector.tensor_copy(k2m_b[:, :], ps_b[:, 0:1]).then_inc(sFin, 1)

        @block.scalar
        def _(scalar):
            for t in range(NT):
                p = t % 2
                st = stats[p]
                scalar.wait_ge(xL[t % 2], 16 * (t // 2 + 1))
                if t >= 1:
                    scalar.wait_ge(sKV, 4 * (t - 1) + 2)
                scalar.activation(sq_s[:, :], x_t[p][:, :], AF.Square,
                                  accum_out=st[:, 1:2]).then_inc(sSq, 1)
                scalar.wait_ge(sLN, 6 * t + 4)
                scalar.activation(st[:, 5:6], st[:, 4:5], AF.Sqrt,
                                  bias=ceps[:, :]).then_inc(sLN, 1)
                scalar.wait_ge(sMM, 80 * (t + 1))
                for qq in range(4):
                    scalar.activation(q_f[:, qq * 512:(qq + 1) * 512],
                                      ps_q[:, qq * 512:(qq + 1) * 512],
                                      AF.Copy).then_inc(sQE, 1)
                scalar.wait_ge(sRo, 10 * t + 4)
                scalar.activation(sq_s[:, :], q_ro[:, :], AF.Square) \
                    .then_inc(sQE, 4)
                scalar.activation(k_f[:, :], ps_kv[:, 0:128], AF.Copy) \
                    .then_inc(sKV, 1)
                scalar.wait_ge(sKV, 4 * t + 2)
                scalar.activation(sq_s[:, 1920:2048], k_ro[:, :], AF.Square,
                                  accum_out=kn2[:, t:t + 1]).then_inc(sSq, 1)
                scalar.wait_ge(sXn, t + 1)
                scalar.dma_start(xn_d[t * 128:(t + 1) * 128, :], xn_t[p][:, :]) \
                    .then_inc(sStX[t % 2], 16)
                scalar.wait_ge(sTrE, 8 * t + 8)
                scalar.dma_start(qt_d[t, :, :], qT_t[p][:, :]).then_inc(sStQ[t % 2], 16)
            scalar.wait_ge(sKV, 4 * NT)
            scalar.dma_start(kt_d, kT_sb[:, :]).then_inc(sSt, 16)
            scalar.dma_start(v_d, v_sb[:, :]).then_inc(sSt, 16)
            scalar.wait_ge(sSt, 32)
            for p in range(2):
                scalar.wait_ge(sStX[p], 16 * (NT // 2))
                scalar.wait_ge(sStQ[p], 16 * (NT // 2))
            scalar.wait_ge(sFin, 7)

        @block.tensor
        def _(tensor):
            tensor.wait_ge(sLd, 32)
            for t in range(NT):
                p = t % 2
                tensor.wait_ge(sXn, t + 1)
                for g in range(4):
                    H = 8 * t + g
                    if H >= 2:
                        tensor.wait_ge(sTrE, H - 1)
                    mm = None
                    for k in range(4):
                        c = 4 * g + k
                        mm = tensor.matmul(ps_tr[g % 2][:, k * 128:(k + 1) * 128],
                                           xn_t[p][:, c * 128:(c + 1) * 128],
                                           ident[:, :], start=True, stop=True)
                    mm.then_inc(sTr, 1)
                if t == 0:
                    tensor.wait_ge(sWcv, NC_CHUNKS)
                tensor.wait_ge(sTrE, 8 * t + 4)
                if t > 0:
                    tensor.wait_ge(sQE, 8 * (t - 1) + 4)
                    tensor.wait_ge(sKV, 4 * (t - 1) + 3)
                for c in range(NC_CHUNKS):
                    lhs = xnT_t[p][:, c * 128:(c + 1) * 128]
                    for qq in range(4):
                        tensor.matmul(ps_q[:, qq * 512:(qq + 1) * 512], lhs,
                                      wq_bf[:, c * D + qq * 512:c * D + qq * 512 + 512],
                                      start=(c == 0), stop=(c == NC_CHUNKS - 1)) \
                            .then_inc(sMM, 1)
                    tensor.matmul(ps_kv[:, :], lhs, wkv_bf[:, c * 256:(c + 1) * 256],
                                  start=(c == 0), stop=(c == NC_CHUNKS - 1)) \
                        .then_inc(sMM, 1)
                tensor.wait_ge(sRo, 10 * t + 4)
                for g in range(4):
                    H = 8 * t + 4 + g
                    tensor.wait_ge(sTrE, H - 1)
                    mm = None
                    for k in range(4):
                        c = 4 * g + k
                        mm = tensor.matmul(ps_tr[g % 2][:, k * 128:(k + 1) * 128],
                                           q_ro[:, c * 128:(c + 1) * 128],
                                           ident[:, :], start=True, stop=True)
                    mm.then_inc(sTr, 1)
                tensor.wait_ge(sKV, 4 * t + 2)
                tensor.matmul(ps_b[:, :], k_ro[:, :], ident[:, :],
                              start=True, stop=True).then_inc(sKT, 1)
            tensor.wait_ge(sFin, 2)
            tensor.matmul(ps_b[0:1, 0:128], k2bf[:, :], ident[:, :],
                          start=True, stop=True).then_inc(sFin, 1)
            tensor.wait_ge(sFin, 5)
            tensor.matmul(ps_b[:, 0:1], ones[0:1, :], k2bf[0:1, 0:1],
                          start=True, stop=True).then_inc(sFin, 1)




def _passA(nc, outer_es, env):
    xn_d, int_d = env["xn_d"], env["int_d"]
    gwp_d, gsc_d, uwp_d, usc_d = env["gwp_d"], env["gsc_d"], env["uwp_d"], env["usc_d"]
    ident = env["ident"]
    NCH = PA_NCH if PA_NCH is not None else NMLP // 128

    with ExitStack() as es:
        def sb(name, shape, dt):
            return es.enter_context(nc.sbuf_tensor(name, shape, dt))

        def sem(name):
            return es.enter_context(nc.semaphore(name))

        xnT = sb("pa_xnT", [128, NC_CHUNKS * S_LEN], BF16)
        pg = [sb(f"pa_pg{p}", [128, 1024], I32) for p in range(2)]
        pu = [sb(f"pa_pu{p}", [128, 1024], I32) for p in range(2)]
        sg = [sb(f"pa_sg{p}", [128, 64], F32) for p in range(2)]
        su = [sb(f"pa_su{p}", [128, 64], F32) for p in range(2)]
        nib_i = sb("pa_nibi", [128, 1024], I32)
        halfn = [sb(f"pa_hn{p}", [128, 1024], BF16) for p in range(2)]
        m4 = sb("pa_m4", [128, 1024], BF16)
        u_ = [sb(f"pa_u{p}", [128, 1024], BF16) for p in range(2)]
        sgn = [sb(f"pa_sgn{p}", [128, 1024], BF16) for p in range(2)]
        b1 = [sb(f"pa_b1{p}", [128, 1024], BF16) for p in range(2)]
        b2 = [sb(f"pa_b2{p}", [128, 1024], BF16) for p in range(2)]
        wdq = [sb(f"pa_wdq{p}", [128, D], BF16) for p in range(2)]
        wT = {"g": [sb(f"pa_wTg{p}", [128, D], BF16) for p in range(2)],
              "u": [sb(f"pa_wTu{p}", [128, D], BF16) for p in range(2)]}
        xn_l = [wT["g"][0], wT["g"][1]]
        gel = [sb(f"pa_gel{p}", [128, 512], F32) for p in range(2)]
        int_sb = [sb(f"pa_int{p}", [128, 512], BF16) for p in range(2)]
        cm2 = sb("pa_cm2", [128, 1], F32)
        cm3 = sb("pa_cm3", [128, 1], F32)
        cm375 = sb("pa_cm375", [128, 1], F32)

        ps_tr = [es.enter_context(nc.psum_tensor(f"pa_pstr{p}", [128, 512], F32))
                 for p in range(2)]
        ps_g = [es.enter_context(nc.psum_tensor(f"pa_psg{p}", [128, 512], F32))
                for p in range(2)]
        ps_u = [es.enter_context(nc.psum_tensor(f"pa_psu{p}", [128, 512], F32))
                for p in range(2)]

        sLdA, sLdW2 = sem("pa_lda"), sem("pa_ldw")
        xnL = [sem(f"pa_xnl{p}") for p in range(2)]
        pkL = [sem(f"pa_pkl{p}") for p in range(2)]
        sStI = [sem(f"pa_sti{p}") for p in range(2)]
        sTr, sTrE = sem("pa_tr"), sem("pa_tre")
        sDqV1, sDqP, sDqS, sDqV2 = (sem("pa_dqv1"), sem("pa_dqp"),
                                    sem("pa_dqs"), sem("pa_dqv2"))
        sDqS1 = sem("pa_dqs1")
        sMMA, sGel, sInt, sStA = sem("pa_mma"), sem("pa_gel"), sem("pa_int"), sem("pa_sta")

        block = es.enter_context(nc.Block())

        def v1_emit(vector, c, wi, hp):
            u = 4 * c + 2 * wi + hp
            wp = (pg if wi == 0 else pu)[c % 2]
            if u >= 2:
                vector.wait_ge(sDqP, u - 1)
            if hp == 0:
                vector.wait_ge(pkL[c % 2], 64 * (c // 2 + 1))
            lo = nib_i[:, :].rearrange("q (k two) -> q k two", two=2)[:, :, 0:1]
            hi = nib_i[:, :].rearrange("q (k two) -> q k two", two=2)[:, :, 1:2]
            srcp = wp[:, hp * 512:(hp + 1) * 512].unsqueeze(2)
            vector.tensor_scalar(lo, srcp, 15, None, ALU.bitwise_and)
            vector.tensor_scalar(hi, srcp, 4, 15, ALU.logical_shift_right,
                                 ALU.bitwise_and)
            vector.tensor_scalar(halfn[u % 2][:, :], nib_i[:, :], 0.5, None,
                                 ALU.mult).then_inc(sDqV1, 1)

        def v2_emit(vector, c, wi, hp):
            u = 4 * c + 2 * wi + hp
            cw = 2 * c + wi
            sc = (sg if wi == 0 else su)[c % 2]
            p = u % 2
            vector.wait_ge(sDqS, u + 1)
            if cw >= 2 and PA_STAGE >= 2:
                vector.wait_ge(sTr, 128 + 4 * (cw - 2) + 4)
            vector.scalar_tensor_tensor(b2[p][:, :], b2[p][:, :], 2.0, b1[p][:, :],
                                        ALU.mult, ALU.add)
            vector.tensor_add(u_[p][:, :], b2[p][:, :], u_[p][:, :])
            vector.tensor_mul(sgn[p][:, :], u_[p][:, :], sgn[p][:, :])
            scb = sc[:, hp * 32:(hp + 1) * 32].unsqueeze(2).broadcast_to(
                [128, 32, 32])
            vector.tensor_tensor(
                wdq[cw % 2][:, hp * 1024:(hp + 1) * 1024].rearrange(
                    "q (k c2) -> q k c2", c2=32),
                sgn[p][:, :].rearrange("q (k c2) -> q k c2", c2=32),
                scb, ALU.mult).then_inc(sDqV2, 1)

        @block.sync
        def _(sync):
            for t in range(NT):
                if t >= 2:
                    sync.wait_ge(sTr, 4 * (t - 1))
                sync.dma_start(xn_l[t % 2][:, :], xn_d[t * 128:(t + 1) * 128, :]) \
                    .then_inc(xnL[t % 2], 16)
            for c in range(NCH if PA_STAGE >= 1 else 0):
                if c >= 2:
                    sync.wait_ge(sDqV1, 4 * (c - 1))
                    sync.wait_ge(sDqV2, 4 * (c - 1))
                sync.dma_start(pg[c % 2][:, :], gwp_d[c * 128:(c + 1) * 128, :]) \
                    .then_inc(pkL[c % 2], 16)
                sync.dma_start(sg[c % 2][:, :], gsc_d[c * 128:(c + 1) * 128, :]) \
                    .then_inc(pkL[c % 2], 16)
                sync.dma_start(pu[c % 2][:, :], uwp_d[c * 128:(c + 1) * 128, :]) \
                    .then_inc(pkL[c % 2], 16)
                sync.dma_start(su[c % 2][:, :], usc_d[c * 128:(c + 1) * 128, :]) \
                    .then_inc(pkL[c % 2], 16)

        @block.vector
        def _(vector):
            vector.memset(cm2[:, :], -2.0)
            vector.memset(cm3[:, :], -3.0)
            vector.memset(cm375[:, :], -3.75)
            for t in range(NT):
                for g in range(4):
                    GG = 4 * t + g
                    vector.wait_ge(sTr, GG + 1)
                    vector.tensor_copy(
                        xnT[:, :].rearrange("q (ch n) -> q ch n", n=S_LEN)
                        [:, 4 * g:4 * g + 4, t * 128:(t + 1) * 128],
                        ps_tr[GG % 2][:, :].rearrange("q (k n) -> q k n", n=128)) \
                        .then_inc(sTrE, 1)
            units = [(c, wi, hp) for c in range(NCH) for wi in range(2)
                     for hp in range(2)]
            def units_blk(vector, cc):
                for k in range(4):
                    uu = 4 * cc + k
                    v1_emit(vector, *units[uu])
                    if k >= 2:
                        v2_emit(vector, *units[uu - 2])
                v2_emit(vector, *units[4 * cc + 2])
                v2_emit(vector, *units[4 * cc + 3])

            def evacs_blk(vector, cc):
                for wi in range(2):
                    cw = 2 * cc + wi
                    for g in range(4):
                        GG = 128 + 4 * cw + g
                        vector.wait_ge(sTr, GG + 1)
                        if cc >= 2 and PA_STAGE in (3, 5, 6):
                            vector.wait_ge(sMMA, 16 * (cc - 1))
                        vector.tensor_copy(
                            wT[("g", "u")[wi]][cc % 2][:, :].rearrange(
                                "q (k n) -> q k n", n=128)[:, 4 * g:4 * g + 4, :],
                            ps_tr[GG % 2][:, :].rearrange(
                                "q (k n) -> q k n", n=128)).then_inc(sTrE, 1)

            def inter_blk(vector, cm, tcs):
                for tc in tcs:
                    G2 = 8 * cm + tc
                    vector.wait_ge(sGel, G2 + 1)
                    vector.wait_ge(sMMA, 16 * cm + 2 * tc + 2)
                    if G2 >= 2:
                        vector.wait_ge(sStI[tc % 2], 16 * (G2 // 2))
                    vector.tensor_tensor(int_sb[tc % 2][:, :], gel[tc % 2][:, :],
                                         ps_u[tc % 2][:, :], ALU.mult) \
                        .then_inc(sInt, 1)

            if PA_STAGE >= 1:
                units_blk(vector, 0)
                if PA_STAGE >= 2:
                    evacs_blk(vector, 0)
                for c in range(NCH):
                    if c + 1 < NCH:
                        units_blk(vector, c + 1)
                    if PA_STAGE == 3:
                        inter_blk(vector, c, range(0, 4))
                    if PA_STAGE >= 2 and c + 1 < NCH:
                        evacs_blk(vector, c + 1)
                    if PA_STAGE == 3:
                        inter_blk(vector, c, range(4, 8))

        @block.gpsimd
        def _(gpsimd):
            for u in range(4 * NCH if PA_STAGE >= 1 else 0):
                p = u % 2
                gpsimd.wait_ge(sDqV1, u + 1)
                if u >= 2:
                    gpsimd.wait_ge(sDqV2, u - 1)
                gpsimd.tensor_scalar(m4[:, :], halfn[p][:, :], 4.0, -4.0,
                                     ALU.is_ge, ALU.mult)
                gpsimd.tensor_add(u_[p][:, :], halfn[p][:, :], m4[:, :])
                gpsimd.tensor_scalar(sgn[p][:, :], m4[:, :], 0.5, 1.0,
                                     ALU.mult, ALU.add).then_inc(sDqP, 1)

        @block.scalar
        def _(scalar):
            def _sign(scalar, u):
                p = u % 2
                scalar.wait_ge(sDqV1, u + 1)
                if u >= 2:
                    scalar.wait_ge(sDqV2, u - 1)
                scalar.activation(sgn[p][:, :], halfn[p][:, :], AF.Sign,
                                  bias=cm375[:, :]).then_inc(sDqS1, 1)

            def _relus(scalar, u):
                p = u % 2
                scalar.wait_ge(sDqP, u + 1)
                if u >= 2:
                    scalar.wait_ge(sDqV2, u - 1)
                scalar.activation(b1[p][:, :], u_[p][:, :], AF.Relu,
                                  bias=cm2[:, :])
                scalar.activation(b2[p][:, :], u_[p][:, :], AF.Relu,
                                  bias=cm3[:, :]).then_inc(sDqS, 1)

            def relus_blk(scalar, cc):
                for k in range(4):
                    u = 4 * cc + k
                    _sign(scalar, u)
                    if k >= 1:
                        _relus(scalar, u - 1)
                _relus(scalar, 4 * cc + 3)

            if PA_STAGE >= 1:
                relus_blk(scalar, 0)
            for c in range(NCH if PA_STAGE >= 1 else 0):
                if c + 1 < NCH:
                    relus_blk(scalar, c + 1)
                for tc in range(8 if PA_STAGE in (3, 6) else 0):
                    G2 = 8 * c + tc
                    scalar.wait_ge(sMMA, 16 * c + 2 * tc + 1)
                    if G2 >= 2:
                        scalar.wait_ge(sInt, G2 - 1)
                    scalar.activation(gel[tc % 2][:, :], ps_g[tc % 2][:, :],
                                      AF.Gelu_apprx_tanh).then_inc(sGel, 1)
                    if tc >= 1 and PA_STAGE == 3:
                        scalar.wait_ge(sInt, G2)
                        scalar.dma_start(int_d[tc - 1, c, :, :],
                                         int_sb[(tc - 1) % 2][:, :]) \
                            .then_inc(sStI[(tc - 1) % 2], 16)
                if PA_STAGE == 3:
                    scalar.wait_ge(sInt, 8 * c + 8)
                    scalar.dma_start(int_d[7, c, :, :], int_sb[1][:, :]) \
                        .then_inc(sStI[1], 16)
            if PA_STAGE == 3:
                for pp in range(2):
                    scalar.wait_ge(sStI[pp], 16 * 4 * NCH)

        @block.tensor
        def _(tensor):
            for t in range(NT):
                tensor.wait_ge(xnL[t % 2], 16 * (t // 2 + 1))
                for g in range(4):
                    GG = 4 * t + g
                    if GG >= 2:
                        tensor.wait_ge(sTrE, GG - 1)
                    mm = None
                    for k in range(4):
                        ch = 4 * g + k
                        mm = tensor.matmul(
                            ps_tr[GG % 2][:, k * 128:(k + 1) * 128],
                            xn_l[t % 2][:, ch * 128:(ch + 1) * 128],
                            ident[:, :], start=True, stop=True)
                    mm.then_inc(sTr, 1)
            def trans_blk(tensor, cc):
                for wi in range(2):
                    cw = 2 * cc + wi
                    tensor.wait_ge(sDqV2, 4 * cc + 2 * wi + 2)
                    for g in range(4):
                        GG = 128 + 4 * cw + g
                        tensor.wait_ge(sTrE, GG - 1)
                        mm = None
                        for k in range(4):
                            ch = 4 * g + k
                            mm = tensor.matmul(
                                ps_tr[GG % 2][:, k * 128:(k + 1) * 128],
                                wdq[cw % 2][:, ch * 128:(ch + 1) * 128],
                                ident[:, :], start=True, stop=True)
                        mm.then_inc(sTr, 1)

            def mms_blk(tensor, c, tcs):
                for tc in tcs:
                    G2 = 8 * c + tc
                    if G2 >= 2 and PA_STAGE != 5:
                        tensor.wait_ge(sGel, G2 - 1)
                        if PA_STAGE != 6:
                            tensor.wait_ge(sInt, G2 - 1)
                    mm = None
                    for ch in range(NC_CHUNKS):
                        mm = tensor.matmul(
                            ps_g[tc % 2][:, :],
                            wT["g"][c % 2][:, ch * 128:(ch + 1) * 128],
                            xnT[:, ch * S_LEN + tc * 512:
                                ch * S_LEN + tc * 512 + 512],
                            start=(ch == 0), stop=(ch == NC_CHUNKS - 1))
                    mm.then_inc(sMMA, 1)
                    mm = None
                    for ch in range(NC_CHUNKS):
                        mm = tensor.matmul(
                            ps_u[tc % 2][:, :],
                            wT["u"][c % 2][:, ch * 128:(ch + 1) * 128],
                            xnT[:, ch * S_LEN + tc * 512:
                                ch * S_LEN + tc * 512 + 512],
                            start=(ch == 0), stop=(ch == NC_CHUNKS - 1))
                    mm.then_inc(sMMA, 1)

            if PA_STAGE >= 2:
                trans_blk(tensor, 0)
                for c in range(NCH):
                    tensor.wait_ge(sTrE, 128 + 8 * c + 8)
                    if PA_STAGE in (3, 5, 6):
                        mms_blk(tensor, c, range(0, 4))
                    if c + 1 < NCH:
                        trans_blk(tensor, c + 1)
                    if PA_STAGE in (3, 5, 6):
                        mms_blk(tensor, c, range(4, 8))


def _attn(nc, outer_es, env):
    qt_d, kt_d, v_d, s_d = env["qt_d"], env["kt_d"], env["v_d"], env["s_d"]
    wout_d, mask_d, dbg_d = env["wout_d"], env["mask_d"], env["dbg_d"]
    dbg2_d = env["dbg2_d"]
    ident, qn2, k2m_b = env["ident"], env["qn2"], env["k2m_b"]

    NG = NT // 4
    NU = NG * HEADS
    qk_per = [4 * (u // 16 + 1) for u in range(NU)]
    qk_base = [0]
    for u in range(NU):
        qk_base.append(qk_base[-1] + qk_per[u])
    pt_per = qk_per
    pt_base = qk_base

    with ExitStack() as es:
        def sb(name, shape, dt):
            return es.enter_context(nc.sbuf_tensor(name, shape, dt))

        def sem(name):
            return es.enter_context(nc.semaphore(name))

        kT = sb("at_kT", [128, S_LEN], BF16)
        v_sb = sb("at_v", [128, S_LEN], BF16)
        wout = sb("at_wout", [128, NC_CHUNKS * D], BF16)
        qT = [sb(f"at_qT{i}", [128, D], BF16) for i in range(4)]
        p_row = [sb(f"at_p{i}", [128, S_LEN], BF16) for i in range(4)]
        pT_sb = [sb(f"at_pT{p}", [128, 512], BF16) for p in range(2)]
        oT_sb = [sb(f"at_oT{p}", [128, HEADS * 512], BF16) for p in range(2)]
        diag = [[sb(f"at_dg{p}_{i}", [128, 128], BF16) for i in range(4)]
                for p in range(2)]
        masks = sb("at_mask", [128, 4 * 512], BF16)
        msneg = sb("at_msneg", [128, NT * HEADS], F32)
        denp = sb("at_denp", [128, 32], F32)
        den4 = sb("at_den4", [128, 4], F32)
        r4 = sb("at_r4", [128, 4], F32)
        s_t = [sb(f"at_s{p}", [128, D], F32) for p in range(2)]
        dbg2 = sb("at_dbg2", [128, 512], F32) if DEBUG else None
        wout_f = [s_t[0], s_t[1]]

        ps_qk = [es.enter_context(nc.psum_tensor(f"at_psqk{p}", [128, 512], F32))
                 for p in range(2)]
        ps_pt = [es.enter_context(nc.psum_tensor(f"at_pspt{p}", [128, 512], F32))
                 for p in range(2)]
        ps_oT = [es.enter_context(nc.psum_tensor(f"at_psot{p}", [128, 512], F32))
                 for p in range(2)]
        ps_s = es.enter_context(nc.psum_tensor("at_pss", [128, 512], F32))

        sLd3, sWcv3 = sem("at_ld"), sem("at_wcv")
        sQT2 = sem("at_qt")
        sQK, sEXP = sem("at_qk"), sem("at_exp")
        sMSK, sDIAG = sem("at_msk"), sem("at_diag")
        sPT, sPTE = sem("at_pt"), sem("at_pte")
        sPV, sOTE = sem("at_pv"), sem("at_ote")
        sSP, sSST = sem("at_sp"), sem("at_sst")
        wtL = [sem(f"at_wtl{p}") for p in range(2)]
        sStS = [sem(f"at_sts{p}") for p in range(2)]
        sPRE = sem("at_pre")

        block = es.enter_context(nc.Block())

        @block.sync
        def _(sync):
            sync.dma_start(kT[:, :], kt_d).then_inc(sLd3, 16)
            sync.dma_start(v_sb[:, :], v_d).then_inc(sLd3, 16)
            sync.dma_start(masks[:, :].rearrange("q (v j) -> q v j", j=512),
                           mask_d.rearrange("(v q) j -> q v j", q=128)) \
                .then_inc(sLd3, 16)
            for c in range(NC_CHUNKS):
                if c >= 2:
                    sync.wait_ge(sWcv3, c - 1)
                sync.dma_start(wout_f[c % 2][:, :], wout_d[c * 128:(c + 1) * 128, :]) \
                    .then_inc(wtL[c % 2], 16)
            for g in range(NG):
                if g >= 1:
                    sync.wait_ge(sQK, qk_base[16 * g])
                for i in range(4):
                    sync.dma_start(qT[i][:, :], qt_d[4 * g + i, :, :]) \
                        .then_inc(sQT2, 16)

        @block.vector
        def _(vector):
            for c in range(NC_CHUNKS):
                vector.wait_ge(wtL[c % 2], 16 * (c // 2 + 1))
                vector.tensor_copy(wout[:, c * D:(c + 1) * D], wout_f[c % 2][:, :]) \
                    .then_inc(sWcv3, 1)
            vector.tensor_tensor(msneg[:, :], qn2[:, :],
                                 k2m_b[:, :].broadcast_to([128, NT * HEADS]),
                                 ALU.mult).then_inc(sPRE, 1)
            vector.wait_ge(sPRE, 2)
            vector.tensor_scalar(msneg[:, :], msneg[:, :], -1.0, -1.0,
                                 ALU.mult, ALU.add).then_inc(sPRE, 1)
            for u in range(NU):
                g, h = u // 16, u % 16
                J = g + 1
                for i in range(4):
                    qk_idx = qk_base[u] + i * J + (J - 1)
                    vector.wait_ge(sEXP, qk_idx + 1)
                    vector.scalar_tensor_tensor(
                        p_row[i][:, g * 512:(g + 1) * 512],
                        p_row[i][:, g * 512:(g + 1) * 512], 1.0,
                        masks[:, i * 512:(i + 1) * 512],
                        ALU.mult, ALU.mult,
                        accum_out=denp[:, i * 8 + (J - 1):i * 8 + J]) \
                        .then_inc(sMSK, 1)
                vector.wait_ge(sEXP, qk_base[u + 1])
                vector.wait_ge(sMSK, 4 * (u + 1))
                vector.tensor_reduce(
                    den4[:, :].unsqueeze(2),
                    denp[:, :].rearrange("q (i j) -> q i j", j=8)[:, :, 0:J],
                    AX.X, ALU.add).then_inc(sDIAG, 1)
                vector.wait_ge(sDIAG, 3 * u + 1)
                if DEBUG:
                    vector.tensor_copy(dbg2[:, 4 * u:4 * u + 4], den4[:, :])
                vector.reciprocal(r4[:, :], den4[:, :]).then_inc(sDIAG, 1)
                vector.wait_ge(sDIAG, 3 * u + 2)
                for i in range(4):
                    ins = vector.tensor_scalar(diag[u % 2][i][:, :], ident[:, :],
                                               r4[:, i:i + 1], None, ALU.mult)
                ins.then_inc(sDIAG, 1)
                for jj in range(4 * J):
                    pt_idx = pt_base[u] + jj
                    vector.wait_ge(sPT, pt_idx + 1)
                    vector.tensor_copy(pT_sb[jj % 2][:, :], ps_pt[jj % 2][:, :]) \
                        .then_inc(sPTE, 1)
                vector.wait_ge(sPV, u + 1)
                if g >= 2:
                    vector.wait_ge(sSP, 16 * (g - 1))
                vector.tensor_copy(oT_sb[g % 2][:, h * 512:(h + 1) * 512],
                                   ps_oT[u % 2][:, :]).then_inc(sOTE, 1)

        @block.scalar
        def _(scalar):
            scalar.wait_ge(sPRE, 1)
            scalar.activation(msneg[:, :], msneg[:, :], AF.Sqrt).then_inc(sPRE, 1)
            if DEBUG:
                scalar.wait_ge(sPRE, 3)
                scalar.dma_start(dbg_d[:, 0:NT * HEADS], msneg[:, :]).then_inc(sPRE, 16)
                scalar.dma_start(dbg_d[:, NT * HEADS:2 * NT * HEADS], qn2[:, :]) \
                    .then_inc(sPRE, 16)
                scalar.dma_start(dbg_d[:, 2 * NT * HEADS:], k2m_b[:, :],
                                 allow_slow_non_contiguous=True).then_inc(sPRE, 16)
                scalar.wait_ge(sPRE, 51)
            for u in range(NU):
                g, h = u // 16, u % 16
                J = g + 1
                if u >= 1:
                    scalar.wait_ge(sPT, pt_base[u])
                    scalar.wait_ge(sDIAG, 3 * u)
                for i in range(4):
                    for jt in range(J):
                        qk_idx = qk_base[u] + i * J + jt
                        scalar.wait_ge(sQK, qk_idx + 1)
                        col = (4 * g + i) * HEADS + h
                        if jt < J - 1:
                            scalar.activation(
                                p_row[i][:, jt * 512:(jt + 1) * 512],
                                ps_qk[qk_idx % 2][:, :], AF.Exp,
                                bias=msneg[:, col:col + 1],
                                accum_out=denp[:, i * 8 + jt:i * 8 + jt + 1]) \
                                .then_inc(sEXP, 1)
                        else:
                            scalar.activation(
                                p_row[i][:, jt * 512:(jt + 1) * 512],
                                ps_qk[qk_idx % 2][:, :], AF.Exp,
                                bias=msneg[:, col:col + 1]).then_inc(sEXP, 1)
                if h == 15:
                    for i in range(4):
                        it = 4 * g + i
                        if it >= 2:
                            scalar.wait_ge(sStS[it % 2], 16 * (it // 2))
                        for dq in range(4):
                            sp_idx = 16 * g + 4 * i + dq
                            scalar.wait_ge(sSP, sp_idx + 1)
                            scalar.activation(s_t[i % 2][:, dq * 512:(dq + 1) * 512],
                                              ps_s[:, :], AF.Copy).then_inc(sSST, 1)
                        scalar.dma_start(s_d[it, :, :], s_t[i % 2][:, :]) \
                            .then_inc(sStS[it % 2], 16)
                    if g == NG - 1:
                        for pp in range(2):
                            scalar.wait_ge(sStS[pp], 16 * (NT // 2))
            if DEBUG and u == NU - 1:
                scalar.wait_ge(sDIAG, 3 * NU)
                scalar.dma_start(dbg2_d, dbg2[:, :]).then_inc(sPRE, 16)
                scalar.wait_ge(sPRE, 67)


        @block.tensor
        def _(tensor):
            tensor.wait_ge(sLd3, 48)
            for u in range(NU):
                g, h = u // 16, u % 16
                J = g + 1
                if u == 0:
                    tensor.wait_ge(sQT2, 64)
                elif h == 0:
                    tensor.wait_ge(sQT2, 64 * (g + 1))
                for i in range(4):
                    lhs = qT[i][:, h * 128:(h + 1) * 128]
                    for jt in range(J):
                        qk_idx = qk_base[u] + i * J + jt
                        if qk_idx >= 2:
                            tensor.wait_ge(sEXP, qk_idx - 1)
                        tensor.matmul(ps_qk[qk_idx % 2][:, :], lhs,
                                      kT[:, jt * 512:(jt + 1) * 512],
                                      start=True, stop=True).then_inc(sQK, 1)
                tensor.wait_ge(sDIAG, 3 * u + 3)
                for jj in range(4 * J):
                    pt_idx = pt_base[u] + jj
                    if pt_idx >= 2:
                        tensor.wait_ge(sPTE, pt_idx - 1)
                    mm = None
                    for i in range(4):
                        mm = tensor.matmul(ps_pt[jj % 2][:, i * 128:(i + 1) * 128],
                                           p_row[i][:, jj * 128:(jj + 1) * 128],
                                           diag[u % 2][i][:, :],
                                           start=True, stop=True)
                    mm.then_inc(sPT, 1)
                    tensor.wait_ge(sPTE, pt_idx + 1)
                    if jj == 0 and u >= 2:
                        tensor.wait_ge(sOTE, u - 1)
                    pv = tensor.matmul(ps_oT[u % 2][:, :],
                                       v_sb[:, jj * 128:(jj + 1) * 128],
                                       pT_sb[jj % 2][:, :],
                                       start=(jj == 0), stop=(jj == 4 * J - 1))
                    if jj == 4 * J - 1:
                        pv.then_inc(sPV, 1)
                if h == 15:
                    tensor.wait_ge(sOTE, 16 * (g + 1))
                    for i in range(4):
                        for dq in range(4):
                            sp_idx = 16 * g + 4 * i + dq
                            if sp_idx >= 1:
                                tensor.wait_ge(sSST, sp_idx)
                            mm = None
                            for e in range(NC_CHUNKS):
                                mm = tensor.matmul(
                                    ps_s[:, :],
                                    oT_sb[g % 2][:, e * 512 + i * 128:
                                                  e * 512 + i * 128 + 128],
                                    wout[:, e * D + dq * 512:e * D + dq * 512 + 512],
                                    start=(e == 0), stop=(e == NC_CHUNKS - 1))
                            mm.then_inc(sSP, 1)


def _sst_before(sp_idx):
    k, q = divmod(sp_idx, 4)
    return 20 * k + q


def _wd(nc, outer_es, env):
    dwp_d, dsc_d, wdt_d = env["dwp_d"], env["dsc_d"], env["wdt_d"]
    ident = env["ident"]

    with ExitStack() as es:
        def sb(name, shape, dt):
            return es.enter_context(nc.sbuf_tensor(name, shape, dt))

        def sem(name):
            return es.enter_context(nc.semaphore(name))

        pk = [sb(f"wd_pk{p}", [128, 4 * 512], I32) for p in range(2)]
        sc = [sb(f"wd_sc{p}", [128, 4 * 32], F32) for p in range(2)]
        nib_i = sb("wd_nibi", [128, 1024], I32)
        halfn = [sb(f"wd_hn{p}", [128, 1024], BF16) for p in range(2)]
        m4 = sb("wd_m4", [128, 1024], BF16)
        u_ = [sb(f"wd_u{p}", [128, 1024], BF16) for p in range(2)]
        sgn = [sb(f"wd_sgn{p}", [128, 1024], BF16) for p in range(2)]
        b1 = [sb(f"wd_b1{p}", [128, 1024], BF16) for p in range(2)]
        b2 = [sb(f"wd_b2{p}", [128, 1024], BF16) for p in range(2)]
        wdq = [sb(f"wd_wdq{s}", [128, 1024], BF16) for s in range(4)]
        stg = [sb(f"wd_stg{p}", [128, 512], BF16) for p in range(8)]
        cm2 = sb("wd_cm2", [128, 1], F32)
        cm3 = sb("wd_cm3", [128, 1], F32)
        cm375 = sb("wd_cm375", [128, 1], F32)
        ps_tr = [es.enter_context(nc.psum_tensor(f"wd_pstr{p}", [128, 512], F32))
                 for p in range(2)]

        sLd = sem("wd_ld")
        pkwL = [sem(f"wd_pkl{p}") for p in range(2)]
        sStW = [sem(f"wd_stw{k}") for k in range(8)]
        sV1, sP, sS, sV2 = sem("wd_v1"), sem("wd_p"), sem("wd_s"), sem("wd_v2")
        sS1 = sem("wd_s1")
        sTr, sTrE, sSt = sem("wd_tr"), sem("wd_tre"), sem("wd_st")

        block = es.enter_context(nc.Block())
        NIT = 64

        @block.sync
        def _(sync):
            for it in range(NIT):
                o, r = it // 4, it % 4
                if it >= 2:
                    sync.wait_ge(sV1, 4 * (it - 1))
                    sync.wait_ge(sV2, 4 * (it - 1))
                sync.dma_start(
                    pk[it % 2][:, :].rearrange("p (s c) -> p s c", c=512),
                    dwp_d[512 * r:512 * (r + 1), 512 * o:512 * (o + 1)]
                    .rearrange("(s p) c -> p s c", p=128)).then_inc(pkwL[it % 2], 16)
                sync.dma_start(
                    sc[it % 2][:, :].rearrange("p (s c) -> p s c", c=32),
                    dsc_d[512 * r:512 * (r + 1), 32 * o:32 * (o + 1)]
                    .rearrange("(s p) c -> p s c", p=128)).then_inc(pkwL[it % 2], 16)

        @block.vector
        def _(vector):
            vector.memset(cm2[:, :], -2.0)
            vector.memset(cm3[:, :], -3.0)
            vector.memset(cm375[:, :], -3.75)
            for it in range(NIT):
                vector.wait_ge(pkwL[it % 2], 32 * (it // 2 + 1))
                for s in range(4):
                    u = 4 * it + s
                    p = u % 2
                    if u >= 2:
                        vector.wait_ge(sP, u - 1)
                    lo = nib_i[:, :].rearrange("q (k two) -> q k two", two=2)[:, :, 0:1]
                    hi = nib_i[:, :].rearrange("q (k two) -> q k two", two=2)[:, :, 1:2]
                    srcp = pk[it % 2][:, s * 512:(s + 1) * 512].unsqueeze(2)
                    vector.tensor_scalar(lo, srcp, 15, None, ALU.bitwise_and)
                    vector.tensor_scalar(hi, srcp, 4, 15, ALU.logical_shift_right,
                                         ALU.bitwise_and)
                    vector.tensor_scalar(halfn[p][:, :], nib_i[:, :], 0.5, None,
                                         ALU.mult).then_inc(sV1, 1)
                    if s >= 1:
                        _wd_v2(vector, u - 1, sV2, sc, wdq,
                               u_, sgn, b1, b2, sS, sTr, ALU)
                _wd_v2(vector, 4 * it + 3, sV2, sc, wdq, u_, sgn, b1, b2, sS,
                       sTr, ALU)
                for c8 in range(8):
                    GG = 8 * it + c8
                    vector.wait_ge(sTr, GG + 1)
                    if it >= 1:
                        vector.wait_ge(sStW[c8], 16 * it)
                    vector.tensor_copy(stg[c8][:, :], ps_tr[c8 % 2][:, :]) \
                        .then_inc(sTrE, 1)

        @block.gpsimd
        def _(gpsimd):
            for u in range(4 * NIT):
                p = u % 2
                gpsimd.wait_ge(sS1, u + 1)
                if u >= 2:
                    gpsimd.wait_ge(sV2, u - 1)
                gpsimd.tensor_scalar(m4[:, :], sgn[p][:, :], -2.0, -2.0,
                                     ALU.mult, ALU.add)
                gpsimd.tensor_add(u_[p][:, :], halfn[p][:, :], m4[:, :]) \
                    .then_inc(sP, 1)

        @block.scalar
        def _(scalar):
            def _sgn(scalar, u):
                p = u % 2
                scalar.wait_ge(sV1, u + 1)
                if u >= 2:
                    scalar.wait_ge(sV2, u - 1)
                scalar.activation(sgn[p][:, :], halfn[p][:, :], AF.Sign,
                                  bias=cm375[:, :]).then_inc(sS1, 1)

            def _rel(scalar, u):
                p = u % 2
                scalar.wait_ge(sP, u + 1)
                if u >= 2:
                    scalar.wait_ge(sV2, u - 1)
                scalar.activation(b1[p][:, :], u_[p][:, :], AF.Relu,
                                  bias=cm2[:, :])
                scalar.activation(b2[p][:, :], u_[p][:, :], AF.Relu,
                                  bias=cm3[:, :]).then_inc(sS, 1)

            def relus(scalar, it):
                for s in range(4):
                    u = 4 * it + s
                    _sgn(scalar, u)
                    if s >= 1:
                        _rel(scalar, u - 1)
                _rel(scalar, 4 * it + 3)

            relus(scalar, 0)
            for it in range(NIT):
                if it + 1 < NIT:
                    relus(scalar, it + 1)
                o, r = it // 4, it % 4
                for c8 in range(8):
                    GG = 8 * it + c8
                    scalar.wait_ge(sTrE, GG + 1)
                    scalar.dma_start(wdt_d[8 * o + c8, r, :, :], stg[c8][:, :]) \
                        .then_inc(sStW[c8], 16)
            for k in range(8):
                scalar.wait_ge(sStW[k], 16 * NIT)

        @block.tensor
        def _(tensor):
            for it in range(NIT):
                tensor.wait_ge(sV2, 4 * (it + 1))
                for c8 in range(8):
                    GG = 8 * it + c8
                    if GG >= 2:
                        tensor.wait_ge(sTrE, GG - 1)
                    mm = None
                    for s in range(4):
                        mm = tensor.matmul(
                            ps_tr[c8 % 2][:, s * 128:(s + 1) * 128],
                            wdq[s][:, c8 * 128:(c8 + 1) * 128],
                            ident[:, :], start=True, stop=True)
                    mm.then_inc(sTr, 1)


def _wd_v2(vector, u, sV2, sc, wdq, u_, sgn, b1, b2, sS, sTr, _ALU):
    s = u % 4
    p = u % 2
    vector.wait_ge(sS, u + 1)
    if u >= 4:
        vector.wait_ge(sTr, 8 * (u // 4 - 1) + 8)
    vector.scalar_tensor_tensor(b2[p][:, :], b2[p][:, :], -2.0, b1[p][:, :],
                                _ALU.mult, _ALU.subtract)
    vector.tensor_sub(u_[p][:, :], b2[p][:, :], u_[p][:, :])
    vector.tensor_mul(sgn[p][:, :], u_[p][:, :], sgn[p][:, :])
    scb = sc[(u // 4) % 2][:, s * 32:(s + 1) * 32].unsqueeze(2).broadcast_to(
        [128, 32, 32])
    vector.tensor_tensor(wdq[s][:, :].rearrange("q (k c2) -> q k c2", c2=32),
                         sgn[p][:, :].rearrange("q (k c2) -> q k c2", c2=32),
                         scb, _ALU.mult).then_inc(sV2, 1)



def _passB(nc, outer_es, env):
    x_d, s_d, out_d = env["x_d"], env["s_d"], env["out_d"]
    int_d, wdt_d = env["int_d"], env["wdt_d"]
    dbg3_d = env["dbg3_d"]

    with ExitStack() as es:
        def sb(name, shape, dt):
            return es.enter_context(nc.sbuf_tensor(name, shape, dt))

        def sem(name):
            return es.enter_context(nc.semaphore(name))

        intT = sb("pb_int", [128, 128 * 512], BF16)
        wdt = [sb(f"pb_wdt{p}", [128, 512], BF16) for p in range(4)]
        g_sb = [sb(f"pb_g{s}", [128, D], F32) for s in range(4)]
        x_t = [sb(f"pb_x{p}", [128, D], F32) for p in range(2)]
        s_t = [sb(f"pb_s{p}", [128, D], F32) for p in range(2)]
        ps_d = [es.enter_context(nc.psum_tensor(f"pb_ps{k}", [128, 512], F32))
                for k in range(8)]

        sLdI, sXS = sem("pb_ldi"), sem("pb_xs")
        wdtL = [sem(f"pb_wdtl{k}") for k in range(4)]
        sMMc = sem("pb_mmc")
        sSIL, sCMB, sOUT = sem("pb_sil"), sem("pb_cmb"), sem("pb_out")
        xsL = [sem(f"pb_xsl{p}") for p in range(2)]
        sOutS = [sem(f"pb_outs{p}") for p in range(2)]
        sDBG = sem("pb_dbg")

        block = es.enter_context(nc.Block())

        @block.sync
        def _(sync):
            for c in range(128):
                sync.dma_start(intT[:, c * 512:(c + 1) * 512], int_d[0, c, :, :]) \
                    .then_inc(sLdI, 16)
            for ts in range(NSUP):
                for hq in range(4):
                    Q = 4 * ts + hq
                    for c in range(128):
                        if Q * 128 + c >= 4:
                            sync.wait_ge(sMMc, Q * 128 + c - 3)
                        sync.dma_start(wdt[c % 4][:, :], wdt_d[c, hq, :, :]) \
                            .then_inc(wdtL[c % 4], 16)
                        if hq == 3 and ts + 1 < NSUP:
                            sync.wait_ge(sMMc, 128 * Q + c + 1)
                            sync.dma_start(intT[:, c * 512:(c + 1) * 512],
                                           int_d[ts + 1, c, :, :]).then_inc(sLdI, 16)

        @block.gpsimd
        def _(gpsimd):
            for ts in range(NSUP):
                for sub in range(4):
                    SB = 4 * ts + sub
                    if SB >= 2:
                        gpsimd.wait_ge(sCMB, SB - 1)
                        gpsimd.wait_ge(sOutS[SB % 2], 16 * (SB // 2))
                    t0 = (4 * ts + sub) * 128
                    gpsimd.dma_start(x_t[sub % 2][:, :], x_d[t0:t0 + 128, :]) \
                        .then_inc(xsL[sub % 2], 16)
                    gpsimd.dma_start(s_t[sub % 2][:, :], s_d[4 * ts + sub, :, :]) \
                        .then_inc(xsL[sub % 2], 16)

        @block.tensor
        def _(tensor):
            for ts in range(NSUP):
                for hq in range(4):
                    Q = 4 * ts + hq
                    for c in range(128):
                        n_chunk = 128 * Q + c
                        tensor.wait_ge(wdtL[c % 4], 16 * (n_chunk // 4 + 1))
                        tensor.wait_ge(sLdI, 16 * 128 * (ts + 1))
                        mm = None
                        for sub in range(4):
                            if c == 0 and Q >= 2:
                                tensor.wait_ge(sSIL, 4 * (Q - 1))
                            mm = tensor.matmul(
                                ps_d[4 * (hq % 2) + sub][:, :],
                                intT[:, c * 512 + sub * 128:c * 512 + sub * 128 + 128],
                                wdt[c % 4][:, :],
                                start=(c == 0), stop=(c == 127))
                        mm.then_inc(sMMc, 1)
            tensor.matmul(ps_d[0][0:1, 0:1], intT[:, 0:1], wdt[0][:, 0:1],
                          start=True, stop=True).then_inc(sMMc, 1)

        @block.scalar
        def _(scalar):
            for ts in range(NSUP):
                for hq in range(4):
                    Q = 4 * ts + hq
                    for sub in range(4):
                        scalar.wait_ge(sMMc, min(128 * (Q + 1) + 64,
                                                 128 * 4 * NSUP + 1))
                        if ts >= 1 and hq == 0:
                            scalar.wait_ge(sCMB, 4 * (ts - 1) + sub + 1)
                        scalar.activation(g_sb[sub][:, hq * 512:(hq + 1) * 512],
                                          ps_d[4 * (hq % 2) + sub][:, :],
                                          AF.Silu).then_inc(sSIL, 1)
                if DEBUG and ts == 0:
                    for sub in range(4):
                        scalar.wait_ge(sSIL, 12 + sub + 1)
                        scalar.dma_start(dbg3_d[sub, :, :], g_sb[sub][:, :]) \
                            .then_inc(sDBG, 16)
                    scalar.wait_ge(sDBG, 64)
                for sub in range(4):
                    SB = 4 * ts + sub
                    scalar.wait_ge(sCMB, SB + 1)
                    t0 = (4 * ts + sub) * 128
                    scalar.dma_start(out_d[t0:t0 + 128, :], s_t[sub % 2][:, :]) \
                        .then_inc(sOutS[SB % 2], 16)
            for pp in range(2):
                scalar.wait_ge(sOutS[pp], 16 * 2 * NSUP)

        @block.vector
        def _(vector):
            for ts in range(NSUP):
                for sub in range(4):
                    SB = 4 * ts + sub
                    vector.wait_ge(sSIL, 4 * (4 * ts + 3) + sub + 1)
                    vector.wait_ge(xsL[sub % 2], 32 * (SB // 2 + 1))
                    if SB >= 2:
                        vector.wait_ge(sOutS[SB % 2], 16 * (SB // 2))
                    vector.tensor_mul(s_t[sub % 2][:, :], s_t[sub % 2][:, :],
                                      g_sb[sub][:, :])
                    vector.tensor_add(s_t[sub % 2][:, :], s_t[sub % 2][:, :],
                                      x_t[sub % 2][:, :]).then_inc(sCMB, 1)



_CONSTS = None


def kernel(L0_attn_x, L0_attn_Wq, L0_attn_Wkv, L0_attn_Wout,
           L0_ffn_gate_W_packed, L0_ffn_gate_scales,
           L0_ffn_up_W_packed, L0_ffn_up_scales,
           L0_ffn_down_W_packed, L0_ffn_down_scales):
    global _CONSTS
    if _CONSTS is None:
        _CONSTS = _host_consts()
    consts = _CONSTS

    x = np.ascontiguousarray(np.asarray(L0_attn_x, dtype=np.float32))
    shared = dict(
        wq=np.ascontiguousarray(np.asarray(L0_attn_Wq, dtype=np.float32)),
        wkv=np.ascontiguousarray(np.asarray(L0_attn_Wkv, dtype=np.float32)),
        wout=np.ascontiguousarray(np.asarray(L0_attn_Wout, dtype=np.float32)),
        gwp=np.ascontiguousarray(np.asarray(L0_ffn_gate_W_packed, dtype=np.int32)),
        gsc=np.ascontiguousarray(np.asarray(L0_ffn_gate_scales, dtype=np.float32)),
        uwp=np.ascontiguousarray(np.asarray(L0_ffn_up_W_packed, dtype=np.int32)),
        usc=np.ascontiguousarray(np.asarray(L0_ffn_up_scales, dtype=np.float32)),
        dwp=np.ascontiguousarray(np.asarray(L0_ffn_down_W_packed, dtype=np.int32)),
        dsc=np.ascontiguousarray(np.asarray(L0_ffn_down_scales, dtype=np.float32)),
        **consts)

    nc = build_nc()
    in_maps = [dict(x=np.ascontiguousarray(x[b]), **shared) for b in range(8)]
    res = run_bass_kernel_spmd(nc, in_maps, core_ids=list(range(8)))
    out = np.stack([res.results[b]["out"] for b in range(8)], axis=0)
    return out


# version 04 (final)
# speedup vs baseline: 1.3581x; 1.2441x over previous
import numpy as np
import ml_dtypes
from contextlib import ExitStack

import concourse.bass as bass
import concourse.mybir as mybir
from concourse.bass_utils import run_bass_kernel_spmd

F32 = mybir.dt.float32
BF16 = mybir.dt.bfloat16
I32 = mybir.dt.int32
AF = mybir.ActivationFunctionType
ALU = mybir.AluOpType
AX = mybir.AxisListType

S_LEN = 4096
D = 2048
HEADS = 16
DH = 128
NMLP = 16384
NT = S_LEN // 128
NC_CHUNKS = D // 128
SCALE = DH ** -0.5
EPS = 1e-5
TSUP = 512
NSUP = S_LEN // TSUP

DEBUG = False
PA_STAGE = 3
PA_NCH = None
PHASES = "1AWTB"


def _host_consts():
    ident = np.eye(128, dtype=ml_dtypes.bfloat16)
    inv = 1.0 / 10000.0 ** (np.arange(0, DH, 2, dtype=np.float64) / DH)
    tpos = np.arange(S_LEN, dtype=np.float64)[:, None] * inv[None, :]
    cos = np.cos(tpos)
    sin = np.sin(tpos)
    cosf = np.concatenate([cos, cos], 1)
    sinf = np.concatenate([sin, sin], 1)
    rope = np.concatenate(
        [cosf * SCALE, sinf * SCALE, cosf, sinf], axis=1).astype(np.float32)
    jj = np.arange(512)[None, :]
    ii = np.arange(128)[:, None]
    mask = np.zeros((4, 128, 512), dtype=ml_dtypes.bfloat16)
    for v in range(4):
        mask[v] = (jj <= (128 * v + ii)).astype(ml_dtypes.bfloat16)
    ones = np.ones((1, 128), dtype=ml_dtypes.bfloat16)
    return dict(c_ident=ident, c_rope=rope, c_mask=mask.reshape(512, 512),
                c_ones=ones)


def build_nc():
    nc = bass.Bass("TRN2", target_bir_lowering=False)

    x_d = nc.dram_tensor("x", [S_LEN, D], F32, kind="ExternalInput").ap()
    wq_d = nc.dram_tensor("wq", [D, D], F32, kind="ExternalInput").ap()
    wkv_d = nc.dram_tensor("wkv", [D, 256], F32, kind="ExternalInput").ap()
    wout_d = nc.dram_tensor("wout", [D, D], F32, kind="ExternalInput").ap()
    gwp_d = nc.dram_tensor("gwp", [NMLP, D // 2], I32, kind="ExternalInput").ap()
    gsc_d = nc.dram_tensor("gsc", [NMLP, 64], F32, kind="ExternalInput").ap()
    uwp_d = nc.dram_tensor("uwp", [NMLP, D // 2], I32, kind="ExternalInput").ap()
    usc_d = nc.dram_tensor("usc", [NMLP, 64], F32, kind="ExternalInput").ap()
    dwp_d = nc.dram_tensor("dwp", [D, NMLP // 2], I32, kind="ExternalInput").ap()
    dsc_d = nc.dram_tensor("dsc", [D, NMLP // 32], F32, kind="ExternalInput").ap()
    ident_d = nc.dram_tensor("c_ident", [128, 128], BF16, kind="ExternalInput").ap()
    rope_d = nc.dram_tensor("c_rope", [S_LEN, 512], F32, kind="ExternalInput").ap()
    mask_d = nc.dram_tensor("c_mask", [512, 512], BF16, kind="ExternalInput").ap()
    ones_d = nc.dram_tensor("c_ones", [1, 128], BF16, kind="ExternalInput").ap()

    out_d = nc.dram_tensor("out", [S_LEN, D], F32, kind="ExternalOutput").ap()

    skind = dict(kind="ExternalOutput") if DEBUG else {}
    xn_d = nc.dram_tensor("xn_s", [S_LEN, D], BF16, **skind).ap()
    qt_d = nc.dram_tensor("qt_s", [NT, 128, D], BF16, **skind).ap()
    kt_d = nc.dram_tensor("kt_s", [128, S_LEN], BF16, **skind).ap()
    v_d = nc.dram_tensor("v_s", [128, S_LEN], BF16, **skind).ap()
    int_d = nc.dram_tensor("int_s", [NSUP, 128, 128, TSUP], BF16, **skind).ap()
    wdt_d = nc.dram_tensor("wdt_s", [128, 4, 128, 512], BF16, **skind).ap()
    s_d = nc.dram_tensor("s_s", [NT, 128, D], F32, **skind).ap()
    dbg_d = nc.dram_tensor("dbg_s", [128, 2 * NT * HEADS + 1], F32, **skind).ap()
    dbg2_d = nc.dram_tensor("dbg2_s", [128, 512], F32, **skind).ap()
    dbg3_d = nc.dram_tensor("dbg3_s", [4, 128, D], F32, **skind).ap()

    with ExitStack() as es:
        def sb(name, shape, dt):
            return es.enter_context(nc.sbuf_tensor(name, shape, dt))

        ident = sb("ident", [128, 128], BF16)
        ones = sb("ones", [1, 128], BF16)
        kn2 = sb("kn2", [128, NT], F32)
        qn2 = sb("qn2", [128, NT * HEADS], F32)
        k2m_b = sb("k2m_b", [128, 1], F32)
        czero = sb("czero", [128, 1], F32)
        ceps = sb("ceps", [128, 1], F32)

        if "1" in PHASES:
            _stage1(nc, es, locals())
        if "A" in PHASES:
            _passA(nc, es, locals())
        if "T" in PHASES:
            _attn(nc, es, locals())
        if "W" in PHASES:
            _wd(nc, es, locals())
        if "B" in PHASES:
            _passB(nc, es, locals())

    return nc




def _stage1(nc, outer_es, env):
    x_d, wq_d, wkv_d = env["x_d"], env["wq_d"], env["wkv_d"]
    ident_d, rope_d, ones_d = env["ident_d"], env["rope_d"], env["ones_d"]
    xn_d, qt_d, kt_d, v_d = env["xn_d"], env["qt_d"], env["kt_d"], env["v_d"]
    ident, ones, kn2, qn2, k2m_b = (env["ident"], env["ones"], env["kn2"],
                                    env["qn2"], env["k2m_b"])
    czero, ceps = env["czero"], env["ceps"]

    with ExitStack() as es:
        def sb(name, shape, dt):
            return es.enter_context(nc.sbuf_tensor(name, shape, dt))

        def sem(name):
            return es.enter_context(nc.semaphore(name))

        wq_f = [sb(f"wq_f{p}", [128, D], F32) for p in range(2)]
        wq_bf = sb("wq_bf", [128, NC_CHUNKS * D], BF16)
        wkv_f = sb("wkv_f", [128, NC_CHUNKS * 256], F32)
        wkv_bf = sb("wkv_bf", [128, NC_CHUNKS * 256], BF16)
        x_t = [sb(f"x_t{p}", [128, D], F32) for p in range(2)]
        rope_t = [sb(f"rope_t{p}", [128, 512], F32) for p in range(2)]
        xn_t = [sb(f"xn_t{p}", [128, D], BF16) for p in range(2)]
        xnT_t = [sb(f"xnT_t{p}", [128, D], BF16) for p in range(2)]
        q_f = sb("q_f", [128, D], F32)
        q_ro = sb("q_ro", [128, D], BF16)
        qT_t = [sb(f"qT_t{p}", [128, D], BF16) for p in range(2)]
        sq_s = sb("sq_s", [128, D], F32)
        k_f = sb("k_f", [128, DH], F32)
        k_ro = sb("k_ro", [128, DH], BF16)
        kT_sb = sb("kT_sb", [128, S_LEN], BF16)
        v_sb = sb("v_sb", [128, S_LEN], BF16)
        stats = [sb(f"stats{p}", [128, 8], F32) for p in range(2)]
        ksc = sb("ksc", [128, 2], F32)
        k2bf = sb("k2bf", [128, 1], BF16)

        ps_q = es.enter_context(nc.psum_tensor("ps_q", [128, D], F32))
        ps_kv = es.enter_context(nc.psum_tensor("ps_kv", [128, 256], F32))
        ps_tr = [es.enter_context(nc.psum_tensor(f"ps_tr{p}", [128, 512], F32))
                 for p in range(2)]
        ps_b = es.enter_context(nc.psum_tensor("ps_b", [128, 128], F32))

        sLd, sLdW, sWcv = sem("s1_ld"), sem("s1_ldw"), sem("s1_wcv")
        xL = [sem(f"s1_xl{p}") for p in range(2)]
        rL = [sem(f"s1_rl{p}") for p in range(2)]
        wqL = [sem(f"s1_wql{p}") for p in range(2)]
        sStX = [sem(f"s1_stx{p}") for p in range(2)]
        sStQ = [sem(f"s1_stq{p}") for p in range(2)]
        sSq, sLN, sXn = sem("s1_sq"), sem("s1_ln"), sem("s1_xn")
        sTr, sTrE, sKT = sem("s1_tr"), sem("s1_tre"), sem("s1_kt")
        sMM, sQE, sRo = sem("s1_mm"), sem("s1_qe"), sem("s1_ro")
        sKV, sSt, sFin = sem("s1_kv"), sem("s1_st"), sem("s1_fin")

        block = es.enter_context(nc.Block())

        @block.sync
        def _(sync):
            sync.dma_start(ident[:, :], ident_d).then_inc(sLd, 16)
            sync.dma_start(ones[:, :], ones_d).then_inc(sLd, 16)
            sync.dma_start(
                wkv_f[:, :].rearrange("p (c n) -> p c n", n=256),
                wkv_d.rearrange("(c p) n -> p c n", p=128)).then_inc(sLdW, 16)
            for c in range(NC_CHUNKS):
                if c >= 2:
                    sync.wait_ge(sWcv, c - 1)
                sync.dma_start(wq_f[c % 2][:, :], wq_d[c * 128:(c + 1) * 128, :]) \
                    .then_inc(wqL[c % 2], 16)
            for t in range(NT):
                if t >= 2:
                    sync.wait_ge(sXn, t - 1)
                    sync.wait_ge(sSq, 2 * (t - 2) + 1)
                    sync.wait_ge(sKV, 4 * (t - 2) + 2)
                sync.dma_start(x_t[t % 2][:, :], x_d[t * 128:(t + 1) * 128, :]) \
                    .then_inc(xL[t % 2], 16)
                sync.dma_start(rope_t[t % 2][:, :], rope_d[t * 128:(t + 1) * 128, :]) \
                    .then_inc(rL[t % 2], 16)

        @block.vector
        def _(vector):
            vector.memset(czero[:, :], 0.0)
            vector.memset(ceps[:, :], EPS)
            vector.wait_ge(sLdW, 16)
            vector.tensor_copy(wkv_bf[:, :], wkv_f[:, :])
            for c in range(NC_CHUNKS):
                vector.wait_ge(wqL[c % 2], 16 * (c // 2 + 1))
                vector.tensor_copy(wq_bf[:, c * D:(c + 1) * D], wq_f[c % 2][:, :]) \
                    .then_inc(sWcv, 1)
            for t in range(NT):
                p = t % 2
                st = stats[p]
                vector.wait_ge(xL[t % 2], 16 * (t // 2 + 1))
                vector.tensor_reduce(st[:, 0:1], x_t[p][:, :], AX.X, ALU.add,
                                     negate=True).then_inc(sLN, 1)
                vector.wait_ge(sLN, 6 * t + 1)
                vector.tensor_scalar(st[:, 2:3], st[:, 0:1], 1.0 / D, None,
                                     ALU.mult).then_inc(sLN, 1)
                vector.wait_ge(sLN, 6 * t + 2)
                vector.tensor_mul(st[:, 3:4], st[:, 2:3], st[:, 2:3]) \
                    .then_inc(sLN, 1)
                vector.wait_ge(sSq, 2 * t + 1)
                vector.wait_ge(sLN, 6 * t + 3)
                vector.scalar_tensor_tensor(st[:, 4:5], st[:, 1:2], 1.0 / D,
                                            st[:, 3:4], ALU.mult, ALU.subtract) \
                    .then_inc(sLN, 1)
                vector.wait_ge(sLN, 6 * t + 5)
                vector.reciprocal(st[:, 6:7], st[:, 5:6]).then_inc(sLN, 1)
                vector.wait_ge(sLN, 6 * t + 6)
                if t >= 2:
                    vector.wait_ge(sStX[t % 2], 16 * (t // 2))
                vector.tensor_scalar(xn_t[p][:, :], x_t[p][:, :], st[:, 2:3],
                                     st[:, 6:7], ALU.add, ALU.mult) \
                    .then_inc(sXn, 1)
                if t >= 2:
                    vector.wait_ge(sMM, 80 * (t - 1))
                for g in range(4):
                    H = 8 * t + g
                    vector.wait_ge(sTr, H + 1)
                    vector.tensor_copy(
                        xnT_t[p][:, :].rearrange("q (c n) -> q c n", n=128)
                        [:, 4 * g:4 * g + 4, :],
                        ps_tr[g % 2][:, :].rearrange("q (c n) -> q c n", n=128)) \
                        .then_inc(sTrE, 1)
                vector.wait_ge(sQE, 8 * t + 4)
                vector.wait_ge(rL[t % 2], 16 * (t // 2 + 1))
                qv = q_f[:, :].rearrange("q (h c) -> q h c", c=DH)
                ov = q_ro[:, :].rearrange("q (h c) -> q h c", c=DH)
                sv = sq_s[:, :].rearrange("q (h c) -> q h c", c=DH)
                rp = rope_t[p]
                clo = rp[:, 0:64].unsqueeze(1).broadcast_to([128, HEADS, 64])
                chi = rp[:, 64:128].unsqueeze(1).broadcast_to([128, HEADS, 64])
                slo = rp[:, 128:192].unsqueeze(1).broadcast_to([128, HEADS, 64])
                shi = rp[:, 192:256].unsqueeze(1).broadcast_to([128, HEADS, 64])
                vector.tensor_tensor(sv[:, :, 0:64], qv[:, :, 0:64], clo, ALU.mult)
                vector.tensor_tensor(sv[:, :, 64:128], qv[:, :, 64:128], slo, ALU.mult)
                vector.tensor_tensor(ov[:, :, 0:64], sv[:, :, 0:64], sv[:, :, 64:128],
                                     ALU.subtract)
                vector.tensor_tensor(sv[:, :, 0:64], qv[:, :, 64:128], chi, ALU.mult)
                vector.tensor_tensor(sv[:, :, 64:128], qv[:, :, 0:64], shi, ALU.mult)
                vector.tensor_add(ov[:, :, 64:128], sv[:, :, 0:64], sv[:, :, 64:128]) \
                    .then_inc(sRo, 4)
                vector.wait_ge(sQE, 8 * t + 8)
                vector.tensor_reduce(
                    qn2[:, t * HEADS:(t + 1) * HEADS].unsqueeze(2),
                    sq_s[:, :].rearrange("q (h c) -> q h c", c=DH),
                    AX.X, ALU.add).then_inc(sRo, 1)
                if t >= 2:
                    vector.wait_ge(sStQ[t % 2], 16 * (t // 2))
                for g in range(4):
                    H = 8 * t + 4 + g
                    vector.wait_ge(sTr, H + 1)
                    vector.tensor_copy(
                        qT_t[p][:, :].rearrange("q (c n) -> q c n", n=128)
                        [:, 4 * g:4 * g + 4, :],
                        ps_tr[g % 2][:, :].rearrange("q (c n) -> q c n", n=128)) \
                        .then_inc(sTrE, 1)
                vector.wait_ge(sKV, 4 * t + 1)
                t1k, t2k = sq_s[:, 0:64], sq_s[:, 64:128]
                vector.tensor_tensor(t1k, k_f[:, 0:64], rp[:, 256:320], ALU.mult) \
                    .then_inc(sRo, 1)
                vector.tensor_tensor(t2k, k_f[:, 64:128], rp[:, 384:448], ALU.mult) \
                    .then_inc(sRo, 1)
                vector.wait_ge(sRo, 10 * t + 7)
                vector.tensor_tensor(k_ro[:, 0:64], t1k, t2k, ALU.subtract) \
                    .then_inc(sRo, 1)
                vector.tensor_tensor(t1k, k_f[:, 64:128], rp[:, 320:384], ALU.mult) \
                    .then_inc(sRo, 1)
                vector.tensor_tensor(t2k, k_f[:, 0:64], rp[:, 448:512], ALU.mult) \
                    .then_inc(sRo, 1)
                vector.wait_ge(sRo, 10 * t + 10)
                vector.tensor_add(k_ro[:, 64:128], t1k, t2k).then_inc(sKV, 1)
                vector.wait_ge(sMM, 80 * (t + 1))
                vector.tensor_copy(v_sb[:, t * 128:(t + 1) * 128], ps_kv[:, 128:256]) \
                    .then_inc(sKV, 1)
                vector.wait_ge(sKT, t + 1)
                vector.tensor_copy(kT_sb[:, t * 128:(t + 1) * 128], ps_b[:, :]) \
                    .then_inc(sKV, 1)
            vector.reduce_max(ksc[:, 0:1], kn2[:, :], AX.X).then_inc(sFin, 1)
            vector.wait_ge(sFin, 1)
            vector.tensor_copy(k2bf[:, :], ksc[:, 0:1]).then_inc(sFin, 1)
            vector.wait_ge(sFin, 3)
            vector.reduce_max(ksc[0:1, 1:2], ps_b[0:1, 0:128], AX.X).then_inc(sFin, 1)
            vector.wait_ge(sFin, 4)
            vector.tensor_copy(k2bf[0:1, 0:1], ksc[0:1, 1:2]).then_inc(sFin, 1)
            vector.wait_ge(sFin, 6)
            vector.tensor_copy(k2m_b[:, :], ps_b[:, 0:1]).then_inc(sFin, 1)

        @block.scalar
        def _(scalar):
            for t in range(NT):
                p = t % 2
                st = stats[p]
                scalar.wait_ge(xL[t % 2], 16 * (t // 2 + 1))
                if t >= 1:
                    scalar.wait_ge(sKV, 4 * (t - 1) + 2)
                scalar.activation(sq_s[:, :], x_t[p][:, :], AF.Square,
                                  accum_out=st[:, 1:2]).then_inc(sSq, 1)
                scalar.wait_ge(sLN, 6 * t + 4)
                scalar.activation(st[:, 5:6], st[:, 4:5], AF.Sqrt,
                                  bias=ceps[:, :]).then_inc(sLN, 1)
                scalar.wait_ge(sMM, 80 * (t + 1))
                for qq in range(4):
                    scalar.activation(q_f[:, qq * 512:(qq + 1) * 512],
                                      ps_q[:, qq * 512:(qq + 1) * 512],
                                      AF.Copy).then_inc(sQE, 1)
                scalar.wait_ge(sRo, 10 * t + 4)
                scalar.activation(sq_s[:, :], q_ro[:, :], AF.Square) \
                    .then_inc(sQE, 4)
                scalar.activation(k_f[:, :], ps_kv[:, 0:128], AF.Copy) \
                    .then_inc(sKV, 1)
                scalar.wait_ge(sKV, 4 * t + 2)
                scalar.activation(sq_s[:, 1920:2048], k_ro[:, :], AF.Square,
                                  accum_out=kn2[:, t:t + 1]).then_inc(sSq, 1)
                scalar.wait_ge(sXn, t + 1)
                scalar.dma_start(xn_d[t * 128:(t + 1) * 128, :], xn_t[p][:, :]) \
                    .then_inc(sStX[t % 2], 16)
                scalar.wait_ge(sTrE, 8 * t + 8)
                scalar.dma_start(qt_d[t, :, :], qT_t[p][:, :]).then_inc(sStQ[t % 2], 16)
            scalar.wait_ge(sKV, 4 * NT)
            scalar.dma_start(kt_d, kT_sb[:, :]).then_inc(sSt, 16)
            scalar.dma_start(v_d, v_sb[:, :]).then_inc(sSt, 16)
            scalar.wait_ge(sSt, 32)
            for p in range(2):
                scalar.wait_ge(sStX[p], 16 * (NT // 2))
                scalar.wait_ge(sStQ[p], 16 * (NT // 2))
            scalar.wait_ge(sFin, 7)

        @block.tensor
        def _(tensor):
            tensor.wait_ge(sLd, 32)
            for t in range(NT):
                p = t % 2
                tensor.wait_ge(sXn, t + 1)
                for g in range(4):
                    H = 8 * t + g
                    if H >= 2:
                        tensor.wait_ge(sTrE, H - 1)
                    mm = None
                    for k in range(4):
                        c = 4 * g + k
                        mm = tensor.matmul(ps_tr[g % 2][:, k * 128:(k + 1) * 128],
                                           xn_t[p][:, c * 128:(c + 1) * 128],
                                           ident[:, :], start=True, stop=True)
                    mm.then_inc(sTr, 1)
                if t == 0:
                    tensor.wait_ge(sWcv, NC_CHUNKS)
                tensor.wait_ge(sTrE, 8 * t + 4)
                if t > 0:
                    tensor.wait_ge(sQE, 8 * (t - 1) + 4)
                    tensor.wait_ge(sKV, 4 * (t - 1) + 3)
                for c in range(NC_CHUNKS):
                    lhs = xnT_t[p][:, c * 128:(c + 1) * 128]
                    for qq in range(4):
                        tensor.matmul(ps_q[:, qq * 512:(qq + 1) * 512], lhs,
                                      wq_bf[:, c * D + qq * 512:c * D + qq * 512 + 512],
                                      start=(c == 0), stop=(c == NC_CHUNKS - 1)) \
                            .then_inc(sMM, 1)
                    tensor.matmul(ps_kv[:, :], lhs, wkv_bf[:, c * 256:(c + 1) * 256],
                                  start=(c == 0), stop=(c == NC_CHUNKS - 1)) \
                        .then_inc(sMM, 1)
                tensor.wait_ge(sRo, 10 * t + 4)
                for g in range(4):
                    H = 8 * t + 4 + g
                    tensor.wait_ge(sTrE, H - 1)
                    mm = None
                    for k in range(4):
                        c = 4 * g + k
                        mm = tensor.matmul(ps_tr[g % 2][:, k * 128:(k + 1) * 128],
                                           q_ro[:, c * 128:(c + 1) * 128],
                                           ident[:, :], start=True, stop=True)
                    mm.then_inc(sTr, 1)
                tensor.wait_ge(sKV, 4 * t + 2)
                tensor.matmul(ps_b[:, :], k_ro[:, :], ident[:, :],
                              start=True, stop=True).then_inc(sKT, 1)
            tensor.wait_ge(sFin, 2)
            tensor.matmul(ps_b[0:1, 0:128], k2bf[:, :], ident[:, :],
                          start=True, stop=True).then_inc(sFin, 1)
            tensor.wait_ge(sFin, 5)
            tensor.matmul(ps_b[:, 0:1], ones[0:1, :], k2bf[0:1, 0:1],
                          start=True, stop=True).then_inc(sFin, 1)




def _passA(nc, outer_es, env):
    xn_d, int_d = env["xn_d"], env["int_d"]
    gwp_d, gsc_d, uwp_d, usc_d = env["gwp_d"], env["gsc_d"], env["uwp_d"], env["usc_d"]
    ident = env["ident"]
    NCH = PA_NCH if PA_NCH is not None else NMLP // 128

    with ExitStack() as es:
        def sb(name, shape, dt):
            return es.enter_context(nc.sbuf_tensor(name, shape, dt))

        def sem(name):
            return es.enter_context(nc.semaphore(name))

        xnT = sb("pa_xnT", [128, NC_CHUNKS * S_LEN], BF16)
        pg = [sb(f"pa_pg{p}", [128, 1024], I32) for p in range(2)]
        pu = [sb(f"pa_pu{p}", [128, 1024], I32) for p in range(2)]
        sg = [sb(f"pa_sg{p}", [128, 64], F32) for p in range(2)]
        su = [sb(f"pa_su{p}", [128, 64], F32) for p in range(2)]
        nib_i = sb("pa_nibi", [128, 1024], I32)
        halfn = [sb(f"pa_hn{p}", [128, 1024], BF16) for p in range(2)]
        m4 = sb("pa_m4", [128, 1024], BF16)
        u_ = [sb(f"pa_u{p}", [128, 1024], BF16) for p in range(2)]
        sgn = [sb(f"pa_sgn{p}", [128, 1024], BF16) for p in range(2)]
        b1 = [sb(f"pa_b1{p}", [128, 1024], BF16) for p in range(2)]
        b2 = [sb(f"pa_b2{p}", [128, 1024], BF16) for p in range(2)]
        wdq = [sb(f"pa_wdq{p}", [128, D], BF16) for p in range(2)]
        wT = {"g": [sb(f"pa_wTg{p}", [128, D], BF16) for p in range(2)],
              "u": [sb(f"pa_wTu{p}", [128, D], BF16) for p in range(2)]}
        xn_l = [wT["g"][0], wT["g"][1]]
        gel = [sb(f"pa_gel{p}", [128, 512], F32) for p in range(2)]
        int_sb = [sb(f"pa_int{p}", [128, 512], BF16) for p in range(2)]
        cm2 = sb("pa_cm2", [128, 1], F32)
        cm3 = sb("pa_cm3", [128, 1], F32)
        cm375 = sb("pa_cm375", [128, 1], F32)

        ps_tr = [es.enter_context(nc.psum_tensor(f"pa_pstr{p}", [128, 512], F32))
                 for p in range(2)]
        ps_g = [es.enter_context(nc.psum_tensor(f"pa_psg{p}", [128, 512], F32))
                for p in range(2)]
        ps_u = [es.enter_context(nc.psum_tensor(f"pa_psu{p}", [128, 512], F32))
                for p in range(2)]

        sLdA, sLdW2 = sem("pa_lda"), sem("pa_ldw")
        xnL = [sem(f"pa_xnl{p}") for p in range(2)]
        pkL = [sem(f"pa_pkl{p}") for p in range(2)]
        sStI = [sem(f"pa_sti{p}") for p in range(2)]
        sTr, sTrE = sem("pa_tr"), sem("pa_tre")
        sDqV1, sDqP, sDqS, sDqV2 = (sem("pa_dqv1"), sem("pa_dqp"),
                                    sem("pa_dqs"), sem("pa_dqv2"))
        sDqS1 = sem("pa_dqs1")
        sMMA, sGel, sInt, sStA = sem("pa_mma"), sem("pa_gel"), sem("pa_int"), sem("pa_sta")

        block = es.enter_context(nc.Block())

        def v1_emit(vector, c, wi, hp):
            u = 4 * c + 2 * wi + hp
            wp = (pg if wi == 0 else pu)[c % 2]
            if u >= 2:
                vector.wait_ge(sDqP, u - 1)
            if hp == 0:
                vector.wait_ge(pkL[c % 2], 64 * (c // 2 + 1))
            lo = nib_i[:, :].rearrange("q (k two) -> q k two", two=2)[:, :, 0:1]
            hi = nib_i[:, :].rearrange("q (k two) -> q k two", two=2)[:, :, 1:2]
            srcp = wp[:, hp * 512:(hp + 1) * 512].unsqueeze(2)
            vector.tensor_scalar(lo, srcp, 15, None, ALU.bitwise_and)
            vector.tensor_scalar(hi, srcp, 4, 15, ALU.logical_shift_right,
                                 ALU.bitwise_and)
            vector.tensor_scalar(halfn[u % 2][:, :], nib_i[:, :], 0.5, None,
                                 ALU.mult).then_inc(sDqV1, 1)

        def v2_emit(vector, c, wi, hp):
            u = 4 * c + 2 * wi + hp
            cw = 2 * c + wi
            sc = (sg if wi == 0 else su)[c % 2]
            p = u % 2
            vector.wait_ge(sDqS, u + 1)
            if cw >= 2 and PA_STAGE >= 2:
                vector.wait_ge(sTr, 128 + 4 * (cw - 2) + 4)
            vector.scalar_tensor_tensor(b2[p][:, :], b2[p][:, :], -2.0, b1[p][:, :],
                                        ALU.mult, ALU.subtract)
            vector.tensor_sub(u_[p][:, :], b2[p][:, :], u_[p][:, :])
            vector.tensor_mul(sgn[p][:, :], u_[p][:, :], sgn[p][:, :])
            scb = sc[:, hp * 32:(hp + 1) * 32].unsqueeze(2).broadcast_to(
                [128, 32, 32])
            vector.tensor_tensor(
                wdq[cw % 2][:, hp * 1024:(hp + 1) * 1024].rearrange(
                    "q (k c2) -> q k c2", c2=32),
                sgn[p][:, :].rearrange("q (k c2) -> q k c2", c2=32),
                scb, ALU.mult).then_inc(sDqV2, 1)

        @block.sync
        def _(sync):
            for t in range(NT):
                if t >= 2:
                    sync.wait_ge(sTr, 4 * (t - 1))
                sync.dma_start(xn_l[t % 2][:, :], xn_d[t * 128:(t + 1) * 128, :]) \
                    .then_inc(xnL[t % 2], 16)
            for c in range(NCH if PA_STAGE >= 1 else 0):
                if c >= 2:
                    sync.wait_ge(sDqV1, 4 * (c - 1))
                    sync.wait_ge(sDqV2, 4 * (c - 1))
                sync.dma_start(pg[c % 2][:, :], gwp_d[c * 128:(c + 1) * 128, :]) \
                    .then_inc(pkL[c % 2], 16)
                sync.dma_start(sg[c % 2][:, :], gsc_d[c * 128:(c + 1) * 128, :]) \
                    .then_inc(pkL[c % 2], 16)
                sync.dma_start(pu[c % 2][:, :], uwp_d[c * 128:(c + 1) * 128, :]) \
                    .then_inc(pkL[c % 2], 16)
                sync.dma_start(su[c % 2][:, :], usc_d[c * 128:(c + 1) * 128, :]) \
                    .then_inc(pkL[c % 2], 16)

        @block.vector
        def _(vector):
            vector.memset(cm2[:, :], -2.0)
            vector.memset(cm3[:, :], -3.0)
            vector.memset(cm375[:, :], -3.75)
            for t in range(NT):
                for g in range(4):
                    GG = 4 * t + g
                    vector.wait_ge(sTr, GG + 1)
                    vector.tensor_copy(
                        xnT[:, :].rearrange("q (ch n) -> q ch n", n=S_LEN)
                        [:, 4 * g:4 * g + 4, t * 128:(t + 1) * 128],
                        ps_tr[GG % 2][:, :].rearrange("q (k n) -> q k n", n=128)) \
                        .then_inc(sTrE, 1)
            units = [(c, wi, hp) for c in range(NCH) for wi in range(2)
                     for hp in range(2)]
            def units_blk(vector, cc):
                for k in range(4):
                    uu = 4 * cc + k
                    v1_emit(vector, *units[uu])
                    if k >= 2:
                        v2_emit(vector, *units[uu - 2])
                v2_emit(vector, *units[4 * cc + 2])
                v2_emit(vector, *units[4 * cc + 3])

            def evacs_blk(vector, cc):
                for wi in range(2):
                    cw = 2 * cc + wi
                    for g in range(4):
                        GG = 128 + 4 * cw + g
                        vector.wait_ge(sTr, GG + 1)
                        if cc >= 2 and PA_STAGE in (3, 5, 6):
                            vector.wait_ge(sMMA, 16 * (cc - 1))
                        vector.tensor_copy(
                            wT[("g", "u")[wi]][cc % 2][:, :].rearrange(
                                "q (k n) -> q k n", n=128)[:, 4 * g:4 * g + 4, :],
                            ps_tr[GG % 2][:, :].rearrange(
                                "q (k n) -> q k n", n=128)).then_inc(sTrE, 1)

            def inter_blk(vector, cm, tcs):
                for tc in tcs:
                    G2 = 8 * cm + tc
                    vector.wait_ge(sGel, G2 + 1)
                    vector.wait_ge(sMMA, 16 * cm + 2 * tc + 2)
                    if G2 >= 2:
                        vector.wait_ge(sStI[tc % 2], 16 * (G2 // 2))
                    vector.tensor_tensor(int_sb[tc % 2][:, :], gel[tc % 2][:, :],
                                         ps_u[tc % 2][:, :], ALU.mult) \
                        .then_inc(sInt, 1)

            if PA_STAGE >= 1:
                units_blk(vector, 0)
                if PA_STAGE >= 2:
                    evacs_blk(vector, 0)
                for c in range(NCH):
                    if c + 1 < NCH:
                        units_blk(vector, c + 1)
                    if PA_STAGE == 3:
                        inter_blk(vector, c, range(0, 4))
                    if PA_STAGE >= 2 and c + 1 < NCH:
                        evacs_blk(vector, c + 1)
                    if PA_STAGE == 3:
                        inter_blk(vector, c, range(4, 8))

        @block.gpsimd
        def _(gpsimd):
            for u in range(4 * NCH if PA_STAGE >= 1 else 0):
                p = u % 2
                gpsimd.wait_ge(sDqS1, u + 1)
                if u >= 2:
                    gpsimd.wait_ge(sDqV2, u - 1)
                gpsimd.tensor_scalar(m4[:, :], sgn[p][:, :], -2.0, -2.0,
                                     ALU.mult, ALU.add)
                gpsimd.tensor_add(u_[p][:, :], halfn[p][:, :], m4[:, :]) \
                    .then_inc(sDqP, 1)

        @block.scalar
        def _(scalar):
            def _sign(scalar, u):
                p = u % 2
                scalar.wait_ge(sDqV1, u + 1)
                if u >= 2:
                    scalar.wait_ge(sDqV2, u - 1)
                scalar.activation(sgn[p][:, :], halfn[p][:, :], AF.Sign,
                                  bias=cm375[:, :]).then_inc(sDqS1, 1)

            def _relus(scalar, u):
                p = u % 2
                scalar.wait_ge(sDqP, u + 1)
                if u >= 2:
                    scalar.wait_ge(sDqV2, u - 1)
                scalar.activation(b1[p][:, :], u_[p][:, :], AF.Relu,
                                  bias=cm2[:, :])
                scalar.activation(b2[p][:, :], u_[p][:, :], AF.Relu,
                                  bias=cm3[:, :]).then_inc(sDqS, 1)

            def relus_blk(scalar, cc):
                for k in range(4):
                    u = 4 * cc + k
                    _sign(scalar, u)
                    if k >= 1:
                        _relus(scalar, u - 1)
                _relus(scalar, 4 * cc + 3)

            if PA_STAGE >= 1:
                relus_blk(scalar, 0)
            for c in range(NCH if PA_STAGE >= 1 else 0):
                if c + 1 < NCH:
                    relus_blk(scalar, c + 1)
                for tc in range(8 if PA_STAGE in (3, 6) else 0):
                    G2 = 8 * c + tc
                    scalar.wait_ge(sMMA, 16 * c + 2 * tc + 1)
                    if G2 >= 2:
                        scalar.wait_ge(sInt, G2 - 1)
                    scalar.activation(gel[tc % 2][:, :], ps_g[tc % 2][:, :],
                                      AF.Gelu_apprx_tanh).then_inc(sGel, 1)
                    if tc >= 1 and PA_STAGE == 3:
                        scalar.wait_ge(sInt, G2)
                        scalar.dma_start(int_d[tc - 1, c, :, :],
                                         int_sb[(tc - 1) % 2][:, :]) \
                            .then_inc(sStI[(tc - 1) % 2], 16)
                if PA_STAGE == 3:
                    scalar.wait_ge(sInt, 8 * c + 8)
                    scalar.dma_start(int_d[7, c, :, :], int_sb[1][:, :]) \
                        .then_inc(sStI[1], 16)
            if PA_STAGE == 3:
                for pp in range(2):
                    scalar.wait_ge(sStI[pp], 16 * 4 * NCH)

        @block.tensor
        def _(tensor):
            for t in range(NT):
                tensor.wait_ge(xnL[t % 2], 16 * (t // 2 + 1))
                for g in range(4):
                    GG = 4 * t + g
                    if GG >= 2:
                        tensor.wait_ge(sTrE, GG - 1)
                    mm = None
                    for k in range(4):
                        ch = 4 * g + k
                        mm = tensor.matmul(
                            ps_tr[GG % 2][:, k * 128:(k + 1) * 128],
                            xn_l[t % 2][:, ch * 128:(ch + 1) * 128],
                            ident[:, :], start=True, stop=True)
                    mm.then_inc(sTr, 1)
            def trans_blk(tensor, cc):
                for wi in range(2):
                    cw = 2 * cc + wi
                    tensor.wait_ge(sDqV2, 4 * cc + 2 * wi + 2)
                    for g in range(4):
                        GG = 128 + 4 * cw + g
                        tensor.wait_ge(sTrE, GG - 1)
                        mm = None
                        for k in range(4):
                            ch = 4 * g + k
                            mm = tensor.matmul(
                                ps_tr[GG % 2][:, k * 128:(k + 1) * 128],
                                wdq[cw % 2][:, ch * 128:(ch + 1) * 128],
                                ident[:, :], start=True, stop=True)
                        mm.then_inc(sTr, 1)

            def mms_blk(tensor, c, tcs):
                for tc in tcs:
                    G2 = 8 * c + tc
                    if G2 >= 2 and PA_STAGE != 5:
                        tensor.wait_ge(sGel, G2 - 1)
                        if PA_STAGE != 6:
                            tensor.wait_ge(sInt, G2 - 1)
                    mm = None
                    for ch in range(NC_CHUNKS):
                        mm = tensor.matmul(
                            ps_g[tc % 2][:, :],
                            wT["g"][c % 2][:, ch * 128:(ch + 1) * 128],
                            xnT[:, ch * S_LEN + tc * 512:
                                ch * S_LEN + tc * 512 + 512],
                            start=(ch == 0), stop=(ch == NC_CHUNKS - 1))
                    mm.then_inc(sMMA, 1)
                    mm = None
                    for ch in range(NC_CHUNKS):
                        mm = tensor.matmul(
                            ps_u[tc % 2][:, :],
                            wT["u"][c % 2][:, ch * 128:(ch + 1) * 128],
                            xnT[:, ch * S_LEN + tc * 512:
                                ch * S_LEN + tc * 512 + 512],
                            start=(ch == 0), stop=(ch == NC_CHUNKS - 1))
                    mm.then_inc(sMMA, 1)

            if PA_STAGE >= 2:
                trans_blk(tensor, 0)
                for c in range(NCH):
                    tensor.wait_ge(sTrE, 128 + 8 * c + 8)
                    if PA_STAGE in (3, 5, 6):
                        mms_blk(tensor, c, range(0, 4))
                    if c + 1 < NCH:
                        trans_blk(tensor, c + 1)
                    if PA_STAGE in (3, 5, 6):
                        mms_blk(tensor, c, range(4, 8))


def _attn(nc, outer_es, env):
    qt_d, kt_d, v_d, s_d = env["qt_d"], env["kt_d"], env["v_d"], env["s_d"]
    wout_d, mask_d, dbg_d = env["wout_d"], env["mask_d"], env["dbg_d"]
    dbg2_d = env["dbg2_d"]
    ident, qn2, k2m_b = env["ident"], env["qn2"], env["k2m_b"]

    NG = NT // 4
    NU = NG * HEADS
    qk_per = [4 * (u // 16 + 1) for u in range(NU)]
    qk_base = [0]
    for u in range(NU):
        qk_base.append(qk_base[-1] + qk_per[u])
    pt_per = qk_per
    pt_base = qk_base

    with ExitStack() as es:
        def sb(name, shape, dt):
            return es.enter_context(nc.sbuf_tensor(name, shape, dt))

        def sem(name):
            return es.enter_context(nc.semaphore(name))

        kT = sb("at_kT", [128, S_LEN], BF16)
        v_sb = sb("at_v", [128, S_LEN], BF16)
        wout = sb("at_wout", [128, NC_CHUNKS * D], BF16)
        qT = [sb(f"at_qT{i}", [128, D], BF16) for i in range(4)]
        p_row = [sb(f"at_p{i}", [128, S_LEN], BF16) for i in range(4)]
        pT_sb = [sb(f"at_pT{p}", [128, 512], BF16) for p in range(2)]
        oT_sb = [sb(f"at_oT{p}", [128, HEADS * 512], BF16) for p in range(2)]
        diag = [[sb(f"at_dg{p}_{i}", [128, 128], BF16) for i in range(4)]
                for p in range(2)]
        masks = sb("at_mask", [128, 4 * 512], BF16)
        msneg = sb("at_msneg", [128, NT * HEADS], F32)
        denp = sb("at_denp", [128, 32], F32)
        den4 = sb("at_den4", [128, 4], F32)
        r4 = sb("at_r4", [128, 4], F32)
        s_t = [sb(f"at_s{p}", [128, D], F32) for p in range(2)]
        dbg2 = sb("at_dbg2", [128, 512], F32) if DEBUG else None
        wout_f = [s_t[0], s_t[1]]

        ps_qk = [es.enter_context(nc.psum_tensor(f"at_psqk{p}", [128, 512], F32))
                 for p in range(2)]
        ps_pt = [es.enter_context(nc.psum_tensor(f"at_pspt{p}", [128, 512], F32))
                 for p in range(2)]
        ps_oT = [es.enter_context(nc.psum_tensor(f"at_psot{p}", [128, 512], F32))
                 for p in range(2)]
        ps_s = es.enter_context(nc.psum_tensor("at_pss", [128, 512], F32))

        sLd3, sWcv3 = sem("at_ld"), sem("at_wcv")
        sQT2 = sem("at_qt")
        sQK, sEXP = sem("at_qk"), sem("at_exp")
        sMSK, sDIAG = sem("at_msk"), sem("at_diag")
        sPT, sPTE = sem("at_pt"), sem("at_pte")
        sPV, sOTE = sem("at_pv"), sem("at_ote")
        sSP, sSST = sem("at_sp"), sem("at_sst")
        wtL = [sem(f"at_wtl{p}") for p in range(2)]
        sStS = [sem(f"at_sts{p}") for p in range(2)]
        sPRE = sem("at_pre")

        block = es.enter_context(nc.Block())

        @block.sync
        def _(sync):
            sync.dma_start(kT[:, :], kt_d).then_inc(sLd3, 16)
            sync.dma_start(v_sb[:, :], v_d).then_inc(sLd3, 16)
            sync.dma_start(masks[:, :].rearrange("q (v j) -> q v j", j=512),
                           mask_d.rearrange("(v q) j -> q v j", q=128)) \
                .then_inc(sLd3, 16)
            for c in range(NC_CHUNKS):
                if c >= 2:
                    sync.wait_ge(sWcv3, c - 1)
                sync.dma_start(wout_f[c % 2][:, :], wout_d[c * 128:(c + 1) * 128, :]) \
                    .then_inc(wtL[c % 2], 16)
            for g in range(NG):
                if g >= 1:
                    sync.wait_ge(sQK, qk_base[16 * g])
                for i in range(4):
                    sync.dma_start(qT[i][:, :], qt_d[4 * g + i, :, :]) \
                        .then_inc(sQT2, 16)

        @block.vector
        def _(vector):
            for c in range(NC_CHUNKS):
                vector.wait_ge(wtL[c % 2], 16 * (c // 2 + 1))
                vector.tensor_copy(wout[:, c * D:(c + 1) * D], wout_f[c % 2][:, :]) \
                    .then_inc(sWcv3, 1)
            vector.tensor_tensor(msneg[:, :], qn2[:, :],
                                 k2m_b[:, :].broadcast_to([128, NT * HEADS]),
                                 ALU.mult).then_inc(sPRE, 1)
            vector.wait_ge(sPRE, 2)
            vector.tensor_scalar(msneg[:, :], msneg[:, :], -1.0, -1.0,
                                 ALU.mult, ALU.add).then_inc(sPRE, 1)
            for u in range(NU):
                g, h = u // 16, u % 16
                J = g + 1
                for i in range(4):
                    qk_idx = qk_base[u] + i * J + (J - 1)
                    vector.wait_ge(sEXP, qk_idx + 1)
                    vector.scalar_tensor_tensor(
                        p_row[i][:, g * 512:(g + 1) * 512],
                        p_row[i][:, g * 512:(g + 1) * 512], 1.0,
                        masks[:, i * 512:(i + 1) * 512],
                        ALU.mult, ALU.mult,
                        accum_out=denp[:, i * 8 + (J - 1):i * 8 + J]) \
                        .then_inc(sMSK, 1)
                vector.wait_ge(sEXP, qk_base[u + 1])
                vector.wait_ge(sMSK, 4 * (u + 1))
                vector.tensor_reduce(
                    den4[:, :].unsqueeze(2),
                    denp[:, :].rearrange("q (i j) -> q i j", j=8)[:, :, 0:J],
                    AX.X, ALU.add).then_inc(sDIAG, 1)
                vector.wait_ge(sDIAG, 3 * u + 1)
                if DEBUG:
                    vector.tensor_copy(dbg2[:, 4 * u:4 * u + 4], den4[:, :])
                vector.reciprocal(r4[:, :], den4[:, :]).then_inc(sDIAG, 1)
                vector.wait_ge(sDIAG, 3 * u + 2)
                for i in range(4):
                    ins = vector.tensor_scalar(diag[u % 2][i][:, :], ident[:, :],
                                               r4[:, i:i + 1], None, ALU.mult)
                ins.then_inc(sDIAG, 1)
                for jj in range(4 * J):
                    pt_idx = pt_base[u] + jj
                    vector.wait_ge(sPT, pt_idx + 1)
                    vector.tensor_copy(pT_sb[jj % 2][:, :], ps_pt[jj % 2][:, :]) \
                        .then_inc(sPTE, 1)
                vector.wait_ge(sPV, u + 1)
                if g >= 2:
                    vector.wait_ge(sSP, 16 * (g - 1))
                vector.tensor_copy(oT_sb[g % 2][:, h * 512:(h + 1) * 512],
                                   ps_oT[u % 2][:, :]).then_inc(sOTE, 1)

        @block.scalar
        def _(scalar):
            scalar.wait_ge(sPRE, 1)
            scalar.activation(msneg[:, :], msneg[:, :], AF.Sqrt).then_inc(sPRE, 1)
            if DEBUG:
                scalar.wait_ge(sPRE, 3)
                scalar.dma_start(dbg_d[:, 0:NT * HEADS], msneg[:, :]).then_inc(sPRE, 16)
                scalar.dma_start(dbg_d[:, NT * HEADS:2 * NT * HEADS], qn2[:, :]) \
                    .then_inc(sPRE, 16)
                scalar.dma_start(dbg_d[:, 2 * NT * HEADS:], k2m_b[:, :],
                                 allow_slow_non_contiguous=True).then_inc(sPRE, 16)
                scalar.wait_ge(sPRE, 51)
            for u in range(NU):
                g, h = u // 16, u % 16
                J = g + 1
                if u >= 1:
                    scalar.wait_ge(sPT, pt_base[u])
                    scalar.wait_ge(sDIAG, 3 * u)
                for i in range(4):
                    for jt in range(J):
                        qk_idx = qk_base[u] + i * J + jt
                        scalar.wait_ge(sQK, qk_idx + 1)
                        col = (4 * g + i) * HEADS + h
                        if jt < J - 1:
                            scalar.activation(
                                p_row[i][:, jt * 512:(jt + 1) * 512],
                                ps_qk[qk_idx % 2][:, :], AF.Exp,
                                bias=msneg[:, col:col + 1],
                                accum_out=denp[:, i * 8 + jt:i * 8 + jt + 1]) \
                                .then_inc(sEXP, 1)
                        else:
                            scalar.activation(
                                p_row[i][:, jt * 512:(jt + 1) * 512],
                                ps_qk[qk_idx % 2][:, :], AF.Exp,
                                bias=msneg[:, col:col + 1]).then_inc(sEXP, 1)
                if h == 15:
                    for i in range(4):
                        it = 4 * g + i
                        if it >= 2:
                            scalar.wait_ge(sStS[it % 2], 16 * (it // 2))
                        for dq in range(4):
                            sp_idx = 16 * g + 4 * i + dq
                            scalar.wait_ge(sSP, sp_idx + 1)
                            scalar.activation(s_t[i % 2][:, dq * 512:(dq + 1) * 512],
                                              ps_s[:, :], AF.Copy).then_inc(sSST, 1)
                        scalar.dma_start(s_d[it, :, :], s_t[i % 2][:, :]) \
                            .then_inc(sStS[it % 2], 16)
                    if g == NG - 1:
                        for pp in range(2):
                            scalar.wait_ge(sStS[pp], 16 * (NT // 2))
            if DEBUG and u == NU - 1:
                scalar.wait_ge(sDIAG, 3 * NU)
                scalar.dma_start(dbg2_d, dbg2[:, :]).then_inc(sPRE, 16)
                scalar.wait_ge(sPRE, 67)


        @block.tensor
        def _(tensor):
            tensor.wait_ge(sLd3, 48)
            for u in range(NU):
                g, h = u // 16, u % 16
                J = g + 1
                if u == 0:
                    tensor.wait_ge(sQT2, 64)
                elif h == 0:
                    tensor.wait_ge(sQT2, 64 * (g + 1))
                for i in range(4):
                    lhs = qT[i][:, h * 128:(h + 1) * 128]
                    for jt in range(J):
                        qk_idx = qk_base[u] + i * J + jt
                        if qk_idx >= 2:
                            tensor.wait_ge(sEXP, qk_idx - 1)
                        tensor.matmul(ps_qk[qk_idx % 2][:, :], lhs,
                                      kT[:, jt * 512:(jt + 1) * 512],
                                      start=True, stop=True).then_inc(sQK, 1)
                tensor.wait_ge(sDIAG, 3 * u + 3)
                for jj in range(4 * J):
                    pt_idx = pt_base[u] + jj
                    if pt_idx >= 2:
                        tensor.wait_ge(sPTE, pt_idx - 1)
                    mm = None
                    for i in range(4):
                        mm = tensor.matmul(ps_pt[jj % 2][:, i * 128:(i + 1) * 128],
                                           p_row[i][:, jj * 128:(jj + 1) * 128],
                                           diag[u % 2][i][:, :],
                                           start=True, stop=True)
                    mm.then_inc(sPT, 1)
                    tensor.wait_ge(sPTE, pt_idx + 1)
                    if jj == 0 and u >= 2:
                        tensor.wait_ge(sOTE, u - 1)
                    pv = tensor.matmul(ps_oT[u % 2][:, :],
                                       v_sb[:, jj * 128:(jj + 1) * 128],
                                       pT_sb[jj % 2][:, :],
                                       start=(jj == 0), stop=(jj == 4 * J - 1))
                    if jj == 4 * J - 1:
                        pv.then_inc(sPV, 1)
                if h == 15:
                    tensor.wait_ge(sOTE, 16 * (g + 1))
                    for i in range(4):
                        for dq in range(4):
                            sp_idx = 16 * g + 4 * i + dq
                            if sp_idx >= 1:
                                tensor.wait_ge(sSST, sp_idx)
                            mm = None
                            for e in range(NC_CHUNKS):
                                mm = tensor.matmul(
                                    ps_s[:, :],
                                    oT_sb[g % 2][:, e * 512 + i * 128:
                                                  e * 512 + i * 128 + 128],
                                    wout[:, e * D + dq * 512:e * D + dq * 512 + 512],
                                    start=(e == 0), stop=(e == NC_CHUNKS - 1))
                            mm.then_inc(sSP, 1)


def _sst_before(sp_idx):
    k, q = divmod(sp_idx, 4)
    return 20 * k + q


def _wd(nc, outer_es, env):
    dwp_d, dsc_d, wdt_d = env["dwp_d"], env["dsc_d"], env["wdt_d"]
    ident = env["ident"]

    with ExitStack() as es:
        def sb(name, shape, dt):
            return es.enter_context(nc.sbuf_tensor(name, shape, dt))

        def sem(name):
            return es.enter_context(nc.semaphore(name))

        pk = [sb(f"wd_pk{p}", [128, 4 * 512], I32) for p in range(2)]
        sc = [sb(f"wd_sc{p}", [128, 4 * 32], F32) for p in range(2)]
        nib_i = sb("wd_nibi", [128, 1024], I32)
        halfn = [sb(f"wd_hn{p}", [128, 1024], BF16) for p in range(2)]
        m4 = sb("wd_m4", [128, 1024], BF16)
        u_ = [sb(f"wd_u{p}", [128, 1024], BF16) for p in range(2)]
        sgn = [sb(f"wd_sgn{p}", [128, 1024], BF16) for p in range(2)]
        b1 = [sb(f"wd_b1{p}", [128, 1024], BF16) for p in range(2)]
        b2 = [sb(f"wd_b2{p}", [128, 1024], BF16) for p in range(2)]
        wdq = [sb(f"wd_wdq{s}", [128, 1024], BF16) for s in range(4)]
        stg = [sb(f"wd_stg{p}", [128, 512], BF16) for p in range(8)]
        cm2 = sb("wd_cm2", [128, 1], F32)
        cm3 = sb("wd_cm3", [128, 1], F32)
        cm375 = sb("wd_cm375", [128, 1], F32)
        ps_tr = [es.enter_context(nc.psum_tensor(f"wd_pstr{p}", [128, 512], F32))
                 for p in range(2)]

        sLd = sem("wd_ld")
        pkwL = [sem(f"wd_pkl{p}") for p in range(2)]
        sStW = [sem(f"wd_stw{k}") for k in range(8)]
        sV1, sP, sS, sV2 = sem("wd_v1"), sem("wd_p"), sem("wd_s"), sem("wd_v2")
        sS1 = sem("wd_s1")
        sTr, sTrE, sSt = sem("wd_tr"), sem("wd_tre"), sem("wd_st")

        block = es.enter_context(nc.Block())
        NIT = 64

        @block.sync
        def _(sync):
            for it in range(NIT):
                o, r = it // 4, it % 4
                if it >= 2:
                    sync.wait_ge(sV1, 4 * (it - 1))
                    sync.wait_ge(sV2, 4 * (it - 1))
                sync.dma_start(
                    pk[it % 2][:, :].rearrange("p (s c) -> p s c", c=512),
                    dwp_d[512 * r:512 * (r + 1), 512 * o:512 * (o + 1)]
                    .rearrange("(s p) c -> p s c", p=128)).then_inc(pkwL[it % 2], 16)
                sync.dma_start(
                    sc[it % 2][:, :].rearrange("p (s c) -> p s c", c=32),
                    dsc_d[512 * r:512 * (r + 1), 32 * o:32 * (o + 1)]
                    .rearrange("(s p) c -> p s c", p=128)).then_inc(pkwL[it % 2], 16)

        @block.vector
        def _(vector):
            vector.memset(cm2[:, :], -2.0)
            vector.memset(cm3[:, :], -3.0)
            vector.memset(cm375[:, :], -3.75)
            for it in range(NIT):
                vector.wait_ge(pkwL[it % 2], 32 * (it // 2 + 1))
                for s in range(4):
                    u = 4 * it + s
                    p = u % 2
                    if u >= 2:
                        vector.wait_ge(sP, u - 1)
                    lo = nib_i[:, :].rearrange("q (k two) -> q k two", two=2)[:, :, 0:1]
                    hi = nib_i[:, :].rearrange("q (k two) -> q k two", two=2)[:, :, 1:2]
                    srcp = pk[it % 2][:, s * 512:(s + 1) * 512].unsqueeze(2)
                    vector.tensor_scalar(lo, srcp, 15, None, ALU.bitwise_and)
                    vector.tensor_scalar(hi, srcp, 4, 15, ALU.logical_shift_right,
                                         ALU.bitwise_and)
                    vector.tensor_scalar(halfn[p][:, :], nib_i[:, :], 0.5, None,
                                         ALU.mult).then_inc(sV1, 1)
                    if s >= 1:
                        _wd_v2(vector, u - 1, sV2, sc, wdq,
                               u_, sgn, b1, b2, sS, sTr, ALU)
                _wd_v2(vector, 4 * it + 3, sV2, sc, wdq, u_, sgn, b1, b2, sS,
                       sTr, ALU)
                for c8 in range(8):
                    GG = 8 * it + c8
                    vector.wait_ge(sTr, GG + 1)
                    if it >= 1:
                        vector.wait_ge(sStW[c8], 16 * it)
                    vector.tensor_copy(stg[c8][:, :], ps_tr[c8 % 2][:, :]) \
                        .then_inc(sTrE, 1)

        @block.gpsimd
        def _(gpsimd):
            for u in range(4 * NIT):
                p = u % 2
                gpsimd.wait_ge(sS1, u + 1)
                if u >= 2:
                    gpsimd.wait_ge(sV2, u - 1)
                gpsimd.tensor_scalar(m4[:, :], sgn[p][:, :], -2.0, -2.0,
                                     ALU.mult, ALU.add)
                gpsimd.tensor_add(u_[p][:, :], halfn[p][:, :], m4[:, :]) \
                    .then_inc(sP, 1)

        @block.scalar
        def _(scalar):
            def _sgn(scalar, u):
                p = u % 2
                scalar.wait_ge(sV1, u + 1)
                if u >= 2:
                    scalar.wait_ge(sV2, u - 1)
                scalar.activation(sgn[p][:, :], halfn[p][:, :], AF.Sign,
                                  bias=cm375[:, :]).then_inc(sS1, 1)

            def _rel(scalar, u):
                p = u % 2
                scalar.wait_ge(sP, u + 1)
                if u >= 2:
                    scalar.wait_ge(sV2, u - 1)
                scalar.activation(b1[p][:, :], u_[p][:, :], AF.Relu,
                                  bias=cm2[:, :])
                scalar.activation(b2[p][:, :], u_[p][:, :], AF.Relu,
                                  bias=cm3[:, :]).then_inc(sS, 1)

            def relus(scalar, it):
                for s in range(4):
                    u = 4 * it + s
                    _sgn(scalar, u)
                    if s >= 1:
                        _rel(scalar, u - 1)
                _rel(scalar, 4 * it + 3)

            relus(scalar, 0)
            for it in range(NIT):
                if it + 1 < NIT:
                    relus(scalar, it + 1)
                o, r = it // 4, it % 4
                for c8 in range(8):
                    GG = 8 * it + c8
                    scalar.wait_ge(sTrE, GG + 1)
                    scalar.dma_start(wdt_d[8 * o + c8, r, :, :], stg[c8][:, :]) \
                        .then_inc(sStW[c8], 16)
            for k in range(8):
                scalar.wait_ge(sStW[k], 16 * NIT)

        @block.tensor
        def _(tensor):
            for it in range(NIT):
                tensor.wait_ge(sV2, 4 * (it + 1))
                for c8 in range(8):
                    GG = 8 * it + c8
                    if GG >= 2:
                        tensor.wait_ge(sTrE, GG - 1)
                    mm = None
                    for s in range(4):
                        mm = tensor.matmul(
                            ps_tr[c8 % 2][:, s * 128:(s + 1) * 128],
                            wdq[s][:, c8 * 128:(c8 + 1) * 128],
                            ident[:, :], start=True, stop=True)
                    mm.then_inc(sTr, 1)


def _wd_v2(vector, u, sV2, sc, wdq, u_, sgn, b1, b2, sS, sTr, _ALU):
    s = u % 4
    p = u % 2
    vector.wait_ge(sS, u + 1)
    if u >= 4:
        vector.wait_ge(sTr, 8 * (u // 4 - 1) + 8)
    vector.scalar_tensor_tensor(b2[p][:, :], b2[p][:, :], -2.0, b1[p][:, :],
                                _ALU.mult, _ALU.subtract)
    vector.tensor_sub(u_[p][:, :], b2[p][:, :], u_[p][:, :])
    vector.tensor_mul(sgn[p][:, :], u_[p][:, :], sgn[p][:, :])
    scb = sc[(u // 4) % 2][:, s * 32:(s + 1) * 32].unsqueeze(2).broadcast_to(
        [128, 32, 32])
    vector.tensor_tensor(wdq[s][:, :].rearrange("q (k c2) -> q k c2", c2=32),
                         sgn[p][:, :].rearrange("q (k c2) -> q k c2", c2=32),
                         scb, _ALU.mult).then_inc(sV2, 1)



def _passB(nc, outer_es, env):
    x_d, s_d, out_d = env["x_d"], env["s_d"], env["out_d"]
    int_d, wdt_d = env["int_d"], env["wdt_d"]
    dbg3_d = env["dbg3_d"]

    with ExitStack() as es:
        def sb(name, shape, dt):
            return es.enter_context(nc.sbuf_tensor(name, shape, dt))

        def sem(name):
            return es.enter_context(nc.semaphore(name))

        intT = sb("pb_int", [128, 128 * 512], BF16)
        wdt = [sb(f"pb_wdt{p}", [128, 512], BF16) for p in range(4)]
        g_sb = [sb(f"pb_g{s}", [128, D], F32) for s in range(4)]
        x_t = [sb(f"pb_x{p}", [128, D], F32) for p in range(2)]
        s_t = [sb(f"pb_s{p}", [128, D], F32) for p in range(2)]
        ps_d = [es.enter_context(nc.psum_tensor(f"pb_ps{k}", [128, 512], F32))
                for k in range(8)]

        sLdI, sXS = sem("pb_ldi"), sem("pb_xs")
        wdtL = [sem(f"pb_wdtl{k}") for k in range(4)]
        sMMc = sem("pb_mmc")
        sSIL, sCMB, sOUT = sem("pb_sil"), sem("pb_cmb"), sem("pb_out")
        xsL = [sem(f"pb_xsl{p}") for p in range(2)]
        sOutS = [sem(f"pb_outs{p}") for p in range(2)]
        sDBG = sem("pb_dbg")

        block = es.enter_context(nc.Block())

        @block.sync
        def _(sync):
            for c in range(128):
                sync.dma_start(intT[:, c * 512:(c + 1) * 512], int_d[0, c, :, :]) \
                    .then_inc(sLdI, 16)
            for ts in range(NSUP):
                for hq in range(4):
                    Q = 4 * ts + hq
                    for c in range(128):
                        if Q * 128 + c >= 4:
                            sync.wait_ge(sMMc, Q * 128 + c - 3)
                        sync.dma_start(wdt[c % 4][:, :], wdt_d[c, hq, :, :]) \
                            .then_inc(wdtL[c % 4], 16)
                        if hq == 3 and ts + 1 < NSUP:
                            sync.wait_ge(sMMc, 128 * Q + c + 1)
                            sync.dma_start(intT[:, c * 512:(c + 1) * 512],
                                           int_d[ts + 1, c, :, :]).then_inc(sLdI, 16)

        @block.gpsimd
        def _(gpsimd):
            for ts in range(NSUP):
                for sub in range(4):
                    SB = 4 * ts + sub
                    if SB >= 2:
                        gpsimd.wait_ge(sCMB, SB - 1)
                        gpsimd.wait_ge(sOutS[SB % 2], 16 * (SB // 2))
                    t0 = (4 * ts + sub) * 128
                    gpsimd.dma_start(x_t[sub % 2][:, :], x_d[t0:t0 + 128, :]) \
                        .then_inc(xsL[sub % 2], 16)
                    gpsimd.dma_start(s_t[sub % 2][:, :], s_d[4 * ts + sub, :, :]) \
                        .then_inc(xsL[sub % 2], 16)

        @block.tensor
        def _(tensor):
            for ts in range(NSUP):
                for hq in range(4):
                    Q = 4 * ts + hq
                    for c in range(128):
                        n_chunk = 128 * Q + c
                        tensor.wait_ge(wdtL[c % 4], 16 * (n_chunk // 4 + 1))
                        tensor.wait_ge(sLdI, 16 * 128 * (ts + 1))
                        mm = None
                        for sub in range(4):
                            if c == 0 and Q >= 2:
                                tensor.wait_ge(sSIL, 4 * (Q - 1))
                            mm = tensor.matmul(
                                ps_d[4 * (hq % 2) + sub][:, :],
                                intT[:, c * 512 + sub * 128:c * 512 + sub * 128 + 128],
                                wdt[c % 4][:, :],
                                start=(c == 0), stop=(c == 127))
                        mm.then_inc(sMMc, 1)
            tensor.matmul(ps_d[0][0:1, 0:1], intT[:, 0:1], wdt[0][:, 0:1],
                          start=True, stop=True).then_inc(sMMc, 1)

        @block.scalar
        def _(scalar):
            for ts in range(NSUP):
                for hq in range(4):
                    Q = 4 * ts + hq
                    for sub in range(4):
                        scalar.wait_ge(sMMc, min(128 * (Q + 1) + 64,
                                                 128 * 4 * NSUP + 1))
                        if ts >= 1 and hq == 0:
                            scalar.wait_ge(sCMB, 4 * (ts - 1) + sub + 1)
                        scalar.activation(g_sb[sub][:, hq * 512:(hq + 1) * 512],
                                          ps_d[4 * (hq % 2) + sub][:, :],
                                          AF.Silu).then_inc(sSIL, 1)
                if DEBUG and ts == 0:
                    for sub in range(4):
                        scalar.wait_ge(sSIL, 12 + sub + 1)
                        scalar.dma_start(dbg3_d[sub, :, :], g_sb[sub][:, :]) \
                            .then_inc(sDBG, 16)
                    scalar.wait_ge(sDBG, 64)
                for sub in range(4):
                    SB = 4 * ts + sub
                    scalar.wait_ge(sCMB, SB + 1)
                    t0 = (4 * ts + sub) * 128
                    scalar.dma_start(out_d[t0:t0 + 128, :], s_t[sub % 2][:, :]) \
                        .then_inc(sOutS[SB % 2], 16)
            for pp in range(2):
                scalar.wait_ge(sOutS[pp], 16 * 2 * NSUP)

        @block.vector
        def _(vector):
            for ts in range(NSUP):
                for sub in range(4):
                    SB = 4 * ts + sub
                    vector.wait_ge(sSIL, 4 * (4 * ts + 3) + sub + 1)
                    vector.wait_ge(xsL[sub % 2], 32 * (SB // 2 + 1))
                    if SB >= 2:
                        vector.wait_ge(sOutS[SB % 2], 16 * (SB // 2))
                    vector.tensor_mul(s_t[sub % 2][:, :], s_t[sub % 2][:, :],
                                      g_sb[sub][:, :])
                    vector.tensor_add(s_t[sub % 2][:, :], s_t[sub % 2][:, :],
                                      x_t[sub % 2][:, :]).then_inc(sCMB, 1)



_CONSTS = None


def kernel(L0_attn_x, L0_attn_Wq, L0_attn_Wkv, L0_attn_Wout,
           L0_ffn_gate_W_packed, L0_ffn_gate_scales,
           L0_ffn_up_W_packed, L0_ffn_up_scales,
           L0_ffn_down_W_packed, L0_ffn_down_scales):
    global _CONSTS
    if _CONSTS is None:
        _CONSTS = _host_consts()
    consts = _CONSTS

    x = np.ascontiguousarray(np.asarray(L0_attn_x, dtype=np.float32))
    shared = dict(
        wq=np.ascontiguousarray(np.asarray(L0_attn_Wq, dtype=np.float32)),
        wkv=np.ascontiguousarray(np.asarray(L0_attn_Wkv, dtype=np.float32)),
        wout=np.ascontiguousarray(np.asarray(L0_attn_Wout, dtype=np.float32)),
        gwp=np.ascontiguousarray(np.asarray(L0_ffn_gate_W_packed, dtype=np.int32)),
        gsc=np.ascontiguousarray(np.asarray(L0_ffn_gate_scales, dtype=np.float32)),
        uwp=np.ascontiguousarray(np.asarray(L0_ffn_up_W_packed, dtype=np.int32)),
        usc=np.ascontiguousarray(np.asarray(L0_ffn_up_scales, dtype=np.float32)),
        dwp=np.ascontiguousarray(np.asarray(L0_ffn_down_W_packed, dtype=np.int32)),
        dsc=np.ascontiguousarray(np.asarray(L0_ffn_down_scales, dtype=np.float32)),
        **consts)

    nc = build_nc()
    in_maps = [dict(x=np.ascontiguousarray(x[b]), **shared) for b in range(8)]
    res = run_bass_kernel_spmd(nc, in_maps, core_ids=list(range(8)))
    out = np.stack([res.results[b]["out"] for b in range(8)], axis=0)
    return out


# version 09
# speedup vs baseline: 1.5089x; 1.1111x over previous
import numpy as np
import ml_dtypes
from contextlib import ExitStack

import concourse.bass as bass
import concourse.mybir as mybir
from concourse.bass_utils import run_bass_kernel_spmd

F32 = mybir.dt.float32
BF16 = mybir.dt.bfloat16
I32 = mybir.dt.int32
AF = mybir.ActivationFunctionType
ALU = mybir.AluOpType
AX = mybir.AxisListType

S_LEN = 4096
D = 2048
HEADS = 16
DH = 128
NMLP = 16384
NT = S_LEN // 128
NC_CHUNKS = D // 128
SCALE = DH ** -0.5
EPS = 1e-5
TSUP = 512
NSUP = S_LEN // TSUP

DEBUG = False
PA_STAGE = 3
PA_NCH = None
PHASES = "1AWTB"


def _host_consts():
    ident = np.eye(128, dtype=ml_dtypes.bfloat16)
    inv = 1.0 / 10000.0 ** (np.arange(0, DH, 2, dtype=np.float64) / DH)
    tpos = np.arange(S_LEN, dtype=np.float64)[:, None] * inv[None, :]
    cos = np.cos(tpos)
    sin = np.sin(tpos)
    cosf = np.concatenate([cos, cos], 1)
    sinf = np.concatenate([sin, sin], 1)
    rope = np.concatenate(
        [cosf * SCALE, sinf * SCALE, cosf, sinf], axis=1).astype(np.float32)
    jj = np.arange(512)[None, :]
    ii = np.arange(128)[:, None]
    mask = np.zeros((4, 128, 512), dtype=ml_dtypes.bfloat16)
    for v in range(4):
        mask[v] = (jj <= (128 * v + ii)).astype(ml_dtypes.bfloat16)
    ones = np.ones((1, 128), dtype=ml_dtypes.bfloat16)
    return dict(c_ident=ident, c_rope=rope, c_mask=mask.reshape(512, 512),
                c_ones=ones)


def build_nc():
    nc = bass.Bass("TRN2", target_bir_lowering=False)

    x_d = nc.dram_tensor("x", [S_LEN, D], F32, kind="ExternalInput").ap()
    wq_d = nc.dram_tensor("wq", [D, D], F32, kind="ExternalInput").ap()
    wkv_d = nc.dram_tensor("wkv", [D, 256], F32, kind="ExternalInput").ap()
    wout_d = nc.dram_tensor("wout", [D, D], F32, kind="ExternalInput").ap()
    gwp_d = nc.dram_tensor("gwp", [NMLP, D // 2], I32, kind="ExternalInput").ap()
    gsc_d = nc.dram_tensor("gsc", [NMLP, 64], F32, kind="ExternalInput").ap()
    uwp_d = nc.dram_tensor("uwp", [NMLP, D // 2], I32, kind="ExternalInput").ap()
    usc_d = nc.dram_tensor("usc", [NMLP, 64], F32, kind="ExternalInput").ap()
    dwp_d = nc.dram_tensor("dwp", [D, NMLP // 2], I32, kind="ExternalInput").ap()
    dsc_d = nc.dram_tensor("dsc", [D, NMLP // 32], F32, kind="ExternalInput").ap()
    ident_d = nc.dram_tensor("c_ident", [128, 128], BF16, kind="ExternalInput").ap()
    rope_d = nc.dram_tensor("c_rope", [S_LEN, 512], F32, kind="ExternalInput").ap()
    mask_d = nc.dram_tensor("c_mask", [512, 512], BF16, kind="ExternalInput").ap()
    ones_d = nc.dram_tensor("c_ones", [1, 128], BF16, kind="ExternalInput").ap()

    out_d = nc.dram_tensor("out", [S_LEN, D], F32, kind="ExternalOutput").ap()

    skind = dict(kind="ExternalOutput") if DEBUG else {}
    xn_d = nc.dram_tensor("xn_s", [S_LEN, D], BF16, **skind).ap()
    qt_d = nc.dram_tensor("qt_s", [NT, 128, D], BF16, **skind).ap()
    kt_d = nc.dram_tensor("kt_s", [128, S_LEN], BF16, **skind).ap()
    v_d = nc.dram_tensor("v_s", [128, S_LEN], BF16, **skind).ap()
    int_d = nc.dram_tensor("int_s", [NSUP, 128, 128, TSUP], BF16, **skind).ap()
    wdt_d = nc.dram_tensor("wdt_s", [128, 4, 128, 512], BF16, **skind).ap()
    s_d = nc.dram_tensor("s_s", [NT, 128, D], F32, **skind).ap()
    dbg_d = nc.dram_tensor("dbg_s", [128, 2 * NT * HEADS + 1], F32, **skind).ap()
    dbg2_d = nc.dram_tensor("dbg2_s", [128, 512], F32, **skind).ap()
    dbg3_d = nc.dram_tensor("dbg3_s", [4, 128, D], F32, **skind).ap()

    with ExitStack() as es:
        def sb(name, shape, dt):
            return es.enter_context(nc.sbuf_tensor(name, shape, dt))

        ident = sb("ident", [128, 128], BF16)
        ones = sb("ones", [1, 128], BF16)
        kn2 = sb("kn2", [128, NT], F32)
        qn2 = sb("qn2", [128, NT * HEADS], F32)
        k2m_b = sb("k2m_b", [128, 1], F32)
        czero = sb("czero", [128, 1], F32)
        ceps = sb("ceps", [128, 1], F32)

        if "1" in PHASES:
            _stage1(nc, es, locals())
        if "A" in PHASES:
            _passA(nc, es, locals())
        if "T" in PHASES:
            _attn(nc, es, locals())
        if "W" in PHASES:
            _wd(nc, es, locals())
        if "B" in PHASES:
            _passB(nc, es, locals())

    return nc




def _stage1(nc, outer_es, env):
    x_d, wq_d, wkv_d = env["x_d"], env["wq_d"], env["wkv_d"]
    ident_d, rope_d, ones_d = env["ident_d"], env["rope_d"], env["ones_d"]
    xn_d, qt_d, kt_d, v_d = env["xn_d"], env["qt_d"], env["kt_d"], env["v_d"]
    ident, ones, kn2, qn2, k2m_b = (env["ident"], env["ones"], env["kn2"],
                                    env["qn2"], env["k2m_b"])
    czero, ceps = env["czero"], env["ceps"]

    with ExitStack() as es:
        def sb(name, shape, dt):
            return es.enter_context(nc.sbuf_tensor(name, shape, dt))

        def sem(name):
            return es.enter_context(nc.semaphore(name))

        wq_f = [sb(f"wq_f{p}", [128, D], F32) for p in range(2)]
        wq_bf = sb("wq_bf", [128, NC_CHUNKS * D], BF16)
        wkv_f = sb("wkv_f", [128, NC_CHUNKS * 256], F32)
        wkv_bf = sb("wkv_bf", [128, NC_CHUNKS * 256], BF16)
        x_t = [sb(f"x_t{p}", [128, D], F32) for p in range(2)]
        rope_t = [sb(f"rope_t{p}", [128, 512], F32) for p in range(2)]
        xn_t = [sb(f"xn_t{p}", [128, D], BF16) for p in range(2)]
        xnT_t = [sb(f"xnT_t{p}", [128, D], BF16) for p in range(2)]
        q_f = sb("q_f", [128, D], F32)
        q_ro = sb("q_ro", [128, D], BF16)
        qT_t = [sb(f"qT_t{p}", [128, D], BF16) for p in range(2)]
        sq_s = sb("sq_s", [128, D], F32)
        k_f = sb("k_f", [128, DH], F32)
        k_ro = sb("k_ro", [128, DH], BF16)
        kT_sb = sb("kT_sb", [128, S_LEN], BF16)
        v_sb = sb("v_sb", [128, S_LEN], BF16)
        stats = [sb(f"stats{p}", [128, 8], F32) for p in range(2)]
        ksc = sb("ksc", [128, 2], F32)
        k2bf = sb("k2bf", [128, 1], BF16)

        ps_q = es.enter_context(nc.psum_tensor("ps_q", [128, D], F32))
        ps_kv = es.enter_context(nc.psum_tensor("ps_kv", [128, 256], F32))
        ps_tr = [es.enter_context(nc.psum_tensor(f"ps_tr{p}", [128, 512], F32))
                 for p in range(2)]
        ps_b = es.enter_context(nc.psum_tensor("ps_b", [128, 128], F32))

        sLd, sLdW, sWcv = sem("s1_ld"), sem("s1_ldw"), sem("s1_wcv")
        xL = [sem(f"s1_xl{p}") for p in range(2)]
        rL = [sem(f"s1_rl{p}") for p in range(2)]
        wqL = [sem(f"s1_wql{p}") for p in range(2)]
        sStX = [sem(f"s1_stx{p}") for p in range(2)]
        sStQ = [sem(f"s1_stq{p}") for p in range(2)]
        sSq, sLN, sXn = sem("s1_sq"), sem("s1_ln"), sem("s1_xn")
        sTr, sTrE, sKT = sem("s1_tr"), sem("s1_tre"), sem("s1_kt")
        sMM, sQE, sRo = sem("s1_mm"), sem("s1_qe"), sem("s1_ro")
        sKV, sSt, sFin = sem("s1_kv"), sem("s1_st"), sem("s1_fin")

        block = es.enter_context(nc.Block())

        @block.sync
        def _(sync):
            sync.dma_start(ident[:, :], ident_d).then_inc(sLd, 16)
            sync.dma_start(ones[:, :], ones_d).then_inc(sLd, 16)
            sync.dma_start(
                wkv_f[:, :].rearrange("p (c n) -> p c n", n=256),
                wkv_d.rearrange("(c p) n -> p c n", p=128)).then_inc(sLdW, 16)
            for c in range(NC_CHUNKS):
                if c >= 2:
                    sync.wait_ge(sWcv, c - 1)
                sync.dma_start(wq_f[c % 2][:, :], wq_d[c * 128:(c + 1) * 128, :]) \
                    .then_inc(wqL[c % 2], 16)
            for t in range(NT):
                if t >= 2:
                    sync.wait_ge(sXn, t - 1)
                    sync.wait_ge(sSq, 2 * (t - 2) + 1)
                    sync.wait_ge(sKV, 4 * (t - 2) + 2)
                sync.dma_start(x_t[t % 2][:, :], x_d[t * 128:(t + 1) * 128, :]) \
                    .then_inc(xL[t % 2], 16)
                sync.dma_start(rope_t[t % 2][:, :], rope_d[t * 128:(t + 1) * 128, :]) \
                    .then_inc(rL[t % 2], 16)

        @block.vector
        def _(vector):
            vector.memset(czero[:, :], 0.0)
            vector.memset(ceps[:, :], EPS)
            vector.wait_ge(sLdW, 16)
            vector.tensor_copy(wkv_bf[:, :], wkv_f[:, :])
            for c in range(NC_CHUNKS):
                vector.wait_ge(wqL[c % 2], 16 * (c // 2 + 1))
                vector.tensor_copy(wq_bf[:, c * D:(c + 1) * D], wq_f[c % 2][:, :]) \
                    .then_inc(sWcv, 1)
            for t in range(NT):
                p = t % 2
                st = stats[p]
                vector.wait_ge(xL[t % 2], 16 * (t // 2 + 1))
                vector.tensor_reduce(st[:, 0:1], x_t[p][:, :], AX.X, ALU.add,
                                     negate=True).then_inc(sLN, 1)
                vector.wait_ge(sLN, 6 * t + 1)
                vector.tensor_scalar(st[:, 2:3], st[:, 0:1], 1.0 / D, None,
                                     ALU.mult).then_inc(sLN, 1)
                vector.wait_ge(sLN, 6 * t + 2)
                vector.tensor_mul(st[:, 3:4], st[:, 2:3], st[:, 2:3]) \
                    .then_inc(sLN, 1)
                vector.wait_ge(sSq, 2 * t + 1)
                vector.wait_ge(sLN, 6 * t + 3)
                vector.scalar_tensor_tensor(st[:, 4:5], st[:, 1:2], 1.0 / D,
                                            st[:, 3:4], ALU.mult, ALU.subtract) \
                    .then_inc(sLN, 1)
                vector.wait_ge(sLN, 6 * t + 5)
                vector.reciprocal(st[:, 6:7], st[:, 5:6]).then_inc(sLN, 1)
                vector.wait_ge(sLN, 6 * t + 6)
                if t >= 2:
                    vector.wait_ge(sStX[t % 2], 16 * (t // 2))
                vector.tensor_scalar(xn_t[p][:, :], x_t[p][:, :], st[:, 2:3],
                                     st[:, 6:7], ALU.add, ALU.mult) \
                    .then_inc(sXn, 1)
                if t >= 2:
                    vector.wait_ge(sMM, 80 * (t - 1))
                for g in range(4):
                    H = 8 * t + g
                    vector.wait_ge(sTr, H + 1)
                    vector.tensor_copy(
                        xnT_t[p][:, :].rearrange("q (c n) -> q c n", n=128)
                        [:, 4 * g:4 * g + 4, :],
                        ps_tr[g % 2][:, :].rearrange("q (c n) -> q c n", n=128)) \
                        .then_inc(sTrE, 1)
                vector.wait_ge(sQE, 8 * t + 4)
                vector.wait_ge(rL[t % 2], 16 * (t // 2 + 1))
                qv = q_f[:, :].rearrange("q (h c) -> q h c", c=DH)
                ov = q_ro[:, :].rearrange("q (h c) -> q h c", c=DH)
                sv = sq_s[:, :].rearrange("q (h c) -> q h c", c=DH)
                rp = rope_t[p]
                clo = rp[:, 0:64].unsqueeze(1).broadcast_to([128, HEADS, 64])
                chi = rp[:, 64:128].unsqueeze(1).broadcast_to([128, HEADS, 64])
                slo = rp[:, 128:192].unsqueeze(1).broadcast_to([128, HEADS, 64])
                shi = rp[:, 192:256].unsqueeze(1).broadcast_to([128, HEADS, 64])
                vector.tensor_tensor(sv[:, :, 0:64], qv[:, :, 0:64], clo, ALU.mult)
                vector.tensor_tensor(sv[:, :, 64:128], qv[:, :, 64:128], slo, ALU.mult)
                vector.tensor_tensor(ov[:, :, 0:64], sv[:, :, 0:64], sv[:, :, 64:128],
                                     ALU.subtract)
                vector.tensor_tensor(sv[:, :, 0:64], qv[:, :, 64:128], chi, ALU.mult)
                vector.tensor_tensor(sv[:, :, 64:128], qv[:, :, 0:64], shi, ALU.mult)
                vector.tensor_add(ov[:, :, 64:128], sv[:, :, 0:64], sv[:, :, 64:128]) \
                    .then_inc(sRo, 4)
                vector.wait_ge(sQE, 8 * t + 8)
                vector.tensor_reduce(
                    qn2[:, t * HEADS:(t + 1) * HEADS].unsqueeze(2),
                    sq_s[:, :].rearrange("q (h c) -> q h c", c=DH),
                    AX.X, ALU.add).then_inc(sRo, 1)
                if t >= 2:
                    vector.wait_ge(sStQ[t % 2], 16 * (t // 2))
                for g in range(4):
                    H = 8 * t + 4 + g
                    vector.wait_ge(sTr, H + 1)
                    vector.tensor_copy(
                        qT_t[p][:, :].rearrange("q (c n) -> q c n", n=128)
                        [:, 4 * g:4 * g + 4, :],
                        ps_tr[g % 2][:, :].rearrange("q (c n) -> q c n", n=128)) \
                        .then_inc(sTrE, 1)
                vector.wait_ge(sKV, 4 * t + 1)
                t1k, t2k = sq_s[:, 0:64], sq_s[:, 64:128]
                vector.tensor_tensor(t1k, k_f[:, 0:64], rp[:, 256:320], ALU.mult) \
                    .then_inc(sRo, 1)
                vector.tensor_tensor(t2k, k_f[:, 64:128], rp[:, 384:448], ALU.mult) \
                    .then_inc(sRo, 1)
                vector.wait_ge(sRo, 10 * t + 7)
                vector.tensor_tensor(k_ro[:, 0:64], t1k, t2k, ALU.subtract) \
                    .then_inc(sRo, 1)
                vector.tensor_tensor(t1k, k_f[:, 64:128], rp[:, 320:384], ALU.mult) \
                    .then_inc(sRo, 1)
                vector.tensor_tensor(t2k, k_f[:, 0:64], rp[:, 448:512], ALU.mult) \
                    .then_inc(sRo, 1)
                vector.wait_ge(sRo, 10 * t + 10)
                vector.tensor_add(k_ro[:, 64:128], t1k, t2k).then_inc(sKV, 1)
                vector.wait_ge(sMM, 80 * (t + 1))
                vector.tensor_copy(v_sb[:, t * 128:(t + 1) * 128], ps_kv[:, 128:256]) \
                    .then_inc(sKV, 1)
                vector.wait_ge(sKT, t + 1)
                vector.tensor_copy(kT_sb[:, t * 128:(t + 1) * 128], ps_b[:, :]) \
                    .then_inc(sKV, 1)
            vector.reduce_max(ksc[:, 0:1], kn2[:, :], AX.X).then_inc(sFin, 1)
            vector.wait_ge(sFin, 1)
            vector.tensor_copy(k2bf[:, :], ksc[:, 0:1]).then_inc(sFin, 1)
            vector.wait_ge(sFin, 3)
            vector.reduce_max(ksc[0:1, 1:2], ps_b[0:1, 0:128], AX.X).then_inc(sFin, 1)
            vector.wait_ge(sFin, 4)
            vector.tensor_copy(k2bf[0:1, 0:1], ksc[0:1, 1:2]).then_inc(sFin, 1)
            vector.wait_ge(sFin, 6)
            vector.tensor_copy(k2m_b[:, :], ps_b[:, 0:1]).then_inc(sFin, 1)

        @block.scalar
        def _(scalar):
            for t in range(NT):
                p = t % 2
                st = stats[p]
                scalar.wait_ge(xL[t % 2], 16 * (t // 2 + 1))
                if t >= 1:
                    scalar.wait_ge(sKV, 4 * (t - 1) + 2)
                scalar.activation(sq_s[:, :], x_t[p][:, :], AF.Square,
                                  accum_out=st[:, 1:2]).then_inc(sSq, 1)
                scalar.wait_ge(sLN, 6 * t + 4)
                scalar.activation(st[:, 5:6], st[:, 4:5], AF.Sqrt,
                                  bias=ceps[:, :]).then_inc(sLN, 1)
                scalar.wait_ge(sMM, 80 * (t + 1))
                for qq in range(4):
                    scalar.activation(q_f[:, qq * 512:(qq + 1) * 512],
                                      ps_q[:, qq * 512:(qq + 1) * 512],
                                      AF.Copy).then_inc(sQE, 1)
                scalar.wait_ge(sRo, 10 * t + 4)
                scalar.activation(sq_s[:, :], q_ro[:, :], AF.Square) \
                    .then_inc(sQE, 4)
                scalar.activation(k_f[:, :], ps_kv[:, 0:128], AF.Copy) \
                    .then_inc(sKV, 1)
                scalar.wait_ge(sKV, 4 * t + 2)
                scalar.activation(sq_s[:, 1920:2048], k_ro[:, :], AF.Square,
                                  accum_out=kn2[:, t:t + 1]).then_inc(sSq, 1)
                scalar.wait_ge(sXn, t + 1)
                scalar.dma_start(xn_d[t * 128:(t + 1) * 128, :], xn_t[p][:, :]) \
                    .then_inc(sStX[t % 2], 16)
                scalar.wait_ge(sTrE, 8 * t + 8)
                scalar.dma_start(qt_d[t, :, :], qT_t[p][:, :]).then_inc(sStQ[t % 2], 16)
            scalar.wait_ge(sKV, 4 * NT)
            scalar.dma_start(kt_d, kT_sb[:, :]).then_inc(sSt, 16)
            scalar.dma_start(v_d, v_sb[:, :]).then_inc(sSt, 16)
            scalar.wait_ge(sSt, 32)
            for p in range(2):
                scalar.wait_ge(sStX[p], 16 * (NT // 2))
                scalar.wait_ge(sStQ[p], 16 * (NT // 2))
            scalar.wait_ge(sFin, 7)

        @block.tensor
        def _(tensor):
            tensor.wait_ge(sLd, 32)
            for t in range(NT):
                p = t % 2
                tensor.wait_ge(sXn, t + 1)
                for g in range(4):
                    H = 8 * t + g
                    if H >= 2:
                        tensor.wait_ge(sTrE, H - 1)
                    mm = None
                    for k in range(4):
                        c = 4 * g + k
                        mm = tensor.matmul(ps_tr[g % 2][:, k * 128:(k + 1) * 128],
                                           xn_t[p][:, c * 128:(c + 1) * 128],
                                           ident[:, :], start=True, stop=True)
                    mm.then_inc(sTr, 1)
                if t == 0:
                    tensor.wait_ge(sWcv, NC_CHUNKS)
                tensor.wait_ge(sTrE, 8 * t + 4)
                if t > 0:
                    tensor.wait_ge(sQE, 8 * (t - 1) + 4)
                    tensor.wait_ge(sKV, 4 * (t - 1) + 3)
                for c in range(NC_CHUNKS):
                    lhs = xnT_t[p][:, c * 128:(c + 1) * 128]
                    for qq in range(4):
                        tensor.matmul(ps_q[:, qq * 512:(qq + 1) * 512], lhs,
                                      wq_bf[:, c * D + qq * 512:c * D + qq * 512 + 512],
                                      start=(c == 0), stop=(c == NC_CHUNKS - 1)) \
                            .then_inc(sMM, 1)
                    tensor.matmul(ps_kv[:, :], lhs, wkv_bf[:, c * 256:(c + 1) * 256],
                                  start=(c == 0), stop=(c == NC_CHUNKS - 1)) \
                        .then_inc(sMM, 1)
                tensor.wait_ge(sRo, 10 * t + 4)
                for g in range(4):
                    H = 8 * t + 4 + g
                    tensor.wait_ge(sTrE, H - 1)
                    mm = None
                    for k in range(4):
                        c = 4 * g + k
                        mm = tensor.matmul(ps_tr[g % 2][:, k * 128:(k + 1) * 128],
                                           q_ro[:, c * 128:(c + 1) * 128],
                                           ident[:, :], start=True, stop=True)
                    mm.then_inc(sTr, 1)
                tensor.wait_ge(sKV, 4 * t + 2)
                tensor.matmul(ps_b[:, :], k_ro[:, :], ident[:, :],
                              start=True, stop=True).then_inc(sKT, 1)
            tensor.wait_ge(sFin, 2)
            tensor.matmul(ps_b[0:1, 0:128], k2bf[:, :], ident[:, :],
                          start=True, stop=True).then_inc(sFin, 1)
            tensor.wait_ge(sFin, 5)
            tensor.matmul(ps_b[:, 0:1], ones[0:1, :], k2bf[0:1, 0:1],
                          start=True, stop=True).then_inc(sFin, 1)




def _passA(nc, outer_es, env):
    xn_d, int_d = env["xn_d"], env["int_d"]
    gwp_d, gsc_d, uwp_d, usc_d = env["gwp_d"], env["gsc_d"], env["uwp_d"], env["usc_d"]
    ident = env["ident"]
    NCH = PA_NCH if PA_NCH is not None else NMLP // 128

    with ExitStack() as es:
        def sb(name, shape, dt):
            return es.enter_context(nc.sbuf_tensor(name, shape, dt))

        def sem(name):
            return es.enter_context(nc.semaphore(name))

        xnT = sb("pa_xnT", [128, NC_CHUNKS * S_LEN], BF16)
        pg = [sb(f"pa_pg{p}", [128, 1024], I32) for p in range(2)]
        pu = [sb(f"pa_pu{p}", [128, 1024], I32) for p in range(2)]
        sg = [sb(f"pa_sg{p}", [128, 64], F32) for p in range(2)]
        su = [sb(f"pa_su{p}", [128, 64], F32) for p in range(2)]
        nib_i = sb("pa_nibi", [128, 1024], I32)
        halfn = [sb(f"pa_hn{p}", [128, 1024], BF16) for p in range(2)]
        m4 = sb("pa_m4", [128, 1024], BF16)
        u_ = [sb(f"pa_u{p}", [128, 1024], BF16) for p in range(2)]
        sgn = [sb(f"pa_sgn{p}", [128, 1024], BF16) for p in range(2)]
        b1 = [sb(f"pa_b1{p}", [128, 1024], BF16) for p in range(2)]
        b2 = [sb(f"pa_b2{p}", [128, 1024], BF16) for p in range(2)]
        wdq = [sb(f"pa_wdq{p}", [128, D], BF16) for p in range(2)]
        wT = {"g": [sb(f"pa_wTg{p}", [128, D], BF16) for p in range(2)],
              "u": [sb(f"pa_wTu{p}", [128, D], BF16) for p in range(2)]}
        xn_l = [wT["g"][0], wT["g"][1]]
        gel = [sb(f"pa_gel{p}", [128, 512], F32) for p in range(2)]
        int_sb = [sb(f"pa_int{p}", [128, 512], BF16) for p in range(2)]
        cm2 = sb("pa_cm2", [128, 1], F32)
        cm3 = sb("pa_cm3", [128, 1], F32)
        cm375 = sb("pa_cm375", [128, 1], F32)

        ps_tr = [es.enter_context(nc.psum_tensor(f"pa_pstr{p}", [128, 512], F32))
                 for p in range(2)]
        ps_g = [es.enter_context(nc.psum_tensor(f"pa_psg{p}", [128, 512], F32))
                for p in range(2)]
        ps_u = [es.enter_context(nc.psum_tensor(f"pa_psu{p}", [128, 512], F32))
                for p in range(2)]

        sLdA, sLdW2 = sem("pa_lda"), sem("pa_ldw")
        xnL = [sem(f"pa_xnl{p}") for p in range(2)]
        pkL = [sem(f"pa_pkl{p}") for p in range(2)]
        sStI = [sem(f"pa_sti{p}") for p in range(2)]
        sTr, sTrE = sem("pa_tr"), sem("pa_tre")
        sDqV1, sDqP, sDqS, sDqV2 = (sem("pa_dqv1"), sem("pa_dqp"),
                                    sem("pa_dqs"), sem("pa_dqv2"))
        sDqS1 = sem("pa_dqs1")
        sMMA, sGel, sInt, sStA = sem("pa_mma"), sem("pa_gel"), sem("pa_int"), sem("pa_sta")

        block = es.enter_context(nc.Block())

        def v1_emit(vector, c, wi, hp):
            u = 4 * c + 2 * wi + hp
            wp = (pg if wi == 0 else pu)[c % 2]
            if u >= 2:
                vector.wait_ge(sDqP, u - 1)
            if hp == 0:
                vector.wait_ge(pkL[c % 2], 64 * (c // 2 + 1))
            lo = nib_i[:, :].rearrange("q (k two) -> q k two", two=2)[:, :, 0:1]
            hi = nib_i[:, :].rearrange("q (k two) -> q k two", two=2)[:, :, 1:2]
            srcp = wp[:, hp * 512:(hp + 1) * 512].unsqueeze(2)
            vector.tensor_scalar(lo, srcp, 15, None, ALU.bitwise_and)
            vector.tensor_scalar(hi, srcp, 4, 15, ALU.logical_shift_right,
                                 ALU.bitwise_and)
            vector.tensor_scalar(halfn[u % 2][:, :], nib_i[:, :], 0.5, None,
                                 ALU.mult).then_inc(sDqV1, 1)

        def v2_emit(vector, c, wi, hp):
            u = 4 * c + 2 * wi + hp
            cw = 2 * c + wi
            sc = (sg if wi == 0 else su)[c % 2]
            p = u % 2
            vector.wait_ge(sDqS, u + 1)
            if cw >= 2 and PA_STAGE >= 2:
                vector.wait_ge(sTr, 128 + 4 * (cw - 2) + 4)
            vector.scalar_tensor_tensor(b2[p][:, :], b2[p][:, :], -2.0, b1[p][:, :],
                                        ALU.mult, ALU.subtract)
            vector.tensor_sub(u_[p][:, :], b2[p][:, :], u_[p][:, :])
            vector.tensor_mul(sgn[p][:, :], u_[p][:, :], sgn[p][:, :])
            scb = sc[:, hp * 32:(hp + 1) * 32].unsqueeze(2).broadcast_to(
                [128, 32, 32])
            vector.tensor_tensor(
                wdq[cw % 2][:, hp * 1024:(hp + 1) * 1024].rearrange(
                    "q (k c2) -> q k c2", c2=32),
                sgn[p][:, :].rearrange("q (k c2) -> q k c2", c2=32),
                scb, ALU.mult).then_inc(sDqV2, 1)

        @block.sync
        def _(sync):
            for t in range(NT):
                if t >= 2:
                    sync.wait_ge(sTr, 4 * (t - 1))
                sync.dma_start(xn_l[t % 2][:, :], xn_d[t * 128:(t + 1) * 128, :]) \
                    .then_inc(xnL[t % 2], 16)
            for c in range(NCH if PA_STAGE >= 1 else 0):
                if c >= 2:
                    sync.wait_ge(sDqV1, 4 * (c - 1))
                    sync.wait_ge(sDqV2, 4 * (c - 1))
                sync.dma_start(pg[c % 2][:, :], gwp_d[c * 128:(c + 1) * 128, :]) \
                    .then_inc(pkL[c % 2], 16)
                sync.dma_start(sg[c % 2][:, :], gsc_d[c * 128:(c + 1) * 128, :]) \
                    .then_inc(pkL[c % 2], 16)
                sync.dma_start(pu[c % 2][:, :], uwp_d[c * 128:(c + 1) * 128, :]) \
                    .then_inc(pkL[c % 2], 16)
                sync.dma_start(su[c % 2][:, :], usc_d[c * 128:(c + 1) * 128, :]) \
                    .then_inc(pkL[c % 2], 16)

        @block.vector
        def _(vector):
            vector.memset(cm2[:, :], -2.0)
            vector.memset(cm3[:, :], -3.0)
            vector.memset(cm375[:, :], -3.75)
            for t in range(NT):
                for g in range(4):
                    GG = 4 * t + g
                    vector.wait_ge(sTr, GG + 1)
                    vector.tensor_copy(
                        xnT[:, :].rearrange("q (ch n) -> q ch n", n=S_LEN)
                        [:, 4 * g:4 * g + 4, t * 128:(t + 1) * 128],
                        ps_tr[GG % 2][:, :].rearrange("q (k n) -> q k n", n=128)) \
                        .then_inc(sTrE, 1)
            units = [(c, wi, hp) for c in range(NCH) for wi in range(2)
                     for hp in range(2)]
            def units_blk(vector, cc):
                for k in range(4):
                    uu = 4 * cc + k
                    v1_emit(vector, *units[uu])
                    if k >= 2:
                        v2_emit(vector, *units[uu - 2])
                v2_emit(vector, *units[4 * cc + 2])
                v2_emit(vector, *units[4 * cc + 3])

            def evacs_blk(vector, cc):
                for wi in range(2):
                    cw = 2 * cc + wi
                    for g in range(4):
                        GG = 128 + 4 * cw + g
                        vector.wait_ge(sTr, GG + 1)
                        if cc >= 2 and PA_STAGE in (3, 5, 6):
                            vector.wait_ge(sMMA, 16 * (cc - 1))
                        vector.tensor_copy(
                            wT[("g", "u")[wi]][cc % 2][:, :].rearrange(
                                "q (k n) -> q k n", n=128)[:, 4 * g:4 * g + 4, :],
                            ps_tr[GG % 2][:, :].rearrange(
                                "q (k n) -> q k n", n=128)).then_inc(sTrE, 1)

            def inter_blk(vector, cm, tcs):
                for tc in tcs:
                    G2 = 8 * cm + tc
                    vector.wait_ge(sGel, G2 + 1)
                    vector.wait_ge(sMMA, 16 * cm + 2 * tc + 2)
                    if G2 >= 2:
                        vector.wait_ge(sStI[tc % 2], 16 * (G2 // 2))
                    vector.tensor_tensor(int_sb[tc % 2][:, :], gel[tc % 2][:, :],
                                         ps_u[tc % 2][:, :], ALU.mult) \
                        .then_inc(sInt, 1)

            if PA_STAGE >= 1:
                units_blk(vector, 0)
                if PA_STAGE >= 2:
                    evacs_blk(vector, 0)
                for c in range(NCH):
                    if c + 1 < NCH:
                        units_blk(vector, c + 1)
                    if PA_STAGE == 3:
                        inter_blk(vector, c, range(0, 4))
                    if PA_STAGE >= 2 and c + 1 < NCH:
                        evacs_blk(vector, c + 1)
                    if PA_STAGE == 3:
                        inter_blk(vector, c, range(4, 8))

        @block.gpsimd
        def _(gpsimd):
            for u in range(4 * NCH if PA_STAGE >= 1 else 0):
                p = u % 2
                gpsimd.wait_ge(sDqS1, u + 1)
                if u >= 2:
                    gpsimd.wait_ge(sDqV2, u - 1)
                gpsimd.tensor_scalar(m4[:, :], sgn[p][:, :], -2.0, -2.0,
                                     ALU.mult, ALU.add)
                gpsimd.tensor_add(u_[p][:, :], halfn[p][:, :], m4[:, :]) \
                    .then_inc(sDqP, 1)

        @block.scalar
        def _(scalar):
            def _sign(scalar, u):
                p = u % 2
                scalar.wait_ge(sDqV1, u + 1)
                if u >= 2:
                    scalar.wait_ge(sDqV2, u - 1)
                scalar.activation(sgn[p][:, :], halfn[p][:, :], AF.Sign,
                                  bias=cm375[:, :]).then_inc(sDqS1, 1)

            def _relus(scalar, u):
                p = u % 2
                scalar.wait_ge(sDqP, u + 1)
                if u >= 2:
                    scalar.wait_ge(sDqV2, u - 1)
                scalar.activation(b1[p][:, :], u_[p][:, :], AF.Relu,
                                  bias=cm2[:, :])
                scalar.activation(b2[p][:, :], u_[p][:, :], AF.Relu,
                                  bias=cm3[:, :]).then_inc(sDqS, 1)

            def relus_blk(scalar, cc):
                for k in range(4):
                    u = 4 * cc + k
                    _sign(scalar, u)
                    if k >= 1:
                        _relus(scalar, u - 1)
                _relus(scalar, 4 * cc + 3)

            if PA_STAGE >= 1:
                relus_blk(scalar, 0)
            for c in range(NCH if PA_STAGE >= 1 else 0):
                if c + 1 < NCH:
                    relus_blk(scalar, c + 1)
                for tc in range(8 if PA_STAGE in (3, 6) else 0):
                    G2 = 8 * c + tc
                    scalar.wait_ge(sMMA, 16 * c + 2 * tc + 1)
                    if G2 >= 2:
                        scalar.wait_ge(sInt, G2 - 1)
                    scalar.activation(gel[tc % 2][:, :], ps_g[tc % 2][:, :],
                                      AF.Gelu_apprx_tanh).then_inc(sGel, 1)
                    if tc >= 1 and PA_STAGE == 3:
                        scalar.wait_ge(sInt, G2)
                        scalar.dma_start(int_d[tc - 1, c, :, :],
                                         int_sb[(tc - 1) % 2][:, :]) \
                            .then_inc(sStI[(tc - 1) % 2], 16)
                if PA_STAGE == 3:
                    scalar.wait_ge(sInt, 8 * c + 8)
                    scalar.dma_start(int_d[7, c, :, :], int_sb[1][:, :]) \
                        .then_inc(sStI[1], 16)
            if PA_STAGE == 3:
                for pp in range(2):
                    scalar.wait_ge(sStI[pp], 16 * 4 * NCH)

        @block.tensor
        def _(tensor):
            for t in range(NT):
                tensor.wait_ge(xnL[t % 2], 16 * (t // 2 + 1))
                for g in range(4):
                    GG = 4 * t + g
                    if GG >= 2:
                        tensor.wait_ge(sTrE, GG - 1)
                    mm = None
                    for k in range(4):
                        ch = 4 * g + k
                        mm = tensor.matmul(
                            ps_tr[GG % 2][:, k * 128:(k + 1) * 128],
                            xn_l[t % 2][:, ch * 128:(ch + 1) * 128],
                            ident[:, :], start=True, stop=True)
                    mm.then_inc(sTr, 1)
            def trans_blk(tensor, cc):
                for wi in range(2):
                    cw = 2 * cc + wi
                    tensor.wait_ge(sDqV2, 4 * cc + 2 * wi + 2)
                    for g in range(4):
                        GG = 128 + 4 * cw + g
                        tensor.wait_ge(sTrE, GG - 1)
                        mm = None
                        for k in range(4):
                            ch = 4 * g + k
                            mm = tensor.matmul(
                                ps_tr[GG % 2][:, k * 128:(k + 1) * 128],
                                wdq[cw % 2][:, ch * 128:(ch + 1) * 128],
                                ident[:, :], start=True, stop=True)
                        mm.then_inc(sTr, 1)

            def mms_blk(tensor, c, tcs):
                for tc in tcs:
                    G2 = 8 * c + tc
                    if G2 >= 2 and PA_STAGE != 5:
                        tensor.wait_ge(sGel, G2 - 1)
                        if PA_STAGE != 6:
                            tensor.wait_ge(sInt, G2 - 1)
                    mm = None
                    for ch in range(NC_CHUNKS):
                        mm = tensor.matmul(
                            ps_g[tc % 2][:, :],
                            wT["g"][c % 2][:, ch * 128:(ch + 1) * 128],
                            xnT[:, ch * S_LEN + tc * 512:
                                ch * S_LEN + tc * 512 + 512],
                            start=(ch == 0), stop=(ch == NC_CHUNKS - 1))
                    mm.then_inc(sMMA, 1)
                    mm = None
                    for ch in range(NC_CHUNKS):
                        mm = tensor.matmul(
                            ps_u[tc % 2][:, :],
                            wT["u"][c % 2][:, ch * 128:(ch + 1) * 128],
                            xnT[:, ch * S_LEN + tc * 512:
                                ch * S_LEN + tc * 512 + 512],
                            start=(ch == 0), stop=(ch == NC_CHUNKS - 1))
                    mm.then_inc(sMMA, 1)

            if PA_STAGE >= 2:
                trans_blk(tensor, 0)
                for c in range(NCH):
                    tensor.wait_ge(sTrE, 128 + 8 * c + 8)
                    if PA_STAGE in (3, 5, 6):
                        mms_blk(tensor, c, range(0, 4))
                    if c + 1 < NCH:
                        trans_blk(tensor, c + 1)
                    if PA_STAGE in (3, 5, 6):
                        mms_blk(tensor, c, range(4, 8))


def _attn(nc, outer_es, env):
    qt_d, kt_d, v_d, s_d = env["qt_d"], env["kt_d"], env["v_d"], env["s_d"]
    wout_d, mask_d, dbg_d = env["wout_d"], env["mask_d"], env["dbg_d"]
    dbg2_d = env["dbg2_d"]
    ident, qn2, k2m_b = env["ident"], env["qn2"], env["k2m_b"]

    NG = NT // 4
    NU = NG * HEADS
    qk_per = [4 * (u // 16 + 1) for u in range(NU)]
    qk_base = [0]
    for u in range(NU):
        qk_base.append(qk_base[-1] + qk_per[u])
    pt_per = qk_per
    pt_base = qk_base

    with ExitStack() as es:
        def sb(name, shape, dt):
            return es.enter_context(nc.sbuf_tensor(name, shape, dt))

        def sem(name):
            return es.enter_context(nc.semaphore(name))

        kT = sb("at_kT", [128, S_LEN], BF16)
        v_sb = sb("at_v", [128, S_LEN], BF16)
        wout = sb("at_wout", [128, NC_CHUNKS * D], BF16)
        qT = [sb(f"at_qT{i}", [128, D], BF16) for i in range(4)]
        p_row = [sb(f"at_p{i}", [128, S_LEN], BF16) for i in range(4)]
        pT_sb = [sb(f"at_pT{p}", [128, 512], BF16) for p in range(2)]
        oT_sb = [sb(f"at_oT{p}", [128, HEADS * 512], BF16) for p in range(2)]
        diag = [[sb(f"at_dg{p}_{i}", [128, 128], BF16) for i in range(4)]
                for p in range(2)]
        masks = sb("at_mask", [128, 4 * 512], BF16)
        msneg = sb("at_msneg", [128, NT * HEADS], F32)
        denp = sb("at_denp", [128, 32], F32)
        den4 = sb("at_den4", [128, 4], F32)
        r4 = sb("at_r4", [128, 4], F32)
        s_t = [sb(f"at_s{p}", [128, D], F32) for p in range(2)]
        dbg2 = sb("at_dbg2", [128, 512], F32) if DEBUG else None
        wout_f = [s_t[0], s_t[1]]

        ps_qk = [es.enter_context(nc.psum_tensor(f"at_psqk{p}", [128, 512], F32))
                 for p in range(2)]
        ps_pt = [es.enter_context(nc.psum_tensor(f"at_pspt{p}", [128, 512], F32))
                 for p in range(2)]
        ps_oT = [es.enter_context(nc.psum_tensor(f"at_psot{p}", [128, 512], F32))
                 for p in range(2)]
        ps_s = es.enter_context(nc.psum_tensor("at_pss", [128, 512], F32))

        sLd3, sWcv3 = sem("at_ld"), sem("at_wcv")
        sQT2 = sem("at_qt")
        sQK, sEXP = sem("at_qk"), sem("at_exp")
        sMSK, sDIAG = sem("at_msk"), sem("at_diag")
        sPT, sPTE = sem("at_pt"), sem("at_pte")
        sPV, sOTE = sem("at_pv"), sem("at_ote")
        sSP, sSST = sem("at_sp"), sem("at_sst")
        wtL = [sem(f"at_wtl{p}") for p in range(2)]
        sStS = [sem(f"at_sts{p}") for p in range(2)]
        sPRE = sem("at_pre")

        block = es.enter_context(nc.Block())

        @block.sync
        def _(sync):
            sync.dma_start(kT[:, :], kt_d).then_inc(sLd3, 16)
            sync.dma_start(v_sb[:, :], v_d).then_inc(sLd3, 16)
            sync.dma_start(masks[:, :].rearrange("q (v j) -> q v j", j=512),
                           mask_d.rearrange("(v q) j -> q v j", q=128)) \
                .then_inc(sLd3, 16)
            for c in range(NC_CHUNKS):
                if c >= 2:
                    sync.wait_ge(sWcv3, c - 1)
                sync.dma_start(wout_f[c % 2][:, :], wout_d[c * 128:(c + 1) * 128, :]) \
                    .then_inc(wtL[c % 2], 16)
            for g in range(NG):
                if g >= 1:
                    sync.wait_ge(sQK, qk_base[16 * g])
                for i in range(4):
                    sync.dma_start(qT[i][:, :], qt_d[4 * g + i, :, :]) \
                        .then_inc(sQT2, 16)

        @block.vector
        def _(vector):
            for c in range(NC_CHUNKS):
                vector.wait_ge(wtL[c % 2], 16 * (c // 2 + 1))
                vector.tensor_copy(wout[:, c * D:(c + 1) * D], wout_f[c % 2][:, :]) \
                    .then_inc(sWcv3, 1)
            vector.tensor_tensor(msneg[:, :], qn2[:, :],
                                 k2m_b[:, :].broadcast_to([128, NT * HEADS]),
                                 ALU.mult).then_inc(sPRE, 1)
            vector.wait_ge(sPRE, 2)
            vector.tensor_scalar(msneg[:, :], msneg[:, :], -1.0, -1.0,
                                 ALU.mult, ALU.add).then_inc(sPRE, 1)
            for u in range(NU):
                g, h = u // 16, u % 16
                J = g + 1
                for i in range(4):
                    qk_idx = qk_base[u] + i * J + (J - 1)
                    vector.wait_ge(sEXP, qk_idx + 1)
                    vector.scalar_tensor_tensor(
                        p_row[i][:, g * 512:(g + 1) * 512],
                        p_row[i][:, g * 512:(g + 1) * 512], 1.0,
                        masks[:, i * 512:(i + 1) * 512],
                        ALU.mult, ALU.mult,
                        accum_out=denp[:, i * 8 + (J - 1):i * 8 + J]) \
                        .then_inc(sMSK, 1)
                vector.wait_ge(sEXP, qk_base[u + 1])
                vector.wait_ge(sMSK, 4 * (u + 1))
                vector.tensor_reduce(
                    den4[:, :].unsqueeze(2),
                    denp[:, :].rearrange("q (i j) -> q i j", j=8)[:, :, 0:J],
                    AX.X, ALU.add).then_inc(sDIAG, 1)
                vector.wait_ge(sDIAG, 3 * u + 1)
                if DEBUG:
                    vector.tensor_copy(dbg2[:, 4 * u:4 * u + 4], den4[:, :])
                vector.reciprocal(r4[:, :], den4[:, :]).then_inc(sDIAG, 1)
                vector.wait_ge(sDIAG, 3 * u + 2)
                for i in range(4):
                    ins = vector.tensor_scalar(diag[u % 2][i][:, :], ident[:, :],
                                               r4[:, i:i + 1], None, ALU.mult)
                ins.then_inc(sDIAG, 1)
                for jj in range(4 * J):
                    pt_idx = pt_base[u] + jj
                    vector.wait_ge(sPT, pt_idx + 1)
                    vector.tensor_copy(pT_sb[jj % 2][:, :], ps_pt[jj % 2][:, :]) \
                        .then_inc(sPTE, 1)
                vector.wait_ge(sPV, u + 1)
                if g >= 2:
                    vector.wait_ge(sSP, 16 * (g - 1))
                vector.tensor_copy(oT_sb[g % 2][:, h * 512:(h + 1) * 512],
                                   ps_oT[u % 2][:, :]).then_inc(sOTE, 1)

        @block.scalar
        def _(scalar):
            scalar.wait_ge(sPRE, 1)
            scalar.activation(msneg[:, :], msneg[:, :], AF.Sqrt).then_inc(sPRE, 1)
            if DEBUG:
                scalar.wait_ge(sPRE, 3)
                scalar.dma_start(dbg_d[:, 0:NT * HEADS], msneg[:, :]).then_inc(sPRE, 16)
                scalar.dma_start(dbg_d[:, NT * HEADS:2 * NT * HEADS], qn2[:, :]) \
                    .then_inc(sPRE, 16)
                scalar.dma_start(dbg_d[:, 2 * NT * HEADS:], k2m_b[:, :],
                                 allow_slow_non_contiguous=True).then_inc(sPRE, 16)
                scalar.wait_ge(sPRE, 51)
            for u in range(NU):
                g, h = u // 16, u % 16
                J = g + 1
                if u >= 1:
                    scalar.wait_ge(sPT, pt_base[u])
                    scalar.wait_ge(sDIAG, 3 * u)
                for i in range(4):
                    for jt in range(J):
                        qk_idx = qk_base[u] + i * J + jt
                        scalar.wait_ge(sQK, qk_idx + 1)
                        col = (4 * g + i) * HEADS + h
                        if jt < J - 1:
                            scalar.activation(
                                p_row[i][:, jt * 512:(jt + 1) * 512],
                                ps_qk[qk_idx % 2][:, :], AF.Exp,
                                bias=msneg[:, col:col + 1],
                                accum_out=denp[:, i * 8 + jt:i * 8 + jt + 1]) \
                                .then_inc(sEXP, 1)
                        else:
                            scalar.activation(
                                p_row[i][:, jt * 512:(jt + 1) * 512],
                                ps_qk[qk_idx % 2][:, :], AF.Exp,
                                bias=msneg[:, col:col + 1]).then_inc(sEXP, 1)
                if h == 15:
                    for i in range(4):
                        it = 4 * g + i
                        if it >= 2:
                            scalar.wait_ge(sStS[it % 2], 16 * (it // 2))
                        for dq in range(4):
                            sp_idx = 16 * g + 4 * i + dq
                            scalar.wait_ge(sSP, sp_idx + 1)
                            scalar.activation(s_t[i % 2][:, dq * 512:(dq + 1) * 512],
                                              ps_s[:, :], AF.Copy).then_inc(sSST, 1)
                        scalar.dma_start(s_d[it, :, :], s_t[i % 2][:, :]) \
                            .then_inc(sStS[it % 2], 16)
                    if g == NG - 1:
                        for pp in range(2):
                            scalar.wait_ge(sStS[pp], 16 * (NT // 2))
            if DEBUG and u == NU - 1:
                scalar.wait_ge(sDIAG, 3 * NU)
                scalar.dma_start(dbg2_d, dbg2[:, :]).then_inc(sPRE, 16)
                scalar.wait_ge(sPRE, 67)


        @block.tensor
        def _(tensor):
            tensor.wait_ge(sLd3, 48)
            for u in range(NU):
                g, h = u // 16, u % 16
                J = g + 1
                if u == 0:
                    tensor.wait_ge(sQT2, 64)
                elif h == 0:
                    tensor.wait_ge(sQT2, 64 * (g + 1))
                for i in range(4):
                    lhs = qT[i][:, h * 128:(h + 1) * 128]
                    for jt in range(J):
                        qk_idx = qk_base[u] + i * J + jt
                        if qk_idx >= 2:
                            tensor.wait_ge(sEXP, qk_idx - 1)
                        tensor.matmul(ps_qk[qk_idx % 2][:, :], lhs,
                                      kT[:, jt * 512:(jt + 1) * 512],
                                      start=True, stop=True).then_inc(sQK, 1)
                tensor.wait_ge(sDIAG, 3 * u + 3)
                for jj in range(4 * J):
                    pt_idx = pt_base[u] + jj
                    if pt_idx >= 2:
                        tensor.wait_ge(sPTE, pt_idx - 1)
                    mm = None
                    for i in range(4):
                        mm = tensor.matmul(ps_pt[jj % 2][:, i * 128:(i + 1) * 128],
                                           p_row[i][:, jj * 128:(jj + 1) * 128],
                                           diag[u % 2][i][:, :],
                                           start=True, stop=True)
                    mm.then_inc(sPT, 1)
                    tensor.wait_ge(sPTE, pt_idx + 1)
                    if jj == 0 and u >= 2:
                        tensor.wait_ge(sOTE, u - 1)
                    pv = tensor.matmul(ps_oT[u % 2][:, :],
                                       v_sb[:, jj * 128:(jj + 1) * 128],
                                       pT_sb[jj % 2][:, :],
                                       start=(jj == 0), stop=(jj == 4 * J - 1))
                    if jj == 4 * J - 1:
                        pv.then_inc(sPV, 1)
                if h == 15:
                    tensor.wait_ge(sOTE, 16 * (g + 1))
                    for i in range(4):
                        for dq in range(4):
                            sp_idx = 16 * g + 4 * i + dq
                            if sp_idx >= 1:
                                tensor.wait_ge(sSST, sp_idx)
                            mm = None
                            for e in range(NC_CHUNKS):
                                mm = tensor.matmul(
                                    ps_s[:, :],
                                    oT_sb[g % 2][:, e * 512 + i * 128:
                                                  e * 512 + i * 128 + 128],
                                    wout[:, e * D + dq * 512:e * D + dq * 512 + 512],
                                    start=(e == 0), stop=(e == NC_CHUNKS - 1))
                            mm.then_inc(sSP, 1)


def _sst_before(sp_idx):
    k, q = divmod(sp_idx, 4)
    return 20 * k + q


def _wd(nc, outer_es, env):
    dwp_d, dsc_d, wdt_d = env["dwp_d"], env["dsc_d"], env["wdt_d"]
    ident = env["ident"]

    with ExitStack() as es:
        def sb(name, shape, dt):
            return es.enter_context(nc.sbuf_tensor(name, shape, dt))

        def sem(name):
            return es.enter_context(nc.semaphore(name))

        pk = [sb(f"wd_pk{p}", [128, 4 * 512], I32) for p in range(2)]
        sc = [sb(f"wd_sc{p}", [128, 4 * 32], F32) for p in range(2)]
        nib_i = sb("wd_nibi", [128, 1024], I32)
        halfn = [sb(f"wd_hn{p}", [128, 1024], BF16) for p in range(2)]
        m4 = sb("wd_m4", [128, 1024], BF16)
        u_ = [sb(f"wd_u{p}", [128, 1024], BF16) for p in range(2)]
        sgn = [sb(f"wd_sgn{p}", [128, 1024], BF16) for p in range(2)]
        b1 = [sb(f"wd_b1{p}", [128, 1024], BF16) for p in range(2)]
        b2 = [sb(f"wd_b2{p}", [128, 1024], BF16) for p in range(2)]
        wdq = [sb(f"wd_wdq{s}", [128, 1024], BF16) for s in range(4)]
        stg = [sb(f"wd_stg{p}", [128, 512], BF16) for p in range(8)]
        cm2 = sb("wd_cm2", [128, 1], F32)
        cm3 = sb("wd_cm3", [128, 1], F32)
        cm375 = sb("wd_cm375", [128, 1], F32)
        ps_tr = [es.enter_context(nc.psum_tensor(f"wd_pstr{p}", [128, 512], F32))
                 for p in range(2)]

        sLd = sem("wd_ld")
        pkwL = [sem(f"wd_pkl{p}") for p in range(2)]
        sStW = [sem(f"wd_stw{k}") for k in range(8)]
        sV1, sP, sS, sV2 = sem("wd_v1"), sem("wd_p"), sem("wd_s"), sem("wd_v2")
        sS1 = sem("wd_s1")
        sTr, sTrE, sSt = sem("wd_tr"), sem("wd_tre"), sem("wd_st")

        block = es.enter_context(nc.Block())
        NIT = 64

        @block.sync
        def _(sync):
            for it in range(NIT):
                o, r = it // 4, it % 4
                if it >= 2:
                    sync.wait_ge(sV1, 4 * (it - 1))
                    sync.wait_ge(sV2, 4 * (it - 1))
                sync.dma_start(
                    pk[it % 2][:, :].rearrange("p (s c) -> p s c", c=512),
                    dwp_d[512 * r:512 * (r + 1), 512 * o:512 * (o + 1)]
                    .rearrange("(s p) c -> p s c", p=128)).then_inc(pkwL[it % 2], 16)
                sync.dma_start(
                    sc[it % 2][:, :].rearrange("p (s c) -> p s c", c=32),
                    dsc_d[512 * r:512 * (r + 1), 32 * o:32 * (o + 1)]
                    .rearrange("(s p) c -> p s c", p=128)).then_inc(pkwL[it % 2], 16)

        @block.vector
        def _(vector):
            vector.memset(cm2[:, :], -2.0)
            vector.memset(cm3[:, :], -3.0)
            vector.memset(cm375[:, :], -3.75)
            for it in range(NIT):
                vector.wait_ge(pkwL[it % 2], 32 * (it // 2 + 1))
                for s in range(4):
                    u = 4 * it + s
                    p = u % 2
                    if u >= 2:
                        vector.wait_ge(sP, u - 1)
                    lo = nib_i[:, :].rearrange("q (k two) -> q k two", two=2)[:, :, 0:1]
                    hi = nib_i[:, :].rearrange("q (k two) -> q k two", two=2)[:, :, 1:2]
                    srcp = pk[it % 2][:, s * 512:(s + 1) * 512].unsqueeze(2)
                    vector.tensor_scalar(lo, srcp, 15, None, ALU.bitwise_and)
                    vector.tensor_scalar(hi, srcp, 4, 15, ALU.logical_shift_right,
                                         ALU.bitwise_and)
                    vector.tensor_scalar(halfn[p][:, :], nib_i[:, :], 0.5, None,
                                         ALU.mult).then_inc(sV1, 1)
                    if s >= 1:
                        _wd_v2(vector, u - 1, sV2, sc, wdq,
                               u_, sgn, b1, b2, sS, sTr, ALU)
                _wd_v2(vector, 4 * it + 3, sV2, sc, wdq, u_, sgn, b1, b2, sS,
                       sTr, ALU)
                for c8 in range(8):
                    GG = 8 * it + c8
                    vector.wait_ge(sTr, GG + 1)
                    if it >= 1:
                        vector.wait_ge(sStW[c8], 16 * it)
                    vector.tensor_copy(stg[c8][:, :], ps_tr[c8 % 2][:, :]) \
                        .then_inc(sTrE, 1)

        @block.gpsimd
        def _(gpsimd):
            for u in range(4 * NIT):
                p = u % 2
                gpsimd.wait_ge(sS1, u + 1)
                if u >= 2:
                    gpsimd.wait_ge(sV2, u - 1)
                gpsimd.tensor_scalar(m4[:, :], sgn[p][:, :], -2.0, -2.0,
                                     ALU.mult, ALU.add)
                gpsimd.tensor_add(u_[p][:, :], halfn[p][:, :], m4[:, :]) \
                    .then_inc(sP, 1)

        @block.scalar
        def _(scalar):
            def _sgn(scalar, u):
                p = u % 2
                scalar.wait_ge(sV1, u + 1)
                if u >= 2:
                    scalar.wait_ge(sV2, u - 1)
                scalar.activation(sgn[p][:, :], halfn[p][:, :], AF.Sign,
                                  bias=cm375[:, :]).then_inc(sS1, 1)

            def _rel(scalar, u):
                p = u % 2
                scalar.wait_ge(sP, u + 1)
                if u >= 2:
                    scalar.wait_ge(sV2, u - 1)
                scalar.activation(b1[p][:, :], u_[p][:, :], AF.Relu,
                                  bias=cm2[:, :])
                scalar.activation(b2[p][:, :], u_[p][:, :], AF.Relu,
                                  bias=cm3[:, :]).then_inc(sS, 1)

            def relus(scalar, it):
                for s in range(4):
                    u = 4 * it + s
                    _sgn(scalar, u)
                    if s >= 1:
                        _rel(scalar, u - 1)
                _rel(scalar, 4 * it + 3)

            relus(scalar, 0)
            for it in range(NIT):
                if it + 1 < NIT:
                    relus(scalar, it + 1)
                o, r = it // 4, it % 4
                for c8 in range(8):
                    GG = 8 * it + c8
                    scalar.wait_ge(sTrE, GG + 1)
                    scalar.dma_start(wdt_d[8 * o + c8, r, :, :], stg[c8][:, :]) \
                        .then_inc(sStW[c8], 16)
            for k in range(8):
                scalar.wait_ge(sStW[k], 16 * NIT)

        @block.tensor
        def _(tensor):
            for it in range(NIT):
                tensor.wait_ge(sV2, 4 * (it + 1))
                for c8 in range(8):
                    GG = 8 * it + c8
                    if GG >= 2:
                        tensor.wait_ge(sTrE, GG - 1)
                    mm = None
                    for s in range(4):
                        mm = tensor.matmul(
                            ps_tr[c8 % 2][:, s * 128:(s + 1) * 128],
                            wdq[s][:, c8 * 128:(c8 + 1) * 128],
                            ident[:, :], start=True, stop=True)
                    mm.then_inc(sTr, 1)


def _wd_v2(vector, u, sV2, sc, wdq, u_, sgn, b1, b2, sS, sTr, _ALU):
    s = u % 4
    p = u % 2
    vector.wait_ge(sS, u + 1)
    if u >= 4:
        vector.wait_ge(sTr, 8 * (u // 4 - 1) + 8)
    vector.scalar_tensor_tensor(b2[p][:, :], b2[p][:, :], -2.0, b1[p][:, :],
                                _ALU.mult, _ALU.subtract)
    vector.tensor_sub(u_[p][:, :], b2[p][:, :], u_[p][:, :])
    vector.tensor_mul(sgn[p][:, :], u_[p][:, :], sgn[p][:, :])
    scb = sc[(u // 4) % 2][:, s * 32:(s + 1) * 32].unsqueeze(2).broadcast_to(
        [128, 32, 32])
    vector.tensor_tensor(wdq[s][:, :].rearrange("q (k c2) -> q k c2", c2=32),
                         sgn[p][:, :].rearrange("q (k c2) -> q k c2", c2=32),
                         scb, _ALU.mult).then_inc(sV2, 1)



def _passB(nc, outer_es, env):
    x_d, s_d, out_d = env["x_d"], env["s_d"], env["out_d"]
    int_d, wdt_d = env["int_d"], env["wdt_d"]
    dbg3_d = env["dbg3_d"]

    with ExitStack() as es:
        def sb(name, shape, dt):
            return es.enter_context(nc.sbuf_tensor(name, shape, dt))

        def sem(name):
            return es.enter_context(nc.semaphore(name))

        intT = sb("pb_int", [128, 128 * 512], BF16)
        wdt = [sb(f"pb_wdt{p}", [128, 512], BF16) for p in range(4)]
        g_sb = [sb(f"pb_g{s}", [128, D], F32) for s in range(4)]
        x_t = [sb(f"pb_x{p}", [128, D], F32) for p in range(2)]
        s_t = [sb(f"pb_s{p}", [128, D], F32) for p in range(2)]
        ps_d = [es.enter_context(nc.psum_tensor(f"pb_ps{k}", [128, 512], F32))
                for k in range(8)]

        sLdI, sXS = sem("pb_ldi"), sem("pb_xs")
        wdtL = [sem(f"pb_wdtl{k}") for k in range(4)]
        sMMc = sem("pb_mmc")
        sSIL, sCMB, sOUT = sem("pb_sil"), sem("pb_cmb"), sem("pb_out")
        xsL = [sem(f"pb_xsl{p}") for p in range(2)]
        sOutS = [sem(f"pb_outs{p}") for p in range(2)]
        sDBG = sem("pb_dbg")

        block = es.enter_context(nc.Block())

        @block.sync
        def _(sync):
            for k in range(8):
                sync.dma_start(
                    intT[:, :].rearrange("p (c t) -> p c t", t=512)
                    [:, 16 * k:16 * (k + 1), :],
                    int_d[0, 16 * k:16 * (k + 1), :, :].rearrange("c p t -> p c t")) \
                    .then_inc(sLdI, 16)
            for ts in range(NSUP):
                for hq in range(4):
                    Q = 4 * ts + hq
                    for c in range(128):
                        if Q * 128 + c >= 4:
                            sync.wait_ge(sMMc, Q * 128 + c - 3)
                        sync.dma_start(wdt[c % 4][:, :], wdt_d[c, hq, :, :]) \
                            .then_inc(wdtL[c % 4], 16)
                        if hq == 3 and ts + 1 < NSUP and c % 16 == 15:
                            k = c // 16
                            sync.wait_ge(sMMc, 128 * Q + 16 * (k + 1))
                            sync.dma_start(
                                intT[:, :].rearrange("p (c2 t) -> p c2 t", t=512)
                                [:, 16 * k:16 * (k + 1), :],
                                int_d[ts + 1, 16 * k:16 * (k + 1), :, :]
                                .rearrange("c2 p t -> p c2 t")).then_inc(sLdI, 16)

        @block.gpsimd
        def _(gpsimd):
            for ts in range(NSUP):
                for sub in range(4):
                    SB = 4 * ts + sub
                    if SB >= 2:
                        gpsimd.wait_ge(sCMB, SB - 1)
                        gpsimd.wait_ge(sOutS[SB % 2], 16 * (SB // 2))
                    t0 = (4 * ts + sub) * 128
                    gpsimd.dma_start(x_t[sub % 2][:, :], x_d[t0:t0 + 128, :]) \
                        .then_inc(xsL[sub % 2], 16)
                    gpsimd.dma_start(s_t[sub % 2][:, :], s_d[4 * ts + sub, :, :]) \
                        .then_inc(xsL[sub % 2], 16)

        @block.tensor
        def _(tensor):
            for ts in range(NSUP):
                for hq in range(4):
                    Q = 4 * ts + hq
                    for c in range(128):
                        n_chunk = 128 * Q + c
                        tensor.wait_ge(wdtL[c % 4], 16 * (n_chunk // 4 + 1))
                        tensor.wait_ge(sLdI, 16 * 8 * (ts + 1))
                        mm = None
                        for sub in range(4):
                            if c == 0 and Q >= 2:
                                tensor.wait_ge(sSIL, 4 * (Q - 1))
                            mm = tensor.matmul(
                                ps_d[4 * (hq % 2) + sub][:, :],
                                intT[:, c * 512 + sub * 128:c * 512 + sub * 128 + 128],
                                wdt[c % 4][:, :],
                                start=(c == 0), stop=(c == 127))
                        mm.then_inc(sMMc, 1)
            tensor.matmul(ps_d[0][0:1, 0:1], intT[:, 0:1], wdt[0][:, 0:1],
                          start=True, stop=True).then_inc(sMMc, 1)

        @block.scalar
        def _(scalar):
            for ts in range(NSUP):
                for hq in range(4):
                    Q = 4 * ts + hq
                    for sub in range(4):
                        scalar.wait_ge(sMMc, min(128 * (Q + 1) + 64,
                                                 128 * 4 * NSUP + 1))
                        if ts >= 1 and hq == 0:
                            scalar.wait_ge(sCMB, 4 * (ts - 1) + sub + 1)
                        scalar.activation(g_sb[sub][:, hq * 512:(hq + 1) * 512],
                                          ps_d[4 * (hq % 2) + sub][:, :],
                                          AF.Silu).then_inc(sSIL, 1)
                if DEBUG and ts == 0:
                    for sub in range(4):
                        scalar.wait_ge(sSIL, 12 + sub + 1)
                        scalar.dma_start(dbg3_d[sub, :, :], g_sb[sub][:, :]) \
                            .then_inc(sDBG, 16)
                    scalar.wait_ge(sDBG, 64)
                for sub in range(4):
                    SB = 4 * ts + sub
                    scalar.wait_ge(sCMB, SB + 1)
                    t0 = (4 * ts + sub) * 128
                    scalar.dma_start(out_d[t0:t0 + 128, :], s_t[sub % 2][:, :]) \
                        .then_inc(sOutS[SB % 2], 16)
            for pp in range(2):
                scalar.wait_ge(sOutS[pp], 16 * 2 * NSUP)

        @block.vector
        def _(vector):
            for ts in range(NSUP):
                for sub in range(4):
                    SB = 4 * ts + sub
                    vector.wait_ge(sSIL, 4 * (4 * ts + 3) + sub + 1)
                    vector.wait_ge(xsL[sub % 2], 32 * (SB // 2 + 1))
                    if SB >= 2:
                        vector.wait_ge(sOutS[SB % 2], 16 * (SB // 2))
                    vector.tensor_mul(s_t[sub % 2][:, :], s_t[sub % 2][:, :],
                                      g_sb[sub][:, :])
                    vector.tensor_add(s_t[sub % 2][:, :], s_t[sub % 2][:, :],
                                      x_t[sub % 2][:, :]).then_inc(sCMB, 1)



_CONSTS = None


def kernel(L0_attn_x, L0_attn_Wq, L0_attn_Wkv, L0_attn_Wout,
           L0_ffn_gate_W_packed, L0_ffn_gate_scales,
           L0_ffn_up_W_packed, L0_ffn_up_scales,
           L0_ffn_down_W_packed, L0_ffn_down_scales):
    global _CONSTS
    if _CONSTS is None:
        _CONSTS = _host_consts()
    consts = _CONSTS

    x = np.ascontiguousarray(np.asarray(L0_attn_x, dtype=np.float32))
    shared = dict(
        wq=np.ascontiguousarray(np.asarray(L0_attn_Wq, dtype=np.float32)),
        wkv=np.ascontiguousarray(np.asarray(L0_attn_Wkv, dtype=np.float32)),
        wout=np.ascontiguousarray(np.asarray(L0_attn_Wout, dtype=np.float32)),
        gwp=np.ascontiguousarray(np.asarray(L0_ffn_gate_W_packed, dtype=np.int32)),
        gsc=np.ascontiguousarray(np.asarray(L0_ffn_gate_scales, dtype=np.float32)),
        uwp=np.ascontiguousarray(np.asarray(L0_ffn_up_W_packed, dtype=np.int32)),
        usc=np.ascontiguousarray(np.asarray(L0_ffn_up_scales, dtype=np.float32)),
        dwp=np.ascontiguousarray(np.asarray(L0_ffn_down_W_packed, dtype=np.int32)),
        dsc=np.ascontiguousarray(np.asarray(L0_ffn_down_scales, dtype=np.float32)),
        **consts)

    nc = build_nc()
    in_maps = [dict(x=np.ascontiguousarray(x[b]), **shared) for b in range(8)]
    res = run_bass_kernel_spmd(nc, in_maps, core_ids=list(range(8)))
    out = np.stack([res.results[b]["out"] for b in range(8)], axis=0)
    return out
